# version 03
# baseline (speedup 1.0000x reference)
import contextlib
import math
_LN_SEQ = -1
import numpy as np
import concourse.bass as bass
import concourse.mybir as mybir
from concourse.bass import IndirectOffsetOnAxis
from concourse.bass_utils import run_bass_kernel_spmd
from concourse.bass_isa import ReduceOp

F32 = mybir.dt.float32
BF16 = mybir.dt.bfloat16
U32 = mybir.dt.uint32
AF = mybir.ActivationFunctionType
ALU = mybir.AluOpType
AX = mybir.AxisListType

D = 1024
NCORES = 8
TCH = 512

ENGINES = ("tensor", "vector", "scalar", "gpsimd", "sync")


class Ctx:

    N_BUCKETS = 40

    def __init__(self, nc, exitstack):
        self.nc = nc
        self.st = exitstack
        self.sem = {}
        self.cnt = {}
        self.chan_map = {}
        self._rr = {"sync": 0, "gpsimd": 0}
        for e in ENGINES:
            self.sem[e] = exitstack.enter_context(nc.semaphore(f"sem_{e}"))
            self.cnt[e] = 0

    def chan(self, name, queue="sync"):
        if name in self.chan_map:
            key, q = self.chan_map[name]
            assert q == queue, f"channel {name} used from {q} and {queue}"
            return key
        key = f"ch_{queue[0]}{self._rr[queue] % self.N_BUCKETS}"
        self._rr[queue] += 1
        self.chan_map[name] = (key, queue)
        if key not in self.sem:
            self.sem[key] = self.st.enter_context(self.nc.semaphore(key))
            self.cnt[key] = 0
        return key

    def eng(self, name):
        return getattr(self.nc, name)

    def inc(self, instr, engine):
        self.cnt[engine] += 1
        instr.then_inc(self.sem[engine], 1)
        return (engine, self.cnt[engine])

    def tok(self, key):
        if key in self.chan_map:
            key = self.chan_map[key][0]
        return (key, self.cnt.get(key, 0))

    def wait(self, engine, *tokens):
        eng = self.eng(engine)
        for (key, val) in tokens:
            if val > 0:
                eng.wait_ge(self.sem[key], val)

    def barrier(self, engines=("tensor", "vector", "scalar")):
        toks = {e: self.tok(e) for e in engines}
        for e in engines:
            for e2 in engines:
                if e2 != e:
                    self.wait(e, toks[e2])

    def fence(self):
        toks = [(k, self.cnt[k]) for k in self.cnt]
        for e in ENGINES:
            for (k, v) in toks:
                if v > 0 and k != e:
                    self.eng(e).wait_ge(self.sem[k], v)

    def dma(self, queue, channel, out, in_, **kw):
        key = self.chan(channel, queue)
        ins = self.eng(queue).dma_start(out=out, in_=in_, **kw)
        self.cnt[key] += 16
        ins.then_inc(self.sem[key], 16)
        return (key, self.cnt[key])

    def idma(self, channel, instr):
        key = self.chan(channel, "gpsimd")
        self.cnt[key] += 16
        instr.then_inc(self.sem[key], 16)
        return (key, self.cnt[key])


class Common:

    def __init__(self, nc, ctx, st):
        self.nc = nc
        E = st.enter_context
        self.ident_bf = E(nc.sbuf_tensor("ident_bf", [128, 128], BF16, side="right"))
        self.ident_f32 = E(nc.sbuf_tensor("ident_f32", [128, 128], F32, side="right"))
        self.ones_row_f32 = E(nc.sbuf_tensor("ones_row_f32", [1, 128], F32, side="right"))
        self.ones_col_bf = E(nc.sbuf_tensor("ones_col_bf", [128, 1], BF16, side="right"))
        self.ones_col_f32 = E(nc.sbuf_tensor("ones_col_f32", [128, 1], F32, side="right"))
        g = nc.gpsimd
        t1 = ctx.inc(g.memset(self.ident_bf[:], 1.0), "gpsimd")
        t2 = ctx.inc(g.memset(self.ident_f32[:], 1.0), "gpsimd")
        ctx.wait("gpsimd", t1, t2)
        ctx.inc(g.affine_select(out=self.ident_bf[:], in_=self.ident_bf[:],
                                compare_op=ALU.is_equal, fill=0.0, base=0,
                                pattern=[[-1, 128]], channel_multiplier=1), "gpsimd")
        ctx.inc(g.affine_select(out=self.ident_f32[:], in_=self.ident_f32[:],
                                compare_op=ALU.is_equal, fill=0.0, base=0,
                                pattern=[[-1, 128]], channel_multiplier=1), "gpsimd")
        ctx.inc(g.memset(self.ones_row_f32[:], 1.0), "gpsimd")
        ctx.inc(g.memset(self.ones_col_bf[:], 1.0), "gpsimd")
        ctx.inc(g.memset(self.ones_col_f32[:], 1.0), "gpsimd")
        self.ready = ctx.tok("gpsimd")
        self._ctx = ctx
        self._st = st
        self._consts = {}

    def const(self, value):
        value = float(value)
        if value not in self._consts:
            t = self._st.enter_context(
                self.nc.sbuf_tensor(f"const_{len(self._consts)}", [128, 1], F32,
                                    side="right"))
            tok = self._ctx.inc(self.nc.gpsimd.memset(t[:], value), "gpsimd")
            self._consts[value] = (t, tok)
        t, tok = self._consts[value]
        return t, tok


def act(ctx, com, out, in_, func, bias=None, scale=1.0, accum_out=None):
    nc = ctx.nc
    kw = {}
    if bias is not None:
        if isinstance(bias, (float, int)):
            bias_ap, btok = com.const(float(bias))
            ctx.wait("scalar", btok)
            np_ = out.shape[0]
            kw["bias"] = bias_ap[0:np_, 0:1]
        else:
            kw["bias"] = bias
    if accum_out is not None:
        kw["accum_out"] = accum_out
    return ctx.inc(nc.scalar.activation(out, in_, func, scale=float(scale), **kw),
                   "scalar")


def rsqrt_inplace(ctx, ap, eps, scale_in=1.0, com=None):
    t1 = act(ctx, com, ap, ap, AF.Ln, bias=float(eps), scale=float(scale_in))
    ctx.wait("scalar", t1)
    return act(ctx, com, ap, ap, AF.Exp, bias=None, scale=-0.5)


def sqrt_via_ln(ctx, com, out_ap, in_ap, eps, scale_in=1.0, scale_exp=0.5):
    t1 = act(ctx, com, out_ap, in_ap, AF.Ln, bias=float(eps), scale=float(scale_in))
    ctx.wait("scalar", t1)
    return act(ctx, com, out_ap, out_ap, AF.Exp, bias=None, scale=float(scale_exp))


def load_weight_ch(ctx, st, name, w_hbm, kin, m, dtype=BF16, queue="gpsimd"):
    nc = ctx.nc
    kc = (kin + 127) // 128
    w_sb = st.enter_context(nc.sbuf_tensor(name, [128, kc, m], dtype))
    if kc * 128 == kin:
        src = w_hbm.rearrange("(kc p) m -> p kc m", p=128)
        tok = ctx.dma(queue, "w_" + name, w_sb[:], src)
    else:
        nfull = kin // 128
        tok = None
        if nfull:
            src = w_hbm[0:nfull * 128, :].rearrange("(kc p) m -> p kc m", p=128)
            tok = ctx.dma(queue, "w_" + name, w_sb[:, 0:nfull, :], src)
        rem = kin - nfull * 128
        tok = ctx.dma(queue, "w_" + name, w_sb[0:rem, nfull, :], w_hbm[nfull * 128:kin, :])
    return w_sb, tok



def emit_ln(ctx, com, T, x_hbm, xn_ch_hbm=None, xn_tok_hbm=None, tok_dtype=BF16,
            add_hbm=(), x_out_hbm=None, eps=1e-5, shift_half=False,
            xn_f32_hbm=None):
    nc = ctx.nc
    global _LN_SEQ
    _LN_SEQ += 1
    lnq = _LN_SEQ
    ntc = T // TCH
    do_norm = (xn_ch_hbm is not None or xn_tok_hbm is not None
               or xn_f32_hbm is not None)
    with contextlib.ExitStack() as ls:
        E = ls.enter_context
        x_sb = E(nc.sbuf_tensor(f"ln_x{lnq}", [128, 8, TCH], F32))
        a_sb = E(nc.sbuf_tensor(f"ln_a{lnq}", [128, 8, TCH], F32)) if add_hbm else None
        if do_norm:
            xbf_sb = E(nc.sbuf_tensor(f"ln_xbf{lnq}", [128, 8, TCH], BF16))
            xsq_sb = E(nc.sbuf_tensor(f"ln_xsq{lnq}", [128, 8, TCH], BF16))
            xn_sb = E(nc.sbuf_tensor(f"ln_xn{lnq}", [128, 8, TCH], BF16))
            xnf_sb = (E(nc.sbuf_tensor(f"ln_xnf{lnq}", [128, 8, TCH], F32))
                      if ((xn_tok_hbm is not None and tok_dtype == F32)
                          or xn_f32_hbm is not None) else None)
            srow = E(nc.sbuf_tensor(f"ln_srow{lnq}", [1, TCH], F32))
            qrow = E(nc.sbuf_tensor(f"ln_qrow{lnq}", [1, TCH], F32))
            trow = E(nc.sbuf_tensor(f"ln_trow{lnq}", [1, TCH], F32))
            ps_sum = E(nc.psum_tensor(f"ln_ps_sum{lnq}", [1, TCH], F32))
            ps_sq = E(nc.psum_tensor(f"ln_ps_sq{lnq}", [1, TCH], F32))
            ps_b1 = E(nc.psum_tensor(f"ln_ps_b1{lnq}", [128, TCH], F32))
            ps_b2 = E(nc.psum_tensor(f"ln_ps_b2{lnq}", [128, TCH], F32))
        if xn_tok_hbm is not None:
            if tok_dtype == BF16:
                ps_tr = E(nc.psum_tensor(f"ln_ps_tr{lnq}", [128, 8, 128], BF16))
            else:
                ps_tr = E(nc.psum_tensor(f"ln_ps_tr{lnq}", [128, 4, 128], F32))
            xt_sb = E(nc.sbuf_tensor(f"ln_xt{lnq}", [128, 4, D], tok_dtype))

        xr = x_hbm.rearrange("(kb p) t -> p kb t", p=128)
        ar = [a.rearrange("(kb p) t -> p kb t", p=128) for a in add_hbm]
        xor_ = x_out_hbm.rearrange("(kb p) t -> p kb t", p=128) if x_out_hbm is not None else None
        xnr = xn_ch_hbm.rearrange("(kb p) t -> p kb t", p=128) if xn_ch_hbm is not None else None
        xtr = xn_tok_hbm.rearrange("(tt p) d -> p tt d", p=128) if xn_tok_hbm is not None else None
        xfr = xn_f32_hbm.rearrange("(kb p) t -> p kb t", p=128) if xn_f32_hbm is not None else None

        ident = com.ident_bf if tok_dtype == BF16 else com.ident_f32
        prev_reads = []
        prev_tmm = None
        prev_xn_store = None
        prev_xt_store = None
        for j in range(ntc):
            cj = slice(j * TCH, (j + 1) * TCH)
            ctx.wait("sync", *prev_reads)
            prev_reads = []
            if not shift_half:
                tx = ctx.dma("sync", "ln_x", x_sb[:], xr[:, :, cj])
            else:
                ctx.dma("sync", "ln_x", x_sb[:, 0:4, :], xr[:, 0:4, cj])
                if j == 0:
                    tx = ctx.dma("sync", "ln_x", x_sb[:, 4:8, 1:TCH],
                                 xr[:, 4:8, 0:TCH - 1])
                    gz = ctx.inc(nc.gpsimd.memset(x_sb[:, 4:8, 0:1], 0.0), "gpsimd")
                    ctx.wait("vector", gz)
                    ctx.wait("scalar", gz)
                else:
                    tx = ctx.dma("sync", "ln_x", x_sb[:, 4:8, :],
                                 xr[:, 4:8, j * TCH - 1:(j + 1) * TCH - 1])
            x_ready = tx
            for ai, ahr in enumerate(ar):
                if ai > 0:
                    ctx.wait("sync", x_ready)
                ta = ctx.dma("sync", "ln_a", a_sb[:], ahr[:, :, cj])
                ctx.wait("vector", x_ready, ta)
                x_ready = ctx.inc(nc.vector.tensor_tensor(
                    out=x_sb[:], in0=x_sb[:], in1=a_sb[:], op=ALU.add), "vector")
            if x_out_hbm is not None:
                ctx.wait("sync", x_ready)
                txo = ctx.dma("sync", "ln_xo", xor_[:, :, cj], x_sb[:])
                prev_reads.append(txo)
            if not do_norm:
                prev_reads.append(x_ready)
                continue

            if prev_tmm is not None:
                ctx.wait("vector", prev_tmm)
                ctx.wait("scalar", prev_tmm)
            ctx.wait("vector", x_ready)
            vbf = ctx.inc(nc.vector.tensor_copy(xbf_sb[:], x_sb[:]), "vector")
            ctx.wait("scalar", x_ready)
            vsq = ctx.inc(nc.scalar.activation(xsq_sb[:], x_sb[:], AF.Square), "scalar")
            ctx.wait("tensor", vbf, vsq)
            for kb in range(8):
                nc.tensor.matmul(ps_sum[:], com.ones_col_bf[:], xbf_sb[:, kb, :],
                                 start=(kb == 0), stop=(kb == 7))
                ins = nc.tensor.matmul(ps_sq[:], com.ones_col_bf[:], xsq_sb[:, kb, :],
                                       start=(kb == 0), stop=(kb == 7))
            tmm = ctx.inc(ins, "tensor")
            prev_tmm = tmm
            ctx.wait("vector", tmm)
            v1 = ctx.inc(nc.vector.tensor_scalar(out=srow[:], in0=ps_sum[:],
                                                 scalar1=1.0 / D, scalar2=None,
                                                 op0=ALU.mult), "vector")
            v2 = ctx.inc(nc.vector.tensor_scalar(out=qrow[:], in0=ps_sq[:],
                                                 scalar1=1.0 / D, scalar2=None,
                                                 op0=ALU.mult), "vector")
            ctx.wait("vector", v1)
            v3 = ctx.inc(nc.vector.tensor_tensor(out=trow[:], in0=srow[:],
                                                 in1=srow[:], op=ALU.mult), "vector")
            ctx.wait("vector", v2, v3)
            v4 = ctx.inc(nc.vector.tensor_tensor(out=qrow[:], in0=qrow[:],
                                                 in1=trow[:], op=ALU.subtract), "vector")
            ctx.wait("scalar", v4)
            s_rstd = rsqrt_inplace(ctx, qrow[:], eps, com=com)
            ctx.wait("vector", s_rstd)
            v5 = ctx.inc(nc.vector.tensor_tensor(out=trow[:], in0=srow[:],
                                                 in1=qrow[:], op=ALU.mult), "vector")
            ctx.wait("tensor", s_rstd, v5)
            b1 = ctx.inc(nc.tensor.matmul(ps_b1[:], com.ones_row_f32[:], qrow[:],
                                          start=True, stop=True), "tensor")
            b2 = ctx.inc(nc.tensor.matmul(ps_b2[:], com.ones_row_f32[:], trow[:],
                                          start=True, stop=True), "tensor")
            ctx.wait("vector", b1, b2)
            if prev_xn_store is not None:
                ctx.wait("vector", prev_xn_store)
            vlast = []
            tgt = xnf_sb if xnf_sb is not None else xn_sb
            for kb in range(8):
                va = ctx.inc(nc.vector.tensor_tensor(out=tgt[:, kb, :], in0=x_sb[:, kb, :],
                                                     in1=ps_b1[:], op=ALU.mult), "vector")
                ctx.wait("vector", va)
                vb = ctx.inc(nc.vector.tensor_tensor(out=tgt[:, kb, :], in0=tgt[:, kb, :],
                                                     in1=ps_b2[:], op=ALU.subtract), "vector")
                if xnf_sb is not None:
                    ctx.wait("vector", vb)
                    vb = ctx.inc(nc.vector.tensor_copy(xn_sb[:, kb, :], xnf_sb[:, kb, :]),
                                 "vector")
                vlast.append(vb)
            prev_reads += vlast
            prev_reads.append(ctx.tok("vector"))
            if xn_ch_hbm is not None:
                ctx.wait("sync", *vlast)
                tst = ctx.dma("sync", "ln_xn", xnr[:, :, cj], xn_sb[:])
                prev_reads.append(tst)
                prev_xn_store = tst
            if xn_f32_hbm is not None:
                ctx.wait("sync", *vlast)
                tstf = ctx.dma("sync", "ln_xf", xfr[:, :, cj], xnf_sb[:])
                prev_reads.append(tstf)
            if xn_tok_hbm is not None:
                for tt in range(TCH // 128):
                    ctx.wait("tensor", *vlast)
                    if prev_xt_store is not None:
                        ctx.wait("vector", prev_xt_store)
                        prev_xt_store = None
                    if tok_dtype == BF16:
                        for kb in range(8):
                            ins = nc.tensor.transpose(ps_tr[:, kb, :],
                                                      xn_sb[:, kb, tt * 128:(tt + 1) * 128],
                                                      com.ident_bf[:])
                        ttr = ctx.inc(ins, "tensor")
                        ctx.wait("vector", ttr)
                        tv = ctx.inc(nc.vector.tensor_copy(xt_sb[:, tt, :],
                                                           ps_tr.ap().rearrange("p kb f -> p (kb f)")), "vector")
                        ctx.wait("tensor", tv)
                    else:
                        for half in range(2):
                            for kb in range(4):
                                kbg = half * 4 + kb
                                ins = nc.tensor.transpose(
                                    ps_tr[:, kb, :],
                                    xnf_sb[:, kbg, tt * 128:(tt + 1) * 128],
                                    com.ident_f32[:])
                            ttr = ctx.inc(ins, "tensor")
                            ctx.wait("vector", ttr)
                            tv = ctx.inc(nc.vector.tensor_copy(
                                xt_sb[:, tt, half * 512:(half + 1) * 512],
                                ps_tr.ap().rearrange("p kb f -> p (kb f)")), "vector")
                            ctx.wait("tensor", tv)
                tvt = ctx.tok("vector")
                ctx.wait("sync", tvt)
                tst2 = ctx.dma("sync", "ln_xt", xtr[:, 4 * j:4 * (j + 1), :], xt_sb[:])
                prev_reads.append(tst2)
                prev_reads.append(tvt)
                prev_xt_store = tst2
        ctx.wait("sync", *prev_reads)
    return



def scratch_hbm(nc, name, shape, dtype=F32):
    return nc.dram_tensor(name, shape, dtype, kind="Internal").ap()


def mm_acc(nc, ps_ap, lhs_list, rhs_list, start=True, stop=True, skip_group_check=False):
    n = len(lhs_list)
    ins = None
    for i, (l, r) in enumerate(zip(lhs_list, rhs_list)):
        ins = nc.tensor.matmul(ps_ap, l, r, start=(start and i == 0),
                               stop=(stop and i == n - 1),
                               skip_group_check=skip_group_check)
    return ins


def load_bias_col(ctx, st, name, b_hbm, n, queue="gpsimd"):
    nc = ctx.nc
    kb = (n + 127) // 128
    b_sb = st.enter_context(nc.sbuf_tensor(name, [128, kb], F32))
    tok = None
    for i in range(kb):
        lo, hi = i * 128, min((i + 1) * 128, n)
        src = b_hbm[lo:hi].rearrange("(p a) -> p a", a=1)
        tok = ctx.dma(queue, "w_" + name, b_sb[0:hi - lo, i:i + 1], src)
    return b_sb, tok


def emit_elu1_evac(ctx, com, ps_ap, out_ap, tmp_a, tmp_b, bias_bc=None, bias_col=None,
                   prev=None):
    nc = ctx.nc
    if prev is not None:
        ctx.wait("vector", prev)
    if bias_bc is not None:
        t0 = ctx.inc(nc.vector.tensor_tensor(out=tmp_a, in0=ps_ap, in1=bias_bc,
                                             op=ALU.add), "vector")
        ctx.wait("vector", t0)
        src = tmp_a
        t1 = ctx.inc(nc.vector.tensor_scalar(out=tmp_b, in0=src, scalar1=0.0,
                                             scalar2=None, op0=ALU.min), "vector")
        t2 = ctx.inc(nc.vector.tensor_scalar(out=tmp_a, in0=src, scalar1=0.0,
                                             scalar2=None, op0=ALU.max), "vector")
    else:
        sc1 = bias_col if bias_col is not None else 0.0
        op0 = ALU.add if bias_col is not None else ALU.bypass
        if bias_col is not None:
            t1 = ctx.inc(nc.vector.tensor_scalar(out=tmp_b, in0=ps_ap, scalar1=sc1,
                                                 scalar2=0.0, op0=op0, op1=ALU.min), "vector")
            t2 = ctx.inc(nc.vector.tensor_scalar(out=tmp_a, in0=ps_ap, scalar1=sc1,
                                                 scalar2=0.0, op0=op0, op1=ALU.max), "vector")
        else:
            t1 = ctx.inc(nc.vector.tensor_scalar(out=tmp_b, in0=ps_ap, scalar1=0.0,
                                                 scalar2=None, op0=ALU.min), "vector")
            t2 = ctx.inc(nc.vector.tensor_scalar(out=tmp_a, in0=ps_ap, scalar1=0.0,
                                                 scalar2=None, op0=ALU.max), "vector")
    ctx.wait("scalar", t1)
    t3 = act(ctx, com, tmp_b, tmp_b, AF.Exp)
    ctx.wait("vector", t2, t3)
    return ctx.inc(nc.vector.tensor_tensor(out=out_ap, in0=tmp_a, in1=tmp_b,
                                           op=ALU.add), "vector")



def emit_l3_attn(ctx, com, T, xn_ch, w, out_hbm):
    nc = ctx.nc
    ntc = T // TCH
    ntt = T // 128
    H = 16
    with contextlib.ExitStack() as ls:
        E = ls.enter_context
        wq_sb, twq = load_weight_ch(ctx, ls, "l3a_wq", w["Wq"], D, D)
        wd_sb, twd = load_weight_ch(ctx, ls, "l3a_wd", w["Wd"], D, 256)
        wu_sb, twu = load_weight_ch(ctx, ls, "l3a_wu", w["Wu"], 256, 2048)
        wo_sb, two = load_weight_ch(ctx, ls, "l3a_wo", w["Wo"], D, D)
        bq_sb, tbq = load_bias_col(ctx, ls, "l3a_bq", w["bq"], D)
        bd_sb, tbd = load_bias_col(ctx, ls, "l3a_bd", w["bd"], 256)
        bo_sb, tbo = load_bias_col(ctx, ls, "l3a_bo", w["bo"], D)
        bu_row = E(nc.sbuf_tensor("l3a_bu_row", [1, 2048], F32))
        tbu = ctx.dma("gpsimd", "w_l3a_bu", bu_row[:], w["bu"].rearrange("(a b) -> a b", a=1))

        q_sb = E(nc.sbuf_tensor("l3a_q", [128, 8, T], BF16))
        r_sb = E(nc.sbuf_tensor("l3a_r", [128, 2, T], BF16))
        xn_sb = E(nc.sbuf_tensor("l3a_xn", [128, 8, TCH], BF16))
        ta_sb = E(nc.sbuf_tensor("l3a_ta", [128, TCH], F32))
        tb_sb = E(nc.sbuf_tensor("l3a_tb", [128, TCH], F32))
        bu_bc = E(nc.sbuf_tensor("l3a_bu_bc", [128, 2048], F32))
        kv_sb = E(nc.sbuf_tensor("l3a_kv", [128, 2048], BF16))
        ctx_sb = E(nc.sbuf_tensor("l3a_ctx", [128, 512], BF16))
        kst_sb = E(nc.sbuf_tensor("l3a_kst", [128, 8], BF16))
        zt_sb = E(nc.sbuf_tensor("l3a_zt", [128, 16], F32))
        ao_sb = E(nc.sbuf_tensor("l3a_ao", [128, 1024], BF16))
        aoT_sb = E(nc.sbuf_tensor("l3a_aoT", [128, 8, TCH], BF16))
        fo_sb = E(nc.sbuf_tensor("l3a_fo", [128, 8, TCH], F32))

        ps = [E(nc.psum_tensor(f"l3a_ps{i}", [128, TCH], F32)) for i in range(2)]
        ps_ctx = E(nc.psum_tensor("l3a_ps_ctx", [128, 512], F32))
        ps_ks = E(nc.psum_tensor("l3a_ps_ks", [128, 512], F32))
        ps_o = E(nc.psum_tensor("l3a_ps_o", [128, 1024], F32))
        ps_tr = E(nc.psum_tensor("l3a_ps_tr", [128, 8, 128], BF16))

        xnr = xn_ch.rearrange("(kb p) t -> p kb t", p=128)
        outr = out_hbm.rearrange("(kb p) t -> p kb t", p=128)
        kcs8 = list(range(8))

        ctx.wait("tensor", com.ready, tbu)
        bc_ev = [None, None]
        for ni in range(4):
            if bc_ev[ni % 2] is not None:
                ctx.wait("tensor", bc_ev[ni % 2])
            b = ctx.inc(nc.tensor.matmul(ps[ni % 2][:, 0:512], com.ones_row_f32[:],
                                         bu_row[0:1, ni * 512:(ni + 1) * 512],
                                         start=True, stop=True), "tensor")
            ctx.wait("vector", b)
            bc_ev[ni % 2] = ctx.inc(nc.vector.tensor_copy(
                bu_bc[:, ni * 512:(ni + 1) * 512], ps[ni % 2][:, 0:512]), "vector")
        t_bu_bc = ctx.tok("vector")

        ctx.wait("tensor", twq, twd)
        ctx.wait("vector", tbq, tbd, tbo)
        ev_tok = [t_bu_bc, t_bu_bc]
        last_ev = None
        prev_xn_users = None
        for j in range(ntc):
            cj = slice(j * TCH, (j + 1) * TCH)
            if prev_xn_users is not None:
                ctx.wait("sync", prev_xn_users)
            txn = ctx.dma("sync", "l3a_xn", xn_sb[:], xnr[:, :, cj])
            ctx.wait("tensor", txn)
            for mi in range(10):
                pp = ps[mi % 2]
                if ev_tok[mi % 2] is not None:
                    ctx.wait("tensor", ev_tok[mi % 2])
                if mi < 8:
                    msl = slice(mi * 128, (mi + 1) * 128)
                    ins = mm_acc(nc, pp[:], [wq_sb[:, kc, msl] for kc in kcs8],
                                 [xn_sb[:, kc, :] for kc in kcs8])
                    tmm = ctx.inc(ins, "tensor")
                    ctx.wait("vector", tmm)
                    ev = emit_elu1_evac(ctx, com, pp[:], q_sb[:, mi, cj],
                                        ta_sb[:], tb_sb[:],
                                        bias_col=bq_sb[:, mi:mi + 1],
                                        prev=last_ev)
                    last_ev = ev
                else:
                    ri = mi - 8
                    msl = slice(ri * 128, (ri + 1) * 128)
                    ins = mm_acc(nc, pp[:], [wd_sb[:, kc, msl] for kc in kcs8],
                                 [xn_sb[:, kc, :] for kc in kcs8])
                    tmm = ctx.inc(ins, "tensor")
                    ctx.wait("vector", tmm)
                    ev = ctx.inc(nc.vector.tensor_scalar(
                        out=r_sb[:, ri, cj], in0=pp[:], scalar1=bd_sb[:, ri:ri + 1],
                        scalar2=None, op0=ALU.add), "vector")
                ev_tok[mi % 2] = ev
            prev_xn_users = ctx.tok("tensor")
        t_stage1_v = ctx.tok("vector")

        ctx.wait("tensor", twu, t_bu_bc, t_stage1_v)
        ev_kv = [None, None]
        prev_ctx_users = None
        for tt in range(ntt):
            tsl = slice(tt * 128, (tt + 1) * 128)
            kv_ready = []
            for ni in range(4):
                pp = ps[ni % 2]
                if ev_kv[ni % 2] is not None:
                    ctx.wait("tensor", ev_kv[ni % 2])
                nsl = slice(ni * 512, (ni + 1) * 512)
                ins = mm_acc(nc, pp[:], [r_sb[:, kc, tsl] for kc in range(2)],
                             [wu_sb[:, kc, nsl] for kc in range(2)])
                tmm = ctx.inc(ins, "tensor")
                ctx.wait("vector", tmm)
                if last_ev is not None:
                    ctx.wait("vector", last_ev)
                t0 = ctx.inc(nc.vector.tensor_tensor(
                    out=ta_sb[:], in0=pp[:], in1=bu_bc[:, nsl], op=ALU.add), "vector")
                ctx.wait("vector", t0)
                ta3 = ta_sb.ap().rearrange("p (g f) -> p g f", f=128)
                tb3 = tb_sb.ap().rearrange("p (g f) -> p g f", f=128)
                kv3 = kv_sb[:, nsl].rearrange("p (g f) -> p g f", f=128)
                t1 = ctx.inc(nc.vector.tensor_scalar(
                    out=tb3[:, :, 0:64], in0=ta3[:, :, 0:64], scalar1=0.0,
                    scalar2=None, op0=ALU.min), "vector")
                tv = ctx.inc(nc.vector.tensor_copy(kv3[:, :, 64:128],
                                                   ta3[:, :, 64:128]), "vector")
                ctx.wait("scalar", t1)
                t2 = act(ctx, com, tb3[:, :, 0:64], tb3[:, :, 0:64], AF.Exp)
                t3 = ctx.inc(nc.vector.tensor_scalar(
                    out=ta3[:, :, 0:64], in0=ta3[:, :, 0:64], scalar1=0.0,
                    scalar2=None, op0=ALU.max), "vector")
                ctx.wait("vector", t2, t3)
                ev = ctx.inc(nc.vector.tensor_tensor(
                    out=kv3[:, :, 0:64], in0=ta3[:, :, 0:64], in1=tb3[:, :, 0:64],
                    op=ALU.add), "vector")
                last_ev = ev
                ev_kv[ni % 2] = ev
                kv_ready.append((ev, tv))
            kv_ready = [t for pair in kv_ready for t in pair]
            ctx.wait("tensor", *kv_ready)
            ins = None
            for h in range(H):
                k_ap = kv_sb[:, h * 128:h * 128 + 64]
                v_ap = kv_sb[:, h * 128 + 64:(h + 1) * 128]
                po = ps_ctx[(h % 2) * 64:(h % 2) * 64 + 64,
                            (h // 2) * 64:(h // 2) * 64 + 64]
                nc.tensor.matmul(po, k_ap, v_ap, start=(tt == 0 and h < 2),
                                 stop=(tt == ntt - 1 and h == H - 1),
                                 skip_group_check=True)
                pk = ps_ks[(h % 2) * 64:(h % 2) * 64 + 64, (h // 2):(h // 2) + 1]
                ins = nc.tensor.matmul(pk, k_ap, com.ones_col_bf[:],
                                       start=(tt == 0 and h < 2),
                                       stop=(tt == ntt - 1 and h == H - 1),
                                       skip_group_check=True)
            prev_ctx_users = ctx.inc(ins, "tensor")
        ctx.wait("vector", prev_ctx_users)
        tev1 = ctx.inc(nc.vector.tensor_copy(ctx_sb[:], ps_ctx[:]), "vector")
        tev2 = ctx.inc(nc.vector.tensor_copy(kst_sb[:], ps_ks[:, 0:8]), "vector")

        ctx.wait("tensor", tev1, tev2)
        ev_ao = None
        ev_tr = None
        for tt in range(ntt):
            tsl = slice(tt * 128, (tt + 1) * 128)
            if ev_ao is not None:
                ctx.wait("tensor", ev_ao)
            ins = None
            for h in range(H):
                base = (h % 2) * 64
                blk = h // 2
                q_ap = q_sb[base:base + 64, blk, tsl]
                ctx_ap = ctx_sb[base:base + 64, blk * 64:(blk + 1) * 64]
                po = ps_o[:, (h % 2) * 512 + blk * 64:(h % 2) * 512 + (blk + 1) * 64]
                nc.tensor.matmul(po, q_ap, ctx_ap, start=True, stop=True,
                                 skip_group_check=True)
                pz = ps[h % 2][:, blk:blk + 1]
                ins = nc.tensor.matmul(pz, q_ap, kst_sb[base:base + 64, blk:blk + 1],
                                       start=True, stop=True, skip_group_check=True)
            tmm = ctx.inc(ins, "tensor")
            ctx.wait("vector", tmm)
            vz1 = ctx.inc(nc.vector.tensor_scalar(out=zt_sb[:, 0:8], in0=ps[0][:, 0:8],
                                                  scalar1=1e-6, scalar2=None,
                                                  op0=ALU.add), "vector")
            vz2 = ctx.inc(nc.vector.tensor_scalar(out=zt_sb[:, 8:16], in0=ps[1][:, 0:8],
                                                  scalar1=1e-6, scalar2=None,
                                                  op0=ALU.add), "vector")
            ctx.wait("vector", vz1, vz2)
            vz3 = ctx.inc(nc.vector.reciprocal(zt_sb[:], zt_sb[:]), "vector")
            ctx.wait("vector", vz3)
            for h in range(H):
                blk = h // 2
                zcol = zt_sb[:, (h % 2) * 8 + blk:(h % 2) * 8 + blk + 1]
                po = ps_o[:, (h % 2) * 512 + blk * 64:(h % 2) * 512 + (blk + 1) * 64]
                ctx.inc(nc.vector.tensor_scalar(out=ao_sb[:, h * 64:(h + 1) * 64],
                                                in0=po, scalar1=zcol, scalar2=None,
                                                op0=ALU.mult), "vector")
            ev_ao = ctx.tok("vector")
            ctx.wait("tensor", ev_ao)
            if ev_tr is not None:
                ctx.wait("tensor", ev_tr)
            for kb in range(8):
                ins = nc.tensor.transpose(ps_tr[:, kb, :],
                                          ao_sb[:, kb * 128:(kb + 1) * 128],
                                          com.ident_bf[:])
            ttr = ctx.inc(ins, "tensor")
            ctx.wait("vector", ttr)
            ev_tr = ctx.inc(nc.vector.tensor_copy(
                aoT_sb[:, :, (tt % 4) * 128:(tt % 4) * 128 + 128], ps_tr[:]), "vector")
            if tt % 4 == 3:
                j = tt // 4
                cj = slice(j * TCH, (j + 1) * TCH)
                ctx.wait("tensor", ev_tr, two)
                ev5 = None
                for mi in range(8):
                    pp = ps[mi % 2]
                    if ev_tok[mi % 2] is not None:
                        ctx.wait("tensor", ev_tok[mi % 2])
                    msl = slice(mi * 128, (mi + 1) * 128)
                    ins = mm_acc(nc, pp[:], [wo_sb[:, kc, msl] for kc in kcs8],
                                 [aoT_sb[:, kc, :] for kc in kcs8])
                    tmm = ctx.inc(ins, "tensor")
                    ctx.wait("vector", tmm)
                    ev = ctx.inc(nc.vector.tensor_scalar(
                        out=fo_sb[:, mi, :], in0=pp[:], scalar1=bo_sb[:, mi:mi + 1],
                        scalar2=None, op0=ALU.add), "vector")
                    ev_tok[mi % 2] = ev
                ctx.wait("sync", *[t for t in ev_tok if t is not None])
                tso = ctx.dma("sync", "l3a_out", outr[:, :, cj], fo_sb[:])
                ctx.wait("vector", ctx.tok("tensor"))
                ev_tr = None
                ctx.wait("tensor", tso)
                ev_ao = ctx.tok("vector")
        ctx.wait("sync", ctx.tok("l3a_out"))
    return


from concourse.bass import broadcast_tensor_aps


def emit_qd16(ctx, com, x_ap, out_ap, g_sb, y_sb, t_sb, u_sb, prev=None):
    nc = ctx.nc
    p, Fn = x_ap.shape[0], x_ap.shape[-1]
    G = Fn // 16
    M = float(2 ** 23)
    x3 = x_ap.rearrange("p (g f) -> p g f", f=16)
    if prev is not None:
        ctx.wait("vector", prev)
    sc = g_sb[0:p, 0:G]
    rs = g_sb[0:p, G:2 * G]
    sch = g_sb[0:p, 2 * G:3 * G]
    v = ctx.inc(nc.vector.tensor_reduce(sc.rearrange("p (g a) -> p g a", a=1), x3,
                                        axis=AX.X, op=ALU.max,
                                        apply_absolute_value=True), "vector")
    ctx.wait("vector", v)
    v = ctx.inc(nc.vector.tensor_scalar(out=sc, in0=sc, scalar1=1.0 / 6.0,
                                        scalar2=1e-12, op0=ALU.mult, op1=ALU.max), "vector")
    ctx.wait("vector", v)
    v1 = ctx.inc(nc.vector.reciprocal(rs, sc), "vector")
    v2 = ctx.inc(nc.vector.tensor_scalar(out=sch, in0=sc, scalar1=1.0, scalar2=None,
                                         op0=ALU.mult), "vector")
    ctx.wait("vector", v1, v2)
    y3 = y_sb[0:p, 0:Fn].rearrange("p (g f) -> p g f", f=16)
    rs3 = rs.rearrange("p (g a) -> p g a", a=1)
    x3b, rs3b = broadcast_tensor_aps(x3, rs3)
    v = ctx.inc(nc.vector.tensor_tensor(out=y3, in0=x3b, in1=rs3b,
                                        op=ALU.mult), "vector")
    ctx.wait("scalar", v)
    v = act(ctx, com, y_sb[0:p, 0:Fn], y_sb[0:p, 0:Fn], AF.Abs)
    ctx.wait("vector", v)
    yf = y_sb[0:p, 0:Fn]
    tf = t_sb[0:p, 0:Fn]
    uf = u_sb[0:p, 0:Fn]
    va = ctx.inc(nc.vector.tensor_scalar(out=tf, in0=yf, scalar1=2.0, scalar2=4.0,
                                         op0=ALU.mult, op1=ALU.min), "vector")
    ctx.wait("vector", va)
    va = ctx.inc(nc.vector.tensor_scalar(out=tf, in0=tf, scalar1=M, scalar2=M,
                                         op0=ALU.add, op1=ALU.subtract), "vector")
    vb = ctx.inc(nc.vector.tensor_scalar(out=uf, in0=yf, scalar1=2.0, scalar2=4.5,
                                         op0=ALU.max, op1=ALU.min), "vector")
    ctx.wait("vector", vb)
    vb = ctx.inc(nc.vector.tensor_scalar(out=uf, in0=uf, scalar1=M, scalar2=M,
                                         op0=ALU.add, op1=ALU.subtract), "vector")
    ctx.wait("vector", va, vb)
    v = ctx.inc(nc.vector.scalar_tensor_tensor(out=tf, in0=tf, scalar=0.5, in1=uf,
                                               op0=ALU.mult, op1=ALU.add), "vector")
    ctx.wait("vector", v)
    vg = ctx.inc(nc.vector.tensor_scalar(out=uf, in0=yf, scalar1=5.0, scalar2=2.0,
                                         op0=ALU.is_gt, op1=ALU.mult), "vector")
    ctx.wait("vector", v, vg)
    v = ctx.inc(nc.vector.scalar_tensor_tensor(out=tf, in0=tf, scalar=2.0, in1=uf,
                                               op0=ALU.subtract, op1=ALU.add), "vector")
    ctx.wait("vector", v)
    vs = ctx.inc(nc.vector.tensor_scalar(out=uf, in0=x_ap, scalar1=0.0, scalar2=2.0,
                                         op0=ALU.is_ge, op1=ALU.mult), "vector")
    ctx.wait("vector", vs)
    u3 = uf.rearrange("p (g f) -> p g f", f=16)
    sch3 = sch.rearrange("p (g a) -> p g a", a=1)
    u3b, sch3b = broadcast_tensor_aps(u3, sch3)
    vs = ctx.inc(nc.vector.scalar_tensor_tensor(out=u3, in0=u3b, scalar=1.0, in1=sch3b,
                                                op0=ALU.subtract, op1=ALU.mult), "vector")
    ctx.wait("vector", v, vs)
    return ctx.inc(nc.vector.tensor_tensor(out=out_ap, in0=tf, in1=uf,
                                           op=ALU.mult), "vector")


def emit_transpose_to(ctx, com, src_ap2, dst_ap2, ps_tr, npt=8, prev_dst=None):
    nc = ctx.nc
    p = src_ap2.shape[0]
    n = src_ap2.shape[-1] // 128
    tok = None
    for b0 in range(0, n, npt):
        nb = min(npt, n - b0)
        if tok is not None:
            ctx.wait("tensor", tok)
        ins = None
        for i in range(nb):
            ins = nc.tensor.transpose(ps_tr[0:128, i, 0:p],
                                      src_ap2[:, (b0 + i) * 128:(b0 + i + 1) * 128],
                                      com.ident_bf[:])
        tt = ctx.inc(ins, "tensor")
        ctx.wait("vector", tt)
        if prev_dst is not None:
            ctx.wait("vector", prev_dst)
            prev_dst = None
        tok = ctx.inc(nc.vector.tensor_copy(dst_ap2[:, b0:b0 + nb, :],
                                            ps_tr[0:128, 0:nb, 0:p]), "vector")
    return tok



def emit_l2_ffn(ctx, com, T, xn_tok, w, out_hbm):
    nc = ctx.nc
    I = 2816
    KCI = I // 128
    ntt = T // 128
    wdq_hbm = nc.dram_tensor("l2f_wdq_hbm", [I, D], BF16, kind="Internal").ap()
    xq_hbm = nc.dram_tensor("l2f_xq_hbm", [D, T], BF16, kind="Internal").ap()
    with contextlib.ExitStack() as ls:
        E = ls.enter_context
        wgq = E(nc.sbuf_tensor("l2f_wgq", [128, 8, I], BF16))
        wuq = E(nc.sbuf_tensor("l2f_wuq", [128, 8, I], BF16))
        wrow = E(nc.sbuf_tensor("l2f_wrow", [128, I], F32))
        qrow = E(nc.sbuf_tensor("l2f_qrow", [128, I], BF16))
        g_sb = E(nc.sbuf_tensor("l2f_gs", [128, 3 * 176], F32))
        y_sb = E(nc.sbuf_tensor("l2f_y", [128, I], F32))
        t_sb = E(nc.sbuf_tensor("l2f_t", [128, I], F32))
        u_sb = E(nc.sbuf_tensor("l2f_u", [128, I], F32))
        tr_sb = E(nc.sbuf_tensor("l2f_tr", [128, 22, 128], BF16))
        xqt_sb = [E(nc.sbuf_tensor(f"l2f_xqt{i}", [128, 8, 128], BF16)) for i in range(2)]
        wdm_sb = [E(nc.sbuf_tensor(f"l2f_wdm{i}", [128, KCI, 128], BF16)) for i in range(2)]
        hq_T = E(nc.sbuf_tensor("l2f_hqT", [128, KCI, 128], BF16))
        sg = E(nc.sbuf_tensor("l2f_sg", [128, TCH], F32))
        fo_sb = [E(nc.sbuf_tensor(f"l2f_fo{i}", [128, 128], F32)) for i in range(2)]
        ps = [E(nc.psum_tensor(f"l2f_ps{i}", [128, TCH], F32)) for i in range(4)]
        ps_tr = E(nc.psum_tensor("l2f_ps_tr", [128, 8, 128], BF16))

        xr = xn_tok.rearrange("(tt p) d -> p tt d", p=128)
        outr = out_hbm.rearrange("(kb p) t -> p kb t", p=128)
        wdr = wdq_hbm.rearrange("(kc p) m -> p kc m", p=128)
        xqr = xq_hbm.rearrange("(kb p) t -> p kb t", p=128)
        ctx.wait("tensor", com.ready)
        ctx.wait("vector", com.ready)
        ctx.wait("scalar", com.ready)

        def quant_rows(w_hbm, rows, cols, tag, consume, tr_prev):
            for rc in range(rows // 128):
                ctx.wait("sync", ctx.tok("vector"))
                tld = ctx.dma("sync", f"l2f_{tag}", wrow[:, 0:cols],
                              w_hbm[rc * 128:(rc + 1) * 128, :])
                ctx.wait("vector", tld)
                tq = emit_qd16(ctx, com, wrow[:, 0:cols], qrow[:, 0:cols],
                               g_sb, y_sb, t_sb, u_sb, prev=tr_prev)
                ctx.wait("tensor", tq)
                tr_prev = consume(rc, tq)
            return tr_prev

        def consume_gu(dst):
            def fn(rc, tq):
                return emit_transpose_to(
                    ctx, com, qrow[:, 0:D],
                    dst.ap()[:, :, rc * 128:(rc + 1) * 128], ps_tr)
            return fn

        def consume_wd(rc, tq):
            tok = emit_transpose_to(ctx, com, qrow[:, 0:I],
                                    tr_sb.ap()[:, :, :], ps_tr)
            ctx.wait("sync", tok)
            ctx.dma("sync", "l2f_wdst", wdr[:, :, rc * 128:(rc + 1) * 128], tr_sb[:])
            return ctx.tok("l2f_wdst")

        tg = quant_rows(w["w_gate"], I, D, "wg", consume_gu(wgq), None)
        tu = quant_rows(w["w_up"], I, D, "wu", consume_gu(wuq), tg)
        td = quant_rows(w["w_down"], D, I, "wd", consume_wd, tu)
        ctx.wait("vector", ctx.tok("l2f_wdst"))
        td_v = ctx.tok("vector")

        tr_prev = td_v
        for tt in range(ntt):
            ctx.wait("sync", ctx.tok("vector"))
            tld = ctx.dma("sync", "l2f_x", wrow[:, 0:D], xr[:, tt, :])
            ctx.wait("vector", tld)
            tq = emit_qd16(ctx, com, wrow[:, 0:D], qrow[:, 0:D],
                           g_sb, y_sb, t_sb, u_sb, prev=tr_prev)
            ctx.wait("tensor", tq)
            tok = emit_transpose_to(ctx, com, qrow[:, 0:D],
                                    tr_sb.ap()[:, 0:8, :], ps_tr)
            ctx.wait("sync", tok)
            ctx.dma("sync", "l2f_xqst", xqr[:, :, tt * 128:(tt + 1) * 128],
                    tr_sb[:, 0:8, :])
            tr_prev = ctx.tok("l2f_xqst")
        ctx.wait("vector", ctx.tok("l2f_xqst"))

        nch = [(i * 512, min((i + 1) * 512, I)) for i in range((I + 511) // 512)]
        ev_ps = [None, None, None, None]
        last_ev_h = tr_prev
        prev_q_users = None
        txq = [None, None]
        ev5 = [None, None]
        twd_tok = [None, None]
        ctx.wait("sync", ctx.tok("l2f_xqst"), ctx.tok("l2f_wdst"))
        for tt in range(ntt):
            if txq[tt % 2] is not None:
                ctx.wait("sync", txq[tt % 2][1])
            txq[tt % 2] = (ctx.dma("sync", f"l2f_xql{tt % 2}",
                                   xqt_sb[tt % 2][:],
                                   xqr[:, :, tt * 128:(tt + 1) * 128]), None)
            tld = txq[tt % 2][0]
            ctx.wait("tensor", tld)
            for ci, (lo, hi) in enumerate(nch):
                w_ = hi - lo
                pg = ps[2 * (ci % 2)]
                pu = ps[2 * (ci % 2) + 1]
                if ev_ps[2 * (ci % 2)] is not None:
                    ctx.wait("tensor", ev_ps[2 * (ci % 2)])
                mm_acc(nc, pg[:, 0:w_], [xqt_sb[tt % 2][:, kc, :] for kc in range(8)],
                       [wgq[:, kc, lo:hi] for kc in range(8)])
                ins = mm_acc(nc, pu[:, 0:w_], [xqt_sb[tt % 2][:, kc, :] for kc in range(8)],
                             [wuq[:, kc, lo:hi] for kc in range(8)])
                tmm = ctx.inc(ins, "tensor")
                ctx.wait("scalar", tmm)
                if last_ev_h is not None:
                    ctx.wait("scalar", last_ev_h)
                tsg = act(ctx, com, sg[:, 0:w_], pg[:, 0:w_], AF.Silu)
                ctx.wait("vector", tsg)
                ev = ctx.inc(nc.vector.tensor_tensor(out=wrow[:, lo:hi], in0=sg[:, 0:w_],
                                                     in1=pu[:, 0:w_], op=ALU.mult), "vector")
                last_ev_h = ev
                ev_ps[2 * (ci % 2)] = ev
                ev_ps[2 * (ci % 2) + 1] = ev
            txq[tt % 2] = (tld, None)
            txq[tt % 2] = (tld, ctx.tok("tensor"))
            tq = emit_qd16(ctx, com, wrow[:, 0:I], qrow[:, 0:I],
                           g_sb, y_sb, t_sb, u_sb, prev=last_ev_h)
            last_ev_h = tq
            ctx.wait("tensor", tq)
            tok = emit_transpose_to(ctx, com, qrow[:, 0:I], hq_T.ap()[:, :, :],
                                    ps_tr, prev_dst=prev_q_users)
            prev_q_users = None
            ctx.wait("tensor", tok)
            for mi in range(8):
                pp = ps[mi % 2]
                if twd_tok[mi % 2] is not None:
                    ctx.wait("sync", twd_tok[mi % 2][1])
                tldw = ctx.dma("sync", f"l2f_wdl{mi % 2}", wdm_sb[mi % 2][:],
                               wdr[:, :, mi * 128:(mi + 1) * 128])
                ctx.wait("tensor", tldw)
                if ev_ps[mi % 2] is not None:
                    ctx.wait("tensor", ev_ps[mi % 2])
                    ev_ps[mi % 2] = None
                if ev5[mi % 2] is not None:
                    ctx.wait("tensor", ev5[mi % 2])
                ins = mm_acc(nc, pp[:, 0:128], [wdm_sb[mi % 2][:, kc, :] for kc in range(KCI)],
                             [hq_T[:, kc, :] for kc in range(KCI)])
                tmm = ctx.inc(ins, "tensor")
                twd_tok[mi % 2] = (tldw, ctx.tok("tensor"))
                ctx.wait("vector", tmm)
                if ev5[mi % 2] is not None:
                    ctx.wait("vector", ev5[mi % 2])
                ev5[mi % 2] = ctx.inc(nc.vector.tensor_copy(fo_sb[mi % 2][:], pp[:, 0:128]),
                                      "vector")
                ctx.wait("sync", ev5[mi % 2])
                ctx.dma("sync", f"l2f_out{mi % 2}",
                        outr[:, mi, tt * 128:(tt + 1) * 128], fo_sb[mi % 2][:])
                ev5[mi % 2] = (ctx.tok(f"l2f_out{mi % 2}"))
            prev_q_users = ctx.tok("tensor")
        ctx.wait("sync", ctx.tok("l2f_out0"), ctx.tok("l2f_out1"))
    return


def load_weight_T(ctx, com, st, name, w_hbm, rows, cols, tmp_sb, ps_tr, prev=None):
    nc = ctx.nc
    kc = cols // 128
    w_sb = st.enter_context(nc.sbuf_tensor(name, [128, kc, rows], BF16))
    tok = prev
    tld_prev = None
    for rc in range(rows // 128):
        if tok is not None:
            ctx.wait("gpsimd", tok)
        tld = ctx.dma("gpsimd", "w_" + name, tmp_sb[:, 0:cols],
                      w_hbm[rc * 128:(rc + 1) * 128, :])
        ctx.wait("tensor", tld)
        tok = emit_transpose_to(ctx, com, tmp_sb[:, 0:cols],
                                w_sb.ap()[:, :, rc * 128:(rc + 1) * 128], ps_tr)
        tld_prev = tld
    return w_sb, tok



def emit_l1_attn(ctx, com, T, xn_ch, w, out_hbm):
    nc = ctx.nc
    ntc = T // TCH
    H = 16
    EPS = 1e-6
    gh_hbm = nc.dram_tensor("l1a_gh_hbm", [D, T], BF16, kind="Internal").ap()
    qg_hbm = nc.dram_tensor("l1a_qg_hbm", [D, T], BF16, kind="Internal").ap()
    k_hbm = nc.dram_tensor("l1a_k_hbm", [D, T], BF16, kind="Internal").ap()
    v_hbm = nc.dram_tensor("l1a_v_hbm", [T, D], BF16, kind="Internal").ap()
    ao_hbm = nc.dram_tensor("l1a_ao_hbm", [D, T], BF16, kind="Internal").ap()
    xnr = xn_ch.rearrange("(kb p) t -> p kb t", p=128)
    ghr = gh_hbm.rearrange("(kb p) t -> p kb t", p=128)
    qgr = qg_hbm.rearrange("(kb p) t -> p kb t", p=128)
    kr = k_hbm.rearrange("(kb p) t -> p kb t", p=128)
    vr = v_hbm.rearrange("(tt p) d -> p tt d", p=128)
    aor = ao_hbm.rearrange("(kb p) t -> p kb t", p=128)
    outr = out_hbm.rearrange("(kb p) t -> p kb t", p=128)
    ctx.wait("tensor", com.ready)
    ctx.wait("vector", com.ready)
    ctx.wait("scalar", com.ready)
    ctx.wait("gpsimd", com.ready)

    with contextlib.ExitStack() as ls:
        E = ls.enter_context
        tmp_sb = E(nc.sbuf_tensor("l1a_wtmp", [128, 2048], BF16))
        ps_tr = E(nc.psum_tensor("l1a_ps_tr", [128, 8, 128], BF16))
        w1T, t1 = load_weight_T(ctx, com, ls, "l1a_w1T", w["W1"], 2048, D, tmp_sb, ps_tr)
        w2T, t2 = load_weight_T(ctx, com, ls, "l1a_w2T", w["W2"], D, 2048, tmp_sb, ps_tr,
                                prev=t1)
        b1c, tb1 = load_bias_col(ctx, ls, "l1a_b1", w["b1"], 2048)
        b2c, tb2 = load_bias_col(ctx, ls, "l1a_b2", w["b2"], D)
        eta = E(nc.sbuf_tensor("l1a_eta", [128, 8], F32))
        b1m = E(nc.sbuf_tensor("l1a_b1m", [128, 16], F32))
        g1 = ctx.inc(nc.gpsimd.memset(eta[:], 1.0), "gpsimd")
        ctx.wait("gpsimd", g1)
        g2 = ctx.inc(nc.gpsimd.memset(eta[0:1, 0:1], -1.0), "gpsimd")
        xn_sb = E(nc.sbuf_tensor("l1a_xn", [128, 8, TCH], BF16))
        hid_sb = E(nc.sbuf_tensor("l1a_hid", [128, 16, TCH], BF16))
        gh_sb = E(nc.sbuf_tensor("l1a_gh", [128, 8, TCH], BF16))
        ps = [E(nc.psum_tensor(f"l1a_ps{i}", [128, TCH], F32)) for i in range(2)]
        ctx.wait("tensor", t1, t2)
        ctx.wait("vector", tb1, tb2)
        ctx.wait("scalar", g2)
        e0c = E(nc.sbuf_tensor("l1a_e0", [128, 1], BF16))
        psb = E(nc.psum_tensor("l1a_psb", [128, 512], F32))
        ge = ctx.inc(nc.gpsimd.memset(e0c[:], 0.0), "gpsimd")
        ctx.wait("gpsimd", ge)
        ge = ctx.inc(nc.gpsimd.memset(e0c[0:1, 0:1], 1.0), "gpsimd")
        ctx.wait("tensor", ge)
        ins = None
        for mi in range(16):
            ins = nc.tensor.matmul(psb[:, mi:mi + 1], w1T[:, 0, mi * 128:(mi + 1) * 128],
                                   e0c[:], start=True, stop=True, skip_group_check=True)
        tmmc = ctx.inc(ins, "tensor")
        ctx.wait("vector", tmmc)
        tcol = ctx.inc(nc.vector.tensor_tensor(
            out=b1m[:], in0=b1c[:], in1=psb[:, 0:16], op=ALU.subtract), "vector")
        ctx.wait("scalar", tcol)
        ctx.wait("tensor", tcol)
        ev = [None, None]
        prev_x_users = None
        prev_st = None
        for j in range(ntc):
            cj = slice(j * TCH, (j + 1) * TCH)
            if prev_x_users is not None:
                ctx.wait("sync", prev_x_users)
            txn = ctx.dma("sync", "l1a_xa", xn_sb[:], xnr[:, :, cj])
            ctx.wait("tensor", txn)
            for mi in range(16):
                pp = ps[mi % 2]
                if ev[mi % 2] is not None:
                    ctx.wait("tensor", ev[mi % 2])
                msl = slice(mi * 128, (mi + 1) * 128)
                ins = mm_acc(nc, pp[:], [w1T[:, kc, msl] for kc in range(8)],
                             [xn_sb[:, kc, :] for kc in range(8)])
                tmm = ctx.inc(ins, "tensor")
                ctx.wait("scalar", tmm)
                ev[mi % 2] = act(ctx, com, hid_sb[:, mi, :], pp[:], AF.Tanh,
                                 bias=b1m[:, mi:mi + 1], scale=1.0)
            ctx.wait("tensor", ev[0], ev[1])
            for mi in range(8):
                pp = ps[mi % 2]
                ctx.wait("tensor", ev[mi % 2])
                msl = slice(mi * 128, (mi + 1) * 128)
                ins = mm_acc(nc, pp[:], [w2T[:, kc, msl] for kc in range(16)],
                             [hid_sb[:, kc, :] for kc in range(16)])
                tmm = ctx.inc(ins, "tensor")
                ctx.wait("scalar", tmm)
                s1 = act(ctx, com, pp[:], pp[:], AF.Tanh, bias=b2c[:, mi:mi + 1])
                ctx.wait("vector", s1)
                if prev_st is not None:
                    ctx.wait("vector", prev_st)
                    prev_st = None
                v1 = ctx.inc(nc.vector.tensor_scalar(
                    out=gh_sb[:, mi, :], in0=pp[:], scalar1=0.1,
                    scalar2=eta[:, mi:mi + 1], op0=ALU.mult, op1=ALU.add), "vector")
                ctx.wait("scalar", v1)
                ev[mi % 2] = act(ctx, com, gh_sb[:, mi, :], gh_sb[:, mi, :],
                                 AF.Abs)
            ctx.wait("vector", ev[0], ev[1])
            vf = ctx.inc(nc.vector.tensor_scalar(
                out=gh_sb[0:1, 0, :], in0=gh_sb[0:1, 0, :], scalar1=-1.0,
                scalar2=None, op0=ALU.mult), "vector")
            ctx.wait("sync", vf)
            tst = ctx.dma("sync", "l1a_ghst", ghr[:, :, cj], gh_sb[:])
            prev_st = tst
            prev_x_users = ctx.tok("tensor")
        pass
    ctx.fence()

    with contextlib.ExitStack() as ls:
        E = ls.enter_context
        tmp_sb = E(nc.sbuf_tensor("l1a_wtmp1b", [128, 1024], BF16))
        ps_tr = E(nc.psum_tensor("l1a_ps_tr1b", [128, 8, 128], BF16))
        wqT, tq_ = load_weight_T(ctx, com, ls, "l1a_wqT", w["Wq"], D, D, tmp_sb, ps_tr)
        wkT, tk_ = load_weight_T(ctx, com, ls, "l1a_wkT", w["Wk"], D, D, tmp_sb, ps_tr,
                                 prev=tq_)
        wvT, tv_ = load_weight_T(ctx, com, ls, "l1a_wvT", w["Wv"], D, D, tmp_sb, ps_tr,
                                 prev=tk_)
        xn_sb = E(nc.sbuf_tensor("l1a_xn1b", [128, 8, TCH], BF16))
        cl_sb = E(nc.sbuf_tensor("l1a_cl", [128, 8, TCH], BF16))
        sq_sb = E(nc.sbuf_tensor("l1a_sq", [128, 8, TCH], BF16))
        v_sb = E(nc.sbuf_tensor("l1a_v", [128, 8, TCH], BF16))
        gh_sb = E(nc.sbuf_tensor("l1a_gh1b", [128, 8, TCH], BF16))
        qg_sb = E(nc.sbuf_tensor("l1a_qg", [128, 8, TCH], BF16))
        k_sb = E(nc.sbuf_tensor("l1a_ks", [128, 8, TCH], BF16))
        vt_sb = E(nc.sbuf_tensor("l1a_vt", [128, 4, D], BF16))
        rows = E(nc.sbuf_tensor("l1a_rows", [1, 6 * TCH], F32))
        ps = [E(nc.psum_tensor(f"l1a_psb{i}", [128, TCH], F32)) for i in range(2)]
        ps_s = E(nc.psum_tensor("l1a_ps_s", [1, TCH], F32))
        ps_b = E(nc.psum_tensor("l1a_ps_b", [128, TCH], F32))
        ctx.wait("tensor", tq_, tk_, tv_)
        prev_users = None
        ev = [None, None]
        for j in range(ntc):
            cj = slice(j * TCH, (j + 1) * TCH)
            if prev_users is not None:
                ctx.wait("sync", prev_users[0])
                ctx.barrier()
            txn = ctx.dma("sync", "l1a_xb", xn_sb[:], xnr[:, :, cj])
            tgh = ctx.dma("sync", "l1a_ghl", gh_sb[:], ghr[:, :, cj])
            ctx.wait("vector", txn)
            if prev_users is not None:
                ctx.wait("vector", prev_users[1])
            vc = ctx.inc(nc.vector.tensor_scalar(out=cl_sb[:], in0=xn_sb[:],
                                                 scalar1=8.0, scalar2=-8.0,
                                                 op0=ALU.min, op1=ALU.max), "vector")
            ctx.wait("gpsimd", vc)
            vz = ctx.inc(nc.gpsimd.memset(cl_sb[0:1, 0, :], 0.0), "gpsimd")
            ctx.wait("scalar", vc, vz)
            vsq = act(ctx, com, sq_sb[:], cl_sb[:], AF.Square)
            ctx.wait("tensor", vsq)
            ins = None
            for kb in range(8):
                ins = nc.tensor.matmul(ps_s[:], com.ones_col_bf[:], sq_sb[:, kb, :],
                                       start=(kb == 0), stop=(kb == 7))
            tmm = ctx.inc(ins, "tensor")
            s2 = rows[0:1, 0:TCH]
            y0 = rows[0:1, TCH:2 * TCH]
            zr = rows[0:1, 2 * TCH:3 * TCH]
            tr = rows[0:1, 3 * TCH:4 * TCH]
            dr = rows[0:1, 4 * TCH:5 * TCH]
            fr = rows[0:1, 5 * TCH:6 * TCH]
            ctx.wait("vector", tmm)
            v0 = ctx.inc(nc.vector.tensor_copy(s2, ps_s[:]), "vector")
            ctx.wait("scalar", v0)
            sy = sqrt_via_ln(ctx, com, y0, s2, eps=1.0 + EPS)
            ctx.wait("vector", sy)
            v1 = ctx.inc(nc.vector.tensor_scalar(out=zr, in0=y0, scalar1=1.0 + EPS,
                                                 scalar2=None, op0=ALU.max), "vector")
            ctx.wait("vector", v1)
            v2 = ctx.inc(nc.vector.tensor_tensor(out=tr, in0=zr, in1=zr,
                                                 op=ALU.mult), "vector")
            ctx.wait("scalar", v2)
            st_ = sqrt_via_ln(ctx, com, tr, tr, eps=-1.0)
            ctx.wait("vector", st_)
            v3 = ctx.inc(nc.vector.tensor_tensor(out=tr, in0=tr, in1=zr,
                                                 op=ALU.add), "vector")
            ctx.wait("scalar", v3)
            sd = act(ctx, com, dr, tr, AF.Ln)
            ctx.wait("vector", sd)
            v4 = ctx.inc(nc.vector.tensor_scalar(out=dr, in0=dr, scalar1=0.001,
                                                 scalar2=None, op0=ALU.max), "vector")
            v5 = ctx.inc(nc.vector.tensor_scalar(out=fr, in0=s2, scalar1=EPS,
                                                 scalar2=None, op0=ALU.max), "vector")
            ctx.wait("scalar", v5)
            sf = sqrt_via_ln(ctx, com, fr, fr, eps=0.0)
            ctx.wait("vector", sf)
            v6 = ctx.inc(nc.vector.tensor_scalar(out=fr, in0=fr, scalar1=EPS,
                                                 scalar2=None, op0=ALU.add), "vector")
            ctx.wait("vector", v6)
            v7 = ctx.inc(nc.vector.reciprocal(fr, fr), "vector")
            ctx.wait("vector", v4, v7)
            v8 = ctx.inc(nc.vector.tensor_tensor(out=fr, in0=fr, in1=dr,
                                                 op=ALU.mult), "vector")
            ctx.wait("tensor", v8)
            bset = ctx.inc(nc.tensor.matmul(ps_b[:], com.ones_row_f32[:], fr,
                                            start=True, stop=True), "tensor")
            ctx.wait("vector", bset)
            vv = []
            for kb in range(8):
                vv.append(ctx.inc(nc.vector.tensor_tensor(
                    out=v_sb[:, kb, :], in0=cl_sb[:, kb, :], in1=ps_b[:],
                    op=ALU.mult), "vector"))
            ctx.wait("tensor", *vv)
            ctx.wait("vector", tgh, ctx.tok("l1a_qgst"), ctx.tok("l1a_kst"))
            st_toks = []
            for mi in range(8):
                pp = ps[mi % 2]
                if ev[mi % 2] is not None:
                    ctx.wait("tensor", ev[mi % 2])
                msl = slice(mi * 128, (mi + 1) * 128)
                ins = mm_acc(nc, pp[:], [wqT[:, kc, msl] for kc in range(8)],
                             [v_sb[:, kc, :] for kc in range(8)])
                tmm = ctx.inc(ins, "tensor")
                ctx.wait("vector", tmm)
                ev[mi % 2] = ctx.inc(nc.vector.tensor_tensor(
                    out=qg_sb[:, mi, :], in0=pp[:], in1=gh_sb[:, mi, :],
                    op=ALU.mult), "vector")
            for mi in range(8):
                pp = ps[mi % 2]
                ctx.wait("tensor", ev[mi % 2])
                msl = slice(mi * 128, (mi + 1) * 128)
                ins = mm_acc(nc, pp[:], [wkT[:, kc, msl] for kc in range(8)],
                             [v_sb[:, kc, :] for kc in range(8)])
                tmm = ctx.inc(ins, "tensor")
                ctx.wait("vector", tmm)
                ev[mi % 2] = ctx.inc(nc.vector.tensor_copy(k_sb[:, mi, :], pp[:]),
                                     "vector")
            ctx.wait("sync", *[ev[0], ev[1]])
            st_toks.append(ctx.dma("sync", "l1a_qgst", qgr[:, :, cj], qg_sb[:]))
            st_toks.append(ctx.dma("sync", "l1a_kst", kr[:, :, cj], k_sb[:]))
            for tt in range(TCH // 128):
                tsl = slice(cj.start + tt * 128, cj.start + (tt + 1) * 128)
                lsl = slice(tt * 128, (tt + 1) * 128)
                for ni in range(2):
                    pp = ps[ni % 2]
                    ctx.wait("tensor", ev[ni % 2])
                    nsl = slice(ni * 512, (ni + 1) * 512)
                    ins = mm_acc(nc, pp[:], [v_sb[:, kc, lsl] for kc in range(8)],
                                 [wvT[:, kc, nsl] for kc in range(8)])
                    tmm = ctx.inc(ins, "tensor")
                    ctx.wait("vector", tmm)
                    ev[ni % 2] = ctx.inc(nc.vector.tensor_copy(
                        vt_sb[:, tt, nsl], pp[:]), "vector")
            ctx.wait("sync", ev[0], ev[1])
            st_toks.append(ctx.dma("sync", "l1a_vst", vr[:, 4 * j:4 * (j + 1), :], vt_sb[:]))
            prev_users = (ctx.tok("tensor"), ctx.tok("l1a_vst"))
        pass
    ctx.fence()

    QJ = min(1024, T)
    with contextlib.ExitStack() as ls:
        E = ls.enter_context
        kh_sb = E(nc.sbuf_tensor("l1a_kh", [64, T], BF16))
        vh_sb = E(nc.sbuf_tensor("l1a_vh", [128, T // 128, 65], BF16))
        qh_sb = E(nc.sbuf_tensor("l1a_qh", [64, QJ], BF16))
        p_sb = [E(nc.sbuf_tensor(f"l1a_p{i}", [128, QJ], BF16)) for i in range(2)]
        dn_sb = E(nc.sbuf_tensor("l1a_dn", [1, QJ], F32))
        db_sb = E(nc.sbuf_tensor("l1a_db", [64, QJ], F32))
        ao_sb = E(nc.sbuf_tensor("l1a_ao", [64, QJ], BF16))
        ps_st = [E(nc.psum_tensor(f"l1a_pst{i}", [128, QJ], F32)) for i in range(2)]
        ps_o = E(nc.psum_tensor("l1a_ps_o", [128, QJ], F32))
        ps_b2 = E(nc.psum_tensor("l1a_ps_b2", [64, QJ], F32))
        nkt = T // 128
        tk_tok = None
        tv_tok = None
        ev_exp = [None, None]
        prev_ao = None
        for h in range(H):
            hb = h // 2
            hp = (h % 2) * 64
            if tk_tok is not None:
                ctx.wait("sync", tk_tok)
                ctx.wait("sync", tv_tok)
                ctx.wait("sync", ctx.tok("tensor"))
            tk_tok = ctx.dma("sync", "l1a_khl", kh_sb[:],
                             kr[hp:hp + 64, hb, :])
            tv_tok = ctx.dma("sync", "l1a_vhl", vh_sb[:, :, 0:64],
                             vr[:, :, h * 64:(h + 1) * 64])
            if h == 0:
                g1 = ctx.inc(nc.gpsimd.memset(vh_sb[:, :, 64:65], 1.0), "gpsimd")
                ctx.wait("tensor", g1)
            for jq in range(T // QJ):
                qj = slice(jq * QJ, (jq + 1) * QJ)
                if jq > 0 or h > 0:
                    ctx.wait("sync", ctx.tok("tensor"))
                tql = ctx.dma("sync", "l1a_qhl", qh_sb[:], qgr[hp:hp + 64, hb, qj])
                ctx.wait("tensor", tk_tok, tv_tok, tql)
                first = True
                for kt in range(nkt):
                    pst = ps_st[kt % 2]
                    if ev_exp[kt % 2] is not None:
                        ctx.wait("tensor", ev_exp[kt % 2])
                    ins = None
                    for half in range(QJ // 512):
                        ins = nc.tensor.matmul(
                            pst[:, half * 512:(half + 1) * 512],
                            kh_sb[:, kt * 128:(kt + 1) * 128],
                            qh_sb[:, half * 512 + qj.start - qj.start:  (half + 1) * 512],
                            start=True, stop=True, skip_group_check=True)
                    tmm = ctx.inc(ins, "tensor")
                    ctx.wait("scalar", tmm)
                    texp = act(ctx, com, p_sb[kt % 2][:], pst[:], AF.Exp)
                    ev_exp[kt % 2] = texp
                    ctx.wait("tensor", texp)
                    ins = None
                    for half in range(QJ // 512):
                        ins = nc.tensor.matmul(
                            ps_o[0:65, half * 512:(half + 1) * 512],
                            vh_sb[:, kt, :],
                            p_sb[kt % 2][:, half * 512:(half + 1) * 512],
                            start=(kt == 0), stop=(kt == nkt - 1),
                            skip_group_check=True)
                    ctx.inc(ins, "tensor")
                tpv = ctx.tok("tensor")
                ctx.wait("vector", tpv)
                vr1 = ctx.inc(nc.vector.reciprocal(dn_sb[:], ps_o[64:65, :]), "vector")
                ctx.wait("tensor", vr1)
                ins = None
                for half in range(QJ // 512):
                    hs = slice(half * 512, (half + 1) * 512)
                    ins = nc.tensor.matmul(ps_b2[:, hs], com.ones_row_f32[0:1, 0:64],
                                           dn_sb[:, hs], start=True, stop=True,
                                           skip_group_check=True)
                bset = ctx.inc(ins, "tensor")
                ctx.wait("vector", bset)
                vcp = ctx.inc(nc.vector.tensor_copy(db_sb[:], ps_b2[:]), "vector")
                ctx.wait("vector", vcp)
                if prev_ao is not None:
                    ctx.wait("vector", prev_ao)
                vao = ctx.inc(nc.vector.tensor_tensor(out=ao_sb[:], in0=ps_o[0:64, :],
                                                      in1=db_sb[:], op=ALU.mult), "vector")
                ctx.wait("sync", vao)
                prev_ao = ctx.dma("sync", "l1a_aost", aor[hp:hp + 64, hb, qj], ao_sb[:])
                ctx.wait("tensor", vao)
        pass
    ctx.fence()

    with contextlib.ExitStack() as ls:
        E = ls.enter_context
        tmp_sb = E(nc.sbuf_tensor("l1a_wtmp3", [128, 1024], BF16))
        ps_tr = E(nc.psum_tensor("l1a_ps_tr3", [128, 8, 128], BF16))
        woT, tw_ = load_weight_T(ctx, com, ls, "l1a_woT", w["Wo"], D, D, tmp_sb, ps_tr)
        ao_sb = E(nc.sbuf_tensor("l1a_ao3", [128, 8, TCH], BF16))
        cl_sb = E(nc.sbuf_tensor("l1a_cl3", [128, 8, TCH], F32))
        sq_sb = E(nc.sbuf_tensor("l1a_sq3", [128, 8, TCH], BF16))
        rows = E(nc.sbuf_tensor("l1a_rows3", [1, TCH], F32))
        ps = [E(nc.psum_tensor(f"l1a_ps3{i}", [128, TCH], F32)) for i in range(2)]
        ps_s = E(nc.psum_tensor("l1a_ps_s3", [1, TCH], F32))
        onesz = E(nc.sbuf_tensor("l1a_onesz", [128, 1], BF16))
        g1 = ctx.inc(nc.gpsimd.memset(onesz[:], 1.0), "gpsimd")
        ctx.wait("gpsimd", g1)
        g2 = ctx.inc(nc.gpsimd.memset(onesz[0:1, 0:1], 0.0), "gpsimd")
        ctx.wait("tensor", tw_, g2)
        ev = [None, None]
        prev_users = None
        for j in range(ntc):
            cj = slice(j * TCH, (j + 1) * TCH)
            if prev_users is not None:
                ctx.wait("sync", prev_users)
                ctx.barrier()
            tao = ctx.dma("sync", "l1a_aol", ao_sb[:], aor[:, :, cj])
            ctx.wait("tensor", tao)
            for mi in range(8):
                pp = ps[mi % 2]
                if ev[mi % 2] is not None:
                    ctx.wait("tensor", ev[mi % 2])
                msl = slice(mi * 128, (mi + 1) * 128)
                ins = mm_acc(nc, pp[:], [woT[:, kc, msl] for kc in range(8)],
                             [ao_sb[:, kc, :] for kc in range(8)])
                tmm = ctx.inc(ins, "tensor")
                ctx.wait("vector", tmm)
                if mi == 0:
                    v1 = ctx.inc(nc.vector.tensor_scalar(
                        out=cl_sb[:, mi, :], in0=pp[:], scalar1=0.1, scalar2=None,
                        op0=ALU.mult), "vector")
                    ctx.wait("vector", v1)
                    v1 = ctx.inc(nc.vector.tensor_scalar(
                        out=cl_sb[0:1, 0, :], in0=cl_sb[0:1, 0, :],
                        scalar1=float(math.sqrt(1.0 + EPS)), scalar2=None,
                        op0=ALU.add), "vector")
                    ctx.wait("vector", v1)
                    v1 = ctx.inc(nc.vector.tensor_scalar(
                        out=cl_sb[:, mi, :], in0=cl_sb[:, mi, :], scalar1=8.0,
                        scalar2=-8.0, op0=ALU.min, op1=ALU.max), "vector")
                else:
                    v1 = ctx.inc(nc.vector.tensor_scalar(
                        out=cl_sb[:, mi, :], in0=pp[:], scalar1=0.1, scalar2=8.0,
                        op0=ALU.mult, op1=ALU.min), "vector")
                    ctx.wait("vector", v1)
                    v1 = ctx.inc(nc.vector.tensor_scalar(
                        out=cl_sb[:, mi, :], in0=cl_sb[:, mi, :], scalar1=-8.0,
                        scalar2=None, op0=ALU.max), "vector")
                ev[mi % 2] = v1
            ctx.wait("scalar", ev[0], ev[1])
            vsq = act(ctx, com, sq_sb[:], cl_sb[:], AF.Square)
            ctx.wait("tensor", vsq)
            ins = None
            for kb in range(8):
                ins = nc.tensor.matmul(ps_s[:], onesz[:] if kb == 0 else com.ones_col_bf[:],
                                       sq_sb[:, kb, :], start=(kb == 0), stop=(kb == 7))
            tmm = ctx.inc(ins, "tensor")
            ctx.wait("scalar", tmm)
            sy = sqrt_via_ln(ctx, com, rows[:], ps_s[:], eps=1.0 + EPS)
            ctx.wait("vector", sy)
            vx0 = ctx.inc(nc.vector.tensor_copy(cl_sb[0:1, 0, :], rows[:]), "vector")
            ctx.wait("sync", vx0, ev[0], ev[1])
            tst = ctx.dma("sync", "l1a_ost", outr[:, :, cj], cl_sb[:])
            prev_users = tst
        ctx.wait("sync", ctx.tok("l1a_ost"))
    return



def emit_l3_ffn(ctx, com, T, xn_ch, xn_tok, w, out_hbm, CAP=768):
    nc = ctx.nc
    E_ = 8
    I = 2816
    ntc = T // TCH
    ntt = T // 128
    nst = CAP // 128
    BIG = float(1 << 30)
    idx_hbm = nc.dram_tensor("l3f_idx", [E_ * CAP + 128, 1], U32, kind="Internal").ap()
    ot_hbm = nc.dram_tensor("l3f_ot", [T + 128, D], F32, kind="Internal").ap()
    xnr = xn_ch.rearrange("(kb p) t -> p kb t", p=128)
    outr = out_hbm.rearrange("(kb p) t -> p kb t", p=128)
    otr = ot_hbm.rearrange("(tt p) d -> p tt d", p=128)
    ctx.fence()

    with contextlib.ExitStack() as ls:
        E = ls.enter_context
        rw_sb = E(nc.sbuf_tensor("l3f_rw", [8, D], BF16))
        rwT = E(nc.sbuf_tensor("l3f_rwT", [128, 8, 8], BF16))
        xn_sb = E(nc.sbuf_tensor("l3f_xn", [128, 8, TCH], BF16))
        lg_sb = E(nc.sbuf_tensor("l3f_lg", [8, T], BF16))
        lt_sb = E(nc.sbuf_tensor("l3f_lt", [128, 128], BF16))
        lgt = E(nc.sbuf_tensor("l3f_lgt", [128, 8], F32))
        mxc = E(nc.sbuf_tensor("l3f_mxc", [128, 1], F32))
        msk = E(nc.sbuf_tensor("l3f_msk", [128, 8], BF16))
        clm = E(nc.sbuf_tensor("l3f_clm", [128, 1], F32))
        eqt = E(nc.sbuf_tensor("l3f_eqt", [128, 1], F32))
        pt_sb = E(nc.sbuf_tensor("l3f_pt", [128, 8], F32))
        dt_sb = E(nc.sbuf_tensor("l3f_dt", [128, 8], F32))
        off_sb = E(nc.sbuf_tensor("l3f_off", [1, 8], F32))
        tot_sb = E(nc.sbuf_tensor("l3f_tot", [1, 8], F32))
        dcol_u = E(nc.sbuf_tensor("l3f_dcolu", [128, 1], U32))
        tval = E(nc.sbuf_tensor("l3f_tval", [128, 1], U32))
        big_sb = E(nc.sbuf_tensor("l3f_big", [128, 1], U32))
        pcol_f = E(nc.sbuf_tensor("l3f_pcolf", [128, 1], F32))
        zrow = E(nc.sbuf_tensor("l3f_zrow", [128, D], BF16))
        ps = E(nc.psum_tensor("l3f_ps", [128, TCH], F32))
        ps2 = E(nc.psum_tensor("l3f_ps2", [128, 512], F32))

        trw = ctx.dma("gpsimd", "l3f_rwld", rw_sb[:], w["router_w"])
        g1 = ctx.inc(nc.gpsimd.memset(lt_sb[:], 0.0), "gpsimd")
        ctx.wait("gpsimd", g1, com.ready)
        g2 = ctx.inc(nc.gpsimd.affine_select(
            out=lt_sb[:], in_=lt_sb[:], compare_op=ALU.is_ge, fill=1.0,
            base=-1, pattern=[[-1, 128]], channel_multiplier=1), "gpsimd")
        g3 = ctx.inc(nc.gpsimd.iota(big_sb[:], pattern=[[0, 1]], base=T,
                                    channel_multiplier=1), "gpsimd")
        g4 = ctx.inc(nc.gpsimd.memset(off_sb[:], 0.0), "gpsimd")
        g5 = ctx.inc(nc.gpsimd.iota(pcol_f[:], pattern=[[0, 1]], base=E_ * CAP,
                                    channel_multiplier=1,
                                    allow_small_or_imprecise_dtypes=True), "gpsimd")
        g6 = ctx.inc(nc.gpsimd.memset(zrow[:], 0.0), "gpsimd")
        ctx.wait("sync", g6)
        ctx.dma("sync", "l3f_ztail", xn_tok[T:T + 128, :], zrow[:])
        ctx.wait("tensor", trw, com.ready, g2, g3, g4, g5)
        tt2 = None
        for kc in range(8):
            ins = nc.tensor.matmul(ps2[0:128, 0:8],
                                   rw_sb[0:8, kc * 128:(kc + 1) * 128],
                                   com.ident_bf[0:8, 0:8], start=True, stop=True,
                                   skip_group_check=True)
            tt1 = ctx.inc(ins, "tensor")
            ctx.wait("vector", tt1)
            if tt2 is not None:
                ctx.wait("vector", tt2)
            tt2 = ctx.inc(nc.vector.tensor_copy(rwT[:, kc, :], ps2[0:128, 0:8]), "vector")
            ctx.wait("tensor", tt2)
        prev = None
        for j in range(ntc):
            cj = slice(j * TCH, (j + 1) * TCH)
            if prev is not None:
                ctx.wait("sync", prev)
            txn = ctx.dma("sync", "l3f_xnl", xn_sb[:], xnr[:, :, cj])
            ctx.wait("tensor", txn)
            ins = mm_acc(nc, ps[0:8, :], [rwT[:, kc, :] for kc in range(8)],
                         [xn_sb[:, kc, :] for kc in range(8)])
            tmm = ctx.inc(ins, "tensor")
            ctx.wait("vector", tmm)
            tv = ctx.inc(nc.vector.tensor_copy(lg_sb[0:8, cj], ps[0:8, :]), "vector")
            prev = ctx.tok("tensor")
            ctx.wait("sync", tv)
        ctx.wait("tensor", ctx.tok("vector"))
        fill_done = False
        prev_v = None
        for tt in range(ntt):
            tsl = slice(tt * 128, (tt + 1) * 128)
            ins = nc.tensor.matmul(ps2[0:128, 0:8], lg_sb[0:8, tsl],
                                   com.ident_bf[0:8, 0:8], start=True, stop=True,
                                   skip_group_check=True)
            tmm = ctx.inc(ins, "tensor")
            ctx.wait("vector", tmm)
            if prev_v is not None:
                ctx.wait("vector", prev_v)
                ctx.wait("vector", ctx.tok("l3f_scat"))
            v0 = ctx.inc(nc.vector.tensor_copy(lgt[:], ps2[0:128, 0:8]), "vector")
            ctx.wait("vector", v0)
            v1 = ctx.inc(nc.vector.reduce_max(mxc[:], lgt[:], axis=AX.X), "vector")
            ctx.wait("vector", v1)
            vcl = ctx.inc(nc.vector.memset(clm[:], 0.0), "vector")
            for e in range(E_):
                ctx.wait("vector", vcl)
                ve = ctx.inc(nc.vector.tensor_scalar(
                    out=eqt[:], in0=lgt[:, e:e + 1], scalar1=mxc[:], scalar2=None,
                    op0=ALU.is_ge), "vector")
                ctx.wait("vector", ve)
                ve = ctx.inc(nc.vector.tensor_tensor(
                    out=msk[:, e:e + 1], in0=eqt[:], in1=clm[:], op=ALU.mult), "vector")
                ctx.wait("vector", ve)
                ve = ctx.inc(nc.vector.tensor_tensor(
                    out=msk[:, e:e + 1], in0=eqt[:], in1=msk[:, e:e + 1],
                    op=ALU.subtract), "vector")
                ctx.wait("vector", ve)
                vcl = ctx.inc(nc.vector.tensor_tensor(
                    out=clm[:], in0=clm[:], in1=msk[:, e:e + 1], op=ALU.add), "vector")
            ctx.wait("tensor", vcl)
            ins = nc.tensor.matmul(ps2[0:128, 8:16], lt_sb[:], msk[:],
                                   start=True, stop=True, skip_group_check=True)
            ins = nc.tensor.matmul(ps2[0:1, 16:24], com.ones_col_bf[:], msk[:],
                                   start=True, stop=True, skip_group_check=True)
            tmm = ctx.inc(ins, "tensor")
            ctx.wait("vector", tmm)
            v2 = ctx.inc(nc.vector.tensor_copy(pt_sb[:], ps2[0:128, 8:16]), "vector")
            v3 = ctx.inc(nc.vector.tensor_copy(tot_sb[:], ps2[0:1, 16:24]), "vector")
            ctx.wait("tensor", v2)
            bins = nc.tensor.matmul(ps2[0:128, 24:32], com.ones_row_f32[:],
                                    off_sb[:], start=True, stop=True,
                                    skip_group_check=True)
            tb = ctx.inc(bins, "tensor")
            ctx.wait("vector", tb)
            v4 = ctx.inc(nc.vector.tensor_tensor(out=pt_sb[:], in0=pt_sb[:],
                                                 in1=ps2[0:128, 24:32], op=ALU.add), "vector")
            ctx.wait("vector", v4)
            v5 = ctx.inc(nc.vector.tensor_scalar(
                out=dt_sb[:], in0=pt_sb[:], scalar1=float(CAP), scalar2=BIG,
                op0=ALU.is_gt, op1=ALU.mult), "vector")
            ctx.wait("vector", v5)
            v6 = ctx.inc(nc.vector.tensor_tensor(out=dt_sb[:], in0=dt_sb[:],
                                                 in1=pt_sb[:], op=ALU.add), "vector")
            ctx.wait("vector", v6)
            for e in range(E_):
                v7 = ctx.inc(nc.vector.tensor_scalar(
                    out=dt_sb[:, e:e + 1], in0=dt_sb[:, e:e + 1],
                    scalar1=float(e * CAP - 1), scalar2=None, op0=ALU.add), "vector")
            ctx.wait("vector", v7)
            v8 = ctx.inc(nc.vector.tensor_tensor(out=dt_sb[:], in0=dt_sb[:],
                                                 in1=msk[:], op=ALU.mult), "vector")
            ctx.wait("vector", v8)
            v9 = ctx.inc(nc.vector.tensor_reduce(
                eqt[:].rearrange("p (g a) -> p g a", a=1),
                dt_sb[:].rearrange("p (g f) -> p g f", f=8), axis=AX.X,
                op=ALU.add), "vector")
            ctx.wait("vector", v9)
            v9 = ctx.inc(nc.vector.tensor_tensor(out=eqt[:], in0=eqt[:], in1=pcol_f[:],
                                                 op=ALU.min), "vector")
            ctx.wait("vector", v9)
            v9 = ctx.inc(nc.vector.tensor_copy(dcol_u[:], eqt[:]), "vector")
            ctx.wait("vector", v3, v9)
            voff = ctx.inc(nc.vector.tensor_tensor(out=off_sb[:], in0=off_sb[:],
                                                   in1=tot_sb[:], op=ALU.add), "vector")
            prev_v = voff
            if not fill_done:
                nfill = (E_ * CAP) // 128
                for i in range(nfill):
                    ctx.dma("sync", "l3f_fill", idx_hbm[i * 128:(i + 1) * 128, :],
                            big_sb[:, 0:1])
                fill_done = True
            ctx.wait("gpsimd", ctx.tok("l3f_scat"))
            gi = ctx.inc(nc.gpsimd.iota(tval[:], pattern=[[0, 1]], base=tt * 128,
                                        channel_multiplier=1), "gpsimd")
            ctx.wait("gpsimd", v9, gi, ctx.tok("l3f_fill"))
            tsc = ctx.idma("l3f_scat", nc.gpsimd.indirect_dma_start(
                out=idx_hbm, out_offset=IndirectOffsetOnAxis(ap=dcol_u[:], axis=0),
                in_=tval[:], in_offset=None))
            ctx.wait("tensor", v9)
    ctx.fence()

    for e in range(E_):
        with contextlib.ExitStack() as ls:
            E2 = ls.enter_context
            tmp_sb = E2(nc.sbuf_tensor(f"l3f_wtmp{e}", [128, I], BF16))
            ps_tr = E2(nc.psum_tensor(f"l3f_ptr{e}", [128, 8, 128], BF16))
            ps_trf = E2(nc.psum_tensor(f"l3f_ptrf{e}", [128, 4, 128], F32))
            idxc = E2(nc.sbuf_tensor(f"l3f_idxc{e}", [128, nst], U32))
            xg_sb = E2(nc.sbuf_tensor(f"l3f_xg{e}", [128, D], BF16))
            xgT = E2(nc.sbuf_tensor(f"l3f_xgT{e}", [128, 8, CAP], BF16))
            g_sb = E2(nc.sbuf_tensor(f"l3f_g{e}", [128, 22, CAP], BF16))
            ye_sb = E2(nc.sbuf_tensor(f"l3f_ye{e}", [128, 8, CAP], F32))
            yt_sb = E2(nc.sbuf_tensor(f"l3f_yt{e}", [128, D], F32))
            ps = [E2(nc.psum_tensor(f"l3f_ps{e}{i}", [128, TCH], F32)) for i in range(2)]
            tix = None
            for st_ in range(nst):
                lo = e * CAP + st_ * 128
                tix = ctx.dma("sync", f"l3f_ixl{e}", idxc[:, st_:st_ + 1],
                              idx_hbm[lo:lo + 128, 0:1])
            tgt = None
            for st_ in range(nst):
                ctx.wait("gpsimd", tix)
                if tgt is not None:
                    ctx.wait("gpsimd", tgt)
                tga = ctx.idma(f"l3f_ga{e}", nc.gpsimd.indirect_dma_start(
                    out=xg_sb[:], out_offset=None, in_=xn_tok,
                    in_offset=IndirectOffsetOnAxis(ap=idxc[:, st_:st_ + 1], axis=0)))
                ctx.wait("tensor", tga)
                tgt = emit_transpose_to(ctx, com, xg_sb[:],
                                        xgT.ap()[:, :, st_ * 128:(st_ + 1) * 128], ps_tr)
            nch = [(i * 512, min((i + 1) * 512, CAP)) for i in range((CAP + 511) // 512)]
            for phase in range(3):
                wname = ["gate_w", "up_w", "down_w"][phase]
                with contextlib.ExitStack() as ws:
                    if phase < 2:
                        wT, tw_ = load_weight_T(ctx, com, ws, f"l3f_w{e}_{phase}",
                                                w[wname][e], I, D, tmp_sb, ps_tr,
                                                prev=tgt)
                        nmi = 22
                    else:
                        wT, tw_ = load_weight_T(ctx, com, ws, f"l3f_w{e}_{phase}",
                                                w[wname][e], D, I, tmp_sb, ps_tr,
                                                prev=tgt)
                        nmi = 8
                    ctx.wait("tensor", tw_)
                    ev = [None, None]
                    for mi in range(nmi):
                        for (lo, hi) in nch:
                            w_ = hi - lo
                            pp = ps[mi % 2]
                            if ev[mi % 2] is not None:
                                ctx.wait("tensor", ev[mi % 2])
                            msl = slice(mi * 128, (mi + 1) * 128)
                            if phase < 2:
                                ins = mm_acc(nc, pp[:, 0:w_],
                                             [wT[:, kc, msl] for kc in range(8)],
                                             [xgT[:, kc, lo:hi] for kc in range(8)])
                            else:
                                ins = mm_acc(nc, pp[:, 0:w_],
                                             [wT[:, kc, msl] for kc in range(22)],
                                             [g_sb[:, kc, lo:hi] for kc in range(22)])
                            tmm = ctx.inc(ins, "tensor")
                            if phase == 0:
                                ctx.wait("scalar", tmm)
                                ev[mi % 2] = act(ctx, com, g_sb[:, mi, lo:hi],
                                                 pp[:, 0:w_], AF.Silu)
                            elif phase == 1:
                                ctx.wait("vector", tmm)
                                ev[mi % 2] = ctx.inc(nc.vector.tensor_tensor(
                                    out=g_sb[:, mi, lo:hi], in0=g_sb[:, mi, lo:hi],
                                    in1=pp[:, 0:w_], op=ALU.mult), "vector")
                            else:
                                ctx.wait("vector", tmm)
                                ev[mi % 2] = ctx.inc(nc.vector.tensor_copy(
                                    ye_sb[:, mi, lo:hi], pp[:, 0:w_]), "vector")
                    ctx.barrier()
                    tgt = ctx.tok("vector")
            prev_yt = None
            for st_ in range(nst):
                for half in range(2):
                    ins = None
                    for mi in range(4):
                        mig = half * 4 + mi
                        ins = nc.tensor.transpose(
                            ps_trf[0:128, mi, :],
                            ye_sb[:, mig, st_ * 128:(st_ + 1) * 128],
                            com.ident_f32[:])
                    tmm = ctx.inc(ins, "tensor")
                    ctx.wait("vector", tmm)
                    if prev_yt is not None:
                        ctx.wait("vector", prev_yt)
                        prev_yt = None
                    vev = ctx.inc(nc.vector.tensor_copy(
                        yt_sb[:, half * 512:(half + 1) * 512],
                        ps_trf.ap().rearrange("p a b -> p (a b)")), "vector")
                    ctx.wait("tensor", vev)
                ctx.wait("gpsimd", vev)
                tsc = ctx.idma(f"l3f_sc{e}", nc.gpsimd.indirect_dma_start(
                    out=ot_hbm, out_offset=IndirectOffsetOnAxis(
                        ap=idxc[:, st_:st_ + 1], axis=0),
                    in_=yt_sb[:], in_offset=None))
                prev_yt = tsc
            ctx.wait("sync", ctx.tok(f"l3f_sc{e}"))
        ctx.fence()

    with contextlib.ExitStack() as ls:
        E2 = ls.enter_context
        ot_sb = E2(nc.sbuf_tensor("l3f_ots", [128, D], F32))
        oc_sb = E2(nc.sbuf_tensor("l3f_ocs", [128, 8, TCH], F32))
        ps_trf = E2(nc.psum_tensor("l3f_ptrf", [128, 4, 128], F32))
        prev_st = None
        for j in range(ntc):
            for tt in range(TCH // 128):
                tg_ = j * (TCH // 128) + tt
                tld = ctx.dma("sync", "l3f_otl", ot_sb[:], otr[:, tg_, :])
                ctx.wait("tensor", tld)
                for half in range(2):
                    ins = None
                    for kb in range(4):
                        kbg = half * 4 + kb
                        ins = nc.tensor.transpose(
                            ps_trf[0:128, kb, :],
                            ot_sb[:, kbg * 128:(kbg + 1) * 128],
                            com.ident_f32[:])
                    tmm = ctx.inc(ins, "tensor")
                    ctx.wait("vector", tmm)
                    if prev_st is not None:
                        ctx.wait("vector", prev_st)
                        prev_st = None
                    vev = ctx.inc(nc.vector.tensor_copy(
                        oc_sb[:, half * 4:(half + 1) * 4, tt * 128:(tt + 1) * 128],
                        ps_trf[:]), "vector")
                    ctx.wait("tensor", vev)
                ctx.wait("sync", vev, tld)
            cj = slice(j * TCH, (j + 1) * TCH)
            tst = ctx.dma("sync", "l3f_ost", outr[:, :, cj], oc_sb[:])
            prev_st = tst
        ctx.wait("sync", ctx.tok("l3f_ost"))
    return



def emit_l1_ffn(ctx, com, T, xn_ch, xn_tok, w, out_hbm):
    nc = ctx.nc
    E_ = 8
    HP = 128
    NR = 64
    K_ = min(max(T // 8, 2), T)
    assert K_ % 128 == 0
    nst = K_ // 128
    idx_hbm = nc.dram_tensor("l1f_idx", [E_ * K_ + 128, 1], U32, kind="Internal").ap()
    lgt_hbm = nc.dram_tensor("l1f_lgt", [T, 8], F32, kind="Internal").ap()
    ot_hbm = nc.dram_tensor("l1f_ot", [T + 128, D], F32, kind="Internal").ap()
    xnr = xn_ch.rearrange("(kb p) t -> p kb t", p=128)
    outr = out_hbm.rearrange("(kb p) t -> p kb t", p=128)
    otr = ot_hbm.rearrange("(tt p) d -> p tt d", p=128)
    ntc = T // TCH
    ntt = T // 128
    ctx.fence()

    with contextlib.ExitStack() as ls:
        E = ls.enter_context
        rw_sb = E(nc.sbuf_tensor("l1f_rw", [8, D], BF16))
        rwT = E(nc.sbuf_tensor("l1f_rwT", [128, 8, 8], BF16))
        xn_sb = E(nc.sbuf_tensor("l1f_xn", [128, 8, TCH], BF16))
        lg_sb = E(nc.sbuf_tensor("l1f_lg", [8, T], F32))
        lgt_sb = E(nc.sbuf_tensor("l1f_lgtt", [128, 8], F32))
        lo_c = E(nc.sbuf_tensor("l1f_lo", [8, 1], F32))
        hi_c = E(nc.sbuf_tensor("l1f_hi", [8, 1], F32))
        mid_c = E(nc.sbuf_tensor("l1f_mid", [8, 1], F32))
        cnt_c = E(nc.sbuf_tensor("l1f_cnt", [8, 1], F32))
        cnd_c = E(nc.sbuf_tensor("l1f_cnd", [8, 1], F32))
        tmp_r = E(nc.sbuf_tensor("l1f_tmpr", [8, T], F32))
        gmask = E(nc.sbuf_tensor("l1f_gm", [8, T], BF16))
        tie_r = E(nc.sbuf_tensor("l1f_tie", [8, T], F32))
        iota_r = E(nc.sbuf_tensor("l1f_iota", [8, T], F32))
        msk = E(nc.sbuf_tensor("l1f_msk", [128, 8], BF16))
        lt_sb = E(nc.sbuf_tensor("l1f_lt", [128, 128], BF16))
        pt_sb = E(nc.sbuf_tensor("l1f_pt", [128, 8], F32))
        dt_sb = E(nc.sbuf_tensor("l1f_dt", [128, 8], F32))
        off_sb = E(nc.sbuf_tensor("l1f_off", [1, 8], F32))
        tot_sb = E(nc.sbuf_tensor("l1f_tot", [1, 8], F32))
        dcol_u = E(nc.sbuf_tensor("l1f_dcolu", [128, 1], U32))
        tval = E(nc.sbuf_tensor("l1f_tval", [128, 1], U32))
        big_sb = E(nc.sbuf_tensor("l1f_big", [128, 1], U32))
        sltr_sb = E(nc.sbuf_tensor("l1f_sltr", [128, 1], F32))
        ps = E(nc.psum_tensor("l1f_ps", [128, TCH], F32))
        ps2 = E(nc.psum_tensor("l1f_ps2", [128, 512], F32))

        trw = ctx.dma("gpsimd", "l1f_rwld", rw_sb[:], w["Wr"])
        g1 = ctx.inc(nc.gpsimd.memset(lt_sb[:], 0.0), "gpsimd")
        ctx.wait("gpsimd", g1, com.ready)
        g2 = ctx.inc(nc.gpsimd.affine_select(
            out=lt_sb[:], in_=lt_sb[:], compare_op=ALU.is_ge, fill=1.0,
            base=-1, pattern=[[-1, 128]], channel_multiplier=1), "gpsimd")
        g3 = ctx.inc(nc.gpsimd.iota(big_sb[:], pattern=[[0, 1]], base=T,
                                    channel_multiplier=1), "gpsimd")
        g4 = ctx.inc(nc.gpsimd.memset(off_sb[:], 0.0), "gpsimd")
        g5 = ctx.inc(nc.gpsimd.iota(iota_r[:], pattern=[[1, T]], base=0,
                                    channel_multiplier=0,
                                    allow_small_or_imprecise_dtypes=True), "gpsimd")
        g6 = ctx.inc(nc.gpsimd.iota(sltr_sb[:], pattern=[[0, 1]], base=E_ * K_,
                                    channel_multiplier=1,
                                    allow_small_or_imprecise_dtypes=True), "gpsimd")
        ctx.wait("tensor", trw, com.ready, g2, g3, g4, g5, g6)
        ctx.wait("vector", g5, g6)
        tt2 = None
        for kc in range(8):
            ins = nc.tensor.matmul(ps2[0:128, 0:8],
                                   rw_sb[0:8, kc * 128:(kc + 1) * 128],
                                   com.ident_bf[0:8, 0:8], start=True, stop=True,
                                   skip_group_check=True)
            tt1 = ctx.inc(ins, "tensor")
            ctx.wait("vector", tt1)
            if tt2 is not None:
                ctx.wait("vector", tt2)
            tt2 = ctx.inc(nc.vector.tensor_copy(rwT[:, kc, :], ps2[0:128, 0:8]), "vector")
            ctx.wait("tensor", tt2)
        prev = None
        for j in range(ntc):
            cj = slice(j * TCH, (j + 1) * TCH)
            if prev is not None:
                ctx.wait("sync", prev)
            txn = ctx.dma("sync", "l1f_xnl", xn_sb[:], xnr[:, :, cj])
            ctx.wait("tensor", txn)
            ins = mm_acc(nc, ps[0:8, :], [rwT[:, kc, :] for kc in range(8)],
                         [xn_sb[:, kc, :] for kc in range(8)])
            tmm = ctx.inc(ins, "tensor")
            ctx.wait("scalar", tmm)
            tv = act(ctx, com, lg_sb[0:8, cj], ps[0:8, :], AF.Sigmoid)
            prev = ctx.tok("tensor")
            ctx.wait("sync", tv)
        ctx.wait("tensor", ctx.tok("scalar"))
        prev_v = None
        for tt in range(ntt):
            tsl = slice(tt * 128, (tt + 1) * 128)
            ins = nc.tensor.matmul(ps2[0:128, 8:16], lg_sb[0:8, tsl],
                                   com.ident_f32[0:8, 0:8], start=True, stop=True,
                                   skip_group_check=True)
            tmm = ctx.inc(ins, "tensor")
            ctx.wait("vector", tmm)
            if prev_v is not None:
                ctx.wait("vector", prev_v)
            vv = ctx.inc(nc.vector.tensor_copy(lgt_sb[:], ps2[0:128, 8:16]), "vector")
            ctx.wait("sync", vv)
            tst = ctx.dma("sync", "l1f_lgst", lgt_hbm[tt * 128:(tt + 1) * 128, :], lgt_sb[:])
            prev_v = tst
            ctx.wait("tensor", vv)
        v1 = ctx.inc(nc.vector.memset(lo_c[:], 0.0), "vector")
        v2 = ctx.inc(nc.vector.memset(hi_c[:], 1.0), "vector")
        vb = (v1, v2)
        ctx.wait("vector", *vb)
        for it in range(45):
            v = ctx.inc(nc.vector.tensor_tensor(out=mid_c[:], in0=lo_c[:], in1=hi_c[:],
                                                op=ALU.add), "vector")
            ctx.wait("vector", v)
            v = ctx.inc(nc.vector.tensor_scalar(out=mid_c[:], in0=mid_c[:], scalar1=0.5,
                                                scalar2=None, op0=ALU.mult), "vector")
            ctx.wait("vector", v)
            v = ctx.inc(nc.vector.tensor_scalar(out=tmp_r[:], in0=lg_sb[:],
                                                scalar1=mid_c[:], scalar2=None,
                                                op0=ALU.is_gt), "vector")
            ctx.wait("vector", v)
            v = ctx.inc(nc.vector.tensor_reduce(
                cnt_c[:].rearrange("p (g a) -> p g a", a=1),
                tmp_r[:].rearrange("p (g f) -> p g f", f=T), axis=AX.X,
                op=ALU.add), "vector")
            ctx.wait("vector", v)
            v = ctx.inc(nc.vector.tensor_scalar(out=cnd_c[:], in0=cnt_c[:],
                                                scalar1=float(K_), scalar2=None,
                                                op0=ALU.is_ge), "vector")
            ctx.wait("vector", v)
            v = ctx.inc(nc.vector.tensor_tensor(out=tmp_r[:, 0:1], in0=mid_c[:],
                                                in1=lo_c[:], op=ALU.subtract), "vector")
            ctx.wait("vector", v)
            v = ctx.inc(nc.vector.tensor_tensor(out=tmp_r[:, 0:1], in0=tmp_r[:, 0:1],
                                                in1=cnd_c[:], op=ALU.mult), "vector")
            ctx.wait("vector", v)
            v = ctx.inc(nc.vector.tensor_tensor(out=lo_c[:], in0=lo_c[:],
                                                in1=tmp_r[:, 0:1], op=ALU.add), "vector")
            vA = v
            v = ctx.inc(nc.vector.tensor_tensor(out=tmp_r[:, 1:2], in0=mid_c[:],
                                                in1=hi_c[:], op=ALU.subtract), "vector")
            ctx.wait("vector", v)
            v = ctx.inc(nc.vector.tensor_scalar(out=cnd_c[:], in0=cnd_c[:],
                                                scalar1=-1.0, scalar2=1.0,
                                                op0=ALU.mult, op1=ALU.add), "vector")
            ctx.wait("vector", v)
            v = ctx.inc(nc.vector.tensor_tensor(out=tmp_r[:, 1:2], in0=tmp_r[:, 1:2],
                                                in1=cnd_c[:], op=ALU.mult), "vector")
            ctx.wait("vector", v)
            v = ctx.inc(nc.vector.tensor_tensor(out=hi_c[:], in0=hi_c[:],
                                                in1=tmp_r[:, 1:2], op=ALU.add), "vector")
            ctx.wait("vector", v, vA)
        v = ctx.inc(nc.vector.tensor_scalar(out=gmask[:], in0=lg_sb[:],
                                            scalar1=hi_c[:], scalar2=None,
                                            op0=ALU.is_gt), "vector")
        ctx.wait("vector", v)
        v = ctx.inc(nc.vector.tensor_scalar(out=tie_r[:], in0=lg_sb[:],
                                            scalar1=lo_c[:], scalar2=None,
                                            op0=ALU.is_gt), "vector")
        ctx.wait("vector", v)
        v = ctx.inc(nc.vector.tensor_tensor(out=tie_r[:], in0=tie_r[:], in1=gmask[:],
                                            op=ALU.subtract), "vector")
        ctx.wait("vector", v)
        v = ctx.inc(nc.vector.tensor_reduce(
            cnt_c[:].rearrange("p (g a) -> p g a", a=1),
            gmask[:].rearrange("p (g f) -> p g f", f=T), axis=AX.X, op=ALU.add), "vector")
        ctx.wait("vector", v)
        v = ctx.inc(nc.vector.tensor_scalar(out=cnt_c[:], in0=cnt_c[:], scalar1=-1.0,
                                            scalar2=float(K_), op0=ALU.mult,
                                            op1=ALU.add), "vector")
        need_c = cnt_c
        v1 = ctx.inc(nc.vector.memset(lo_c[:], 0.0), "vector")
        v2 = ctx.inc(nc.vector.memset(hi_c[:], float(T)), "vector")
        ctx.wait("vector", v, v1, v2)
        for it in range(14):
            v = ctx.inc(nc.vector.tensor_tensor(out=mid_c[:], in0=lo_c[:], in1=hi_c[:],
                                                op=ALU.add), "vector")
            ctx.wait("vector", v)
            v = ctx.inc(nc.vector.tensor_scalar(out=mid_c[:], in0=mid_c[:], scalar1=0.5,
                                                scalar2=None, op0=ALU.mult), "vector")
            ctx.wait("vector", v)
            v = ctx.inc(nc.vector.tensor_scalar(out=tmp_r[:], in0=iota_r[:],
                                                scalar1=mid_c[:], scalar2=None,
                                                op0=ALU.is_lt), "vector")
            ctx.wait("vector", v)
            v = ctx.inc(nc.vector.tensor_tensor(out=tmp_r[:], in0=tmp_r[:], in1=tie_r[:],
                                                op=ALU.mult), "vector")
            ctx.wait("vector", v)
            v = ctx.inc(nc.vector.tensor_reduce(
                cnd_c[:].rearrange("p (g a) -> p g a", a=1),
                tmp_r[:].rearrange("p (g f) -> p g f", f=T), axis=AX.X, op=ALU.add), "vector")
            ctx.wait("vector", v)
            v = ctx.inc(nc.vector.tensor_tensor(out=cnd_c[:], in0=cnd_c[:], in1=need_c[:],
                                                op=ALU.is_lt), "vector")
            ctx.wait("vector", v)
            v = ctx.inc(nc.vector.tensor_tensor(out=tmp_r[:, 0:1], in0=mid_c[:],
                                                in1=lo_c[:], op=ALU.subtract), "vector")
            ctx.wait("vector", v)
            v = ctx.inc(nc.vector.tensor_tensor(out=tmp_r[:, 0:1], in0=tmp_r[:, 0:1],
                                                in1=cnd_c[:], op=ALU.mult), "vector")
            ctx.wait("vector", v)
            vA = ctx.inc(nc.vector.tensor_tensor(out=lo_c[:], in0=lo_c[:],
                                                 in1=tmp_r[:, 0:1], op=ALU.add), "vector")
            v = ctx.inc(nc.vector.tensor_tensor(out=tmp_r[:, 1:2], in0=mid_c[:],
                                                in1=hi_c[:], op=ALU.subtract), "vector")
            ctx.wait("vector", v)
            v = ctx.inc(nc.vector.tensor_scalar(out=cnd_c[:], in0=cnd_c[:], scalar1=-1.0,
                                                scalar2=1.0, op0=ALU.mult,
                                                op1=ALU.add), "vector")
            ctx.wait("vector", v)
            v = ctx.inc(nc.vector.tensor_tensor(out=tmp_r[:, 1:2], in0=tmp_r[:, 1:2],
                                                in1=cnd_c[:], op=ALU.mult), "vector")
            ctx.wait("vector", v)
            v = ctx.inc(nc.vector.tensor_tensor(out=hi_c[:], in0=hi_c[:],
                                                in1=tmp_r[:, 1:2], op=ALU.add), "vector")
            ctx.wait("vector", v, vA)
        v = ctx.inc(nc.vector.tensor_scalar(out=tmp_r[:], in0=iota_r[:],
                                            scalar1=hi_c[:], scalar2=None,
                                            op0=ALU.is_lt), "vector")
        ctx.wait("vector", v)
        v = ctx.inc(nc.vector.tensor_tensor(out=tmp_r[:], in0=tmp_r[:], in1=tie_r[:],
                                            op=ALU.mult), "vector")
        ctx.wait("vector", v)
        v = ctx.inc(nc.vector.tensor_tensor(out=tmp_r[:], in0=tmp_r[:], in1=gmask[:],
                                            op=ALU.add), "vector")
        maskrow_done = v
        ctx.wait("vector", maskrow_done)
        vbf = ctx.inc(nc.vector.tensor_copy(gmask[:], tmp_r[:]), "vector")
        ctx.wait("tensor", vbf)
        fill_done = False
        prev_v = None
        for tt in range(ntt):
            tsl = slice(tt * 128, (tt + 1) * 128)
            ins = nc.tensor.matmul(ps2[0:128, 0:8], gmask[0:8, tsl],
                                   com.ident_bf[0:8, 0:8], start=True, stop=True,
                                   skip_group_check=True)
            tmm = ctx.inc(ins, "tensor")
            ctx.wait("vector", tmm)
            if prev_v is not None:
                ctx.wait("vector", prev_v)
                ctx.wait("vector", ctx.tok("l1f_scat"))
            v0 = ctx.inc(nc.vector.tensor_copy(msk[:], ps2[0:128, 0:8]), "vector")
            ctx.wait("tensor", v0)
            ins = nc.tensor.matmul(ps2[0:128, 8:16], lt_sb[:], msk[:],
                                   start=True, stop=True, skip_group_check=True)
            ins2 = nc.tensor.matmul(ps2[0:1, 16:24], com.ones_col_bf[:], msk[:],
                                    start=True, stop=True, skip_group_check=True)
            bins = nc.tensor.matmul(ps2[0:128, 24:32], com.ones_row_f32[:],
                                    off_sb[:], start=True, stop=True,
                                    skip_group_check=True)
            tmm = ctx.inc(bins, "tensor")
            ctx.wait("vector", tmm)
            v2 = ctx.inc(nc.vector.tensor_copy(tot_sb[:], ps2[0:1, 16:24]), "vector")
            v3 = ctx.inc(nc.vector.tensor_copy(pt_sb[:], ps2[0:128, 24:32]),
                         "vector")
            ctx.wait("vector", v3)
            v3 = ctx.inc(nc.vector.tensor_tensor(out=pt_sb[:], in0=pt_sb[:],
                                                 in1=ps2[0:128, 8:16],
                                                 op=ALU.add), "vector")
            ctx.wait("vector", v3)
            for e in range(E_):
                v4 = ctx.inc(nc.vector.tensor_scalar(
                    out=dt_sb[:, e:e + 1], in0=pt_sb[:, e:e + 1],
                    scalar1=float(e * K_ - 1), scalar2=None, op0=ALU.add), "vector")
            ctx.wait("vector", v4)
            if not fill_done:
                nfill = (E_ * K_) // 128 + 1
                for i in range(nfill):
                    ctx.dma("sync", "l1f_fill", idx_hbm[i * 128:(i + 1) * 128, :],
                            big_sb[:, 0:1])
                fill_done = True
            ctx.wait("gpsimd", ctx.tok("l1f_scat"))
            gi = ctx.inc(nc.gpsimd.iota(tval[:], pattern=[[0, 1]], base=tt * 128,
                                        channel_multiplier=1), "gpsimd")
            for e in range(E_):
                v5 = ctx.inc(nc.vector.tensor_tensor(
                    out=dt_sb[:, e:e + 1], in0=dt_sb[:, e:e + 1], in1=sltr_sb[:],
                    op=ALU.subtract), "vector")
                ctx.wait("vector", v5)
                v5 = ctx.inc(nc.vector.tensor_tensor(
                    out=dt_sb[:, e:e + 1], in0=dt_sb[:, e:e + 1], in1=msk[:, e:e + 1],
                    op=ALU.mult), "vector")
                ctx.wait("vector", v5)
                v5 = ctx.inc(nc.vector.tensor_tensor(
                    out=dt_sb[:, e:e + 1], in0=dt_sb[:, e:e + 1], in1=sltr_sb[:],
                    op=ALU.add), "vector")
                ctx.wait("vector", v5)
                v6 = ctx.inc(nc.vector.tensor_copy(dcol_u[:], dt_sb[:, e:e + 1]), "vector")
                ctx.wait("gpsimd", v6, gi, ctx.tok("l1f_fill"))
                tsc = ctx.idma("l1f_scat", nc.gpsimd.indirect_dma_start(
                    out=idx_hbm, out_offset=IndirectOffsetOnAxis(ap=dcol_u[:], axis=0),
                    in_=tval[:], in_offset=None))
                ctx.wait("vector", tsc)
            ctx.wait("vector", v2)
            voff = ctx.inc(nc.vector.tensor_tensor(out=off_sb[:], in0=off_sb[:],
                                                   in1=tot_sb[:], op=ALU.add), "vector")
            prev_v = voff
            ctx.wait("tensor", voff)
    ctx.fence()

    SQ = 1.0 / math.sqrt(HP)
    with contextlib.ExitStack() as ls:
        E2 = ls.enter_context
        wq_sb = E2(nc.sbuf_tensor("l1f_wqkv", [128, 8, 384], BF16))
        wo_sb = E2(nc.sbuf_tensor("l1f_wo", [128, 1, 1024], BF16))
        idxc = E2(nc.sbuf_tensor("l1f_idxc", [128, nst], U32))
        posf = E2(nc.sbuf_tensor("l1f_posf", [128, nst], F32))
        pos_r = E2(nc.sbuf_tensor("l1f_posr", [1, K_], F32))
        invf = E2(nc.sbuf_tensor("l1f_invf", [1, 32], F32))
        xg_sb = E2(nc.sbuf_tensor("l1f_xg", [128, D], BF16))
        xgT = E2(nc.sbuf_tensor("l1f_xgT", [128, 8, K_], BF16))
        qkv = E2(nc.sbuf_tensor("l1f_qkv", [128, 3, K_], BF16))
        rtm = E2(nc.sbuf_tensor("l1f_rtm", [64, 64], BF16))
        tb_c = E2(nc.sbuf_tensor("l1f_tbc", [64, K_], F32))
        tb_s = E2(nc.sbuf_tensor("l1f_tbs", [64, K_], F32))
        rn_sb = E2(nc.sbuf_tensor("l1f_rn", [32, K_], F32))
        vt_sb = E2(nc.sbuf_tensor("l1f_vt", [128, nst, 128], BF16))
        p_sb = E2(nc.sbuf_tensor("l1f_p", [128, K_], BF16))
        den_r = E2(nc.sbuf_tensor("l1f_den", [1, K_], F32))
        val_r = E2(nc.sbuf_tensor("l1f_val", [1, K_], F32))
        vals_t = E2(nc.sbuf_tensor("l1f_valst", [128, 8], F32))
        sc_bc = E2(nc.sbuf_tensor("l1f_scbc", [128, K_], F32))
        nsc = E2(nc.sbuf_tensor("l1f_nsc", [128, K_], BF16))
        ye_sb = E2(nc.sbuf_tensor("l1f_ye", [128, 8, K_], F32))
        yt_sb = E2(nc.sbuf_tensor("l1f_yt", [128, D], F32))
        zrow = E2(nc.sbuf_tensor("l1f_zrow", [128, D], F32))
        ps_a = [E2(nc.psum_tensor(f"l1f_psa{i}", [128, TCH], F32)) for i in range(2)]
        ps_n = E2(nc.psum_tensor("l1f_psn", [128, TCH], F32))
        ps_d = E2(nc.psum_tensor("l1f_psd", [128, TCH], F32))
        ps_tr = E2(nc.psum_tensor("l1f_ptr", [128, 8, 128], BF16))
        ps_trf = E2(nc.psum_tensor("l1f_ptrf", [128, 4, 128], F32))

        cmsk = E2(nc.sbuf_tensor("l1f_cmsk", [128, nst, K_], BF16))
        gcm = ctx.inc(nc.gpsimd.memset(cmsk[:], 0.0), "gpsimd")
        ctx.wait("gpsimd", gcm)
        for st_ in range(nst):
            gcm = ctx.inc(nc.gpsimd.affine_select(
                out=cmsk[:, st_, :], in_=cmsk[:, st_, :], compare_op=ALU.is_ge,
                fill=1.0, base=st_ * 128 - 1, pattern=[[-1, K_]],
                channel_multiplier=1), "gpsimd")
            ctx.wait("gpsimd", gcm)
        ctx.wait("vector", gcm)
        gz = ctx.inc(nc.gpsimd.memset(zrow[:], 0.0), "gpsimd")
        ctx.wait("sync", gz)
        for i in range(T // 128 + 1):
            ctx.dma("sync", "l1f_zot", ot_hbm[i * 128:(i + 1) * 128, :], zrow[:])
        gi = ctx.inc(nc.gpsimd.iota(invf[:], pattern=[[1, 32]], base=0,
                                    channel_multiplier=0,
                                    allow_small_or_imprecise_dtypes=True), "gpsimd")
        ctx.wait("scalar", gi)
        si = act(ctx, com, invf[:], invf[:], AF.Exp, scale=-math.log(10000.0) / 32.0)
        g1 = ctx.inc(nc.gpsimd.memset(rtm[:], 0.0), "gpsimd")
        ctx.wait("gpsimd", g1)
        g2 = ctx.inc(nc.gpsimd.affine_select(
            out=rtm[:], in_=rtm[:], compare_op=ALU.not_equal, fill=-1.0,
            base=-32, pattern=[[-1, 64]], channel_multiplier=1), "gpsimd")
        ctx.wait("gpsimd", g2)
        g3 = ctx.inc(nc.gpsimd.affine_select(
            out=rtm[:], in_=rtm[:], compare_op=ALU.not_equal, fill=1.0,
            base=32, pattern=[[-1, 64]], channel_multiplier=1), "gpsimd")
        ctx.wait("tensor", g3, si)
        for e in range(E_):
            twq = ctx.dma("gpsimd", "l1f_wql", wq_sb[:], w["Wqkv"][e].rearrange(
                "(kc p) m -> p kc m", p=128))
            two = ctx.dma("gpsimd", "l1f_wol", wo_sb[:], w["Wo"][e].rearrange(
                "(kc p) m -> p kc m", p=128))
            tix = None
            for st_ in range(nst):
                lo = e * K_ + st_ * 128
                tix = ctx.dma("sync", "l1f_ixl", idxc[:, st_:st_ + 1],
                              idx_hbm[lo:lo + 128, 0:1])
            ctx.wait("vector", tix)
            vp = ctx.inc(nc.vector.tensor_copy(posf[:], idxc[:]), "vector")
            tgt = None
            for st_ in range(nst):
                ctx.wait("gpsimd", tix)
                if tgt is not None:
                    ctx.wait("gpsimd", tgt)
                tga = ctx.idma("l1f_ga", nc.gpsimd.indirect_dma_start(
                    out=xg_sb[:], out_offset=None, in_=xn_tok,
                    in_offset=IndirectOffsetOnAxis(ap=idxc[:, st_:st_ + 1], axis=0)))
                ctx.wait("tensor", tga)
                tgt = emit_transpose_to(ctx, com, xg_sb[:],
                                        xgT.ap()[:, :, st_ * 128:(st_ + 1) * 128], ps_tr)
                ctx.wait("tensor", vp)
                ins = nc.tensor.matmul(ps_trf[0:1, 0, :], posf[:, st_:st_ + 1],
                                       com.ident_f32[:], start=True, stop=True,
                                       skip_group_check=True)
                tmm = ctx.inc(ins, "tensor")
                ctx.wait("vector", tmm, vp)
                vp2 = ctx.inc(nc.vector.tensor_copy(pos_r[0:1, st_ * 128:(st_ + 1) * 128],
                                                    ps_trf[0:1, 0, :]), "vector")
                ctx.wait("tensor", vp2)
            ctx.wait("gpsimd", tgt)
            tvv = None
            for st_ in range(nst):
                tvg = ctx.idma("l1f_vg", nc.gpsimd.indirect_dma_start(
                    out=vals_t[:], out_offset=None, in_=lgt_hbm,
                    in_offset=IndirectOffsetOnAxis(ap=idxc[:, st_:st_ + 1], axis=0)))
                ctx.wait("tensor", tvg)
                ins = nc.tensor.matmul(ps_trf[0:1, 1, :], vals_t[:, e:e + 1],
                                       com.ident_f32[:], start=True, stop=True,
                                       skip_group_check=True)
                tmm = ctx.inc(ins, "tensor")
                ctx.wait("vector", tmm)
                if tvv is not None:
                    ctx.wait("vector", tvv)
                tvv = ctx.inc(nc.vector.tensor_copy(val_r[0:1, st_ * 128:(st_ + 1) * 128],
                                                    ps_trf[0:1, 1, :]), "vector")
                ctx.wait("gpsimd", tvv)
                ctx.wait("tensor", tvv)
            ctx.wait("tensor", twq, tgt)
            ev = [None, None]
            for mi in range(3):
                pp = ps_a[mi % 2]
                if ev[mi % 2] is not None:
                    ctx.wait("tensor", ev[mi % 2])
                msl = slice(mi * 128, (mi + 1) * 128)
                ins = mm_acc(nc, pp[:, 0:K_], [wq_sb[:, kc, msl] for kc in range(8)],
                             [xgT[:, kc, :] for kc in range(8)])
                tmm = ctx.inc(ins, "tensor")
                ctx.wait("vector", tmm)
                ev[mi % 2] = ctx.inc(nc.vector.tensor_copy(qkv[:, mi, :], pp[:, 0:K_]),
                                     "vector")
            ctx.wait("tensor", ev[0], ev[1])
            for half, tb in ((0, tb_c), (1, tb_s)):
                ins = nc.tensor.matmul(ps_n[0:32, 0:K_], invf[:], pos_r[:],
                                       start=True, stop=True, skip_group_check=True)
                tmm = ctx.inc(ins, "tensor")
                ctx.wait("vector", tmm)
                sh = math.pi / 2 if half == 0 else 0.0
                v1 = ctx.inc(nc.vector.tensor_scalar(
                    out=tb[0:32, :], in0=ps_n[0:32, 0:K_], scalar1=float(sh),
                    scalar2=None, op0=ALU.add), "vector")
                ctx.wait("vector", v1)
                v2 = ctx.inc(nc.vector.tensor_scalar(
                    out=rn_sb[:], in0=tb[0:32, :],
                    scalar1=float(1.0 / (2 * math.pi)),
                    scalar2=float(2 ** 23), op0=ALU.mult, op1=ALU.add), "vector")
                ctx.wait("vector", v2)
                v2 = ctx.inc(nc.vector.tensor_scalar(
                    out=rn_sb[:], in0=rn_sb[:], scalar1=float(2 ** 23),
                    scalar2=None, op0=ALU.subtract), "vector")
                ctx.wait("vector", v2)
                v3 = ctx.inc(nc.vector.scalar_tensor_tensor(
                    out=tb[0:32, :], in0=rn_sb[:],
                    scalar=float(-2 * math.pi), in1=tb[0:32, :],
                    op0=ALU.mult, op1=ALU.add), "vector")
                ctx.wait("scalar", v3)
                s1 = act(ctx, com, tb[0:32, :], tb[0:32, :], AF.Sin)
                ctx.wait("vector", s1)
                v4 = ctx.inc(nc.vector.tensor_copy(tb[32:64, :], tb[0:32, :]), "vector")
                ctx.wait("tensor", v4)
            for qi in range(2):
                src = qkv[0:64, qi, :]
                ins = nc.tensor.matmul(ps_n[0:64, 0:K_], rtm[:], src,
                                       start=True, stop=True, skip_group_check=True)
                tmm = ctx.inc(ins, "tensor")
                ctx.wait("vector", tmm)
                v1 = ctx.inc(nc.vector.tensor_tensor(out=sc_bc[0:64, 0:K_],
                                                     in0=ps_n[0:64, 0:K_],
                                                     in1=tb_s[:], op=ALU.mult), "vector")
                ctx.wait("vector", v1)
                v2 = ctx.inc(nc.vector.tensor_tensor(
                    out=qkv[0:64, qi, :], in0=src, in1=tb_c[:],
                    op=ALU.mult), "vector")
                ctx.wait("vector", v2)
                v3 = ctx.inc(nc.vector.tensor_tensor(
                    out=qkv[0:64, qi, :], in0=qkv[0:64, qi, :],
                    in1=sc_bc[0:64, 0:K_], op=ALU.add), "vector")
                ctx.wait("tensor", v3)
            tvt = None
            for st_ in range(nst):
                ins = nc.tensor.transpose(ps_tr[0:128, 0, :],
                                          qkv[:, 2, st_ * 128:(st_ + 1) * 128],
                                          com.ident_bf[:])
                tmm = ctx.inc(ins, "tensor")
                ctx.wait("vector", tmm)
                if tvt is not None:
                    ctx.wait("vector", tvt)
                tvt = ctx.inc(nc.vector.tensor_copy(vt_sb[:, st_, :],
                                                    ps_tr[0:128, 0, :]), "vector")
                ctx.wait("tensor", tvt)
            first = True
            ev_exp = None
            for st_ in range(nst):
                pp = ps_a[st_ % 2]
                if ev[st_ % 2] is not None:
                    ctx.wait("tensor", ev[st_ % 2])
                    ev[st_ % 2] = None
                if ev_exp is not None:
                    ctx.wait("tensor", ev_exp)
                ins = nc.tensor.matmul(pp[:, 0:K_], qkv[:, 1, st_ * 128:(st_ + 1) * 128],
                                       qkv[:, 0, :], start=True, stop=True,
                                       skip_group_check=True)
                tmm = ctx.inc(ins, "tensor")
                ctx.wait("scalar", tmm)
                sx = act(ctx, com, p_sb[:, 0:K_], pp[:, 0:K_], AF.Exp, scale=SQ)
                ctx.wait("vector", sx)
                gm = ctx.inc(nc.vector.tensor_tensor(
                    out=p_sb[:, 0:K_], in0=p_sb[:, 0:K_], in1=cmsk[:, st_, :],
                    op=ALU.mult), "vector")
                ev_exp = gm
                ctx.wait("tensor", gm, tvt)
                nc.tensor.matmul(ps_n[0:128, 0:K_], vt_sb[:, st_, :], p_sb[:, 0:K_],
                                 start=(st_ == 0), stop=(st_ == nst - 1),
                                 skip_group_check=True)
                ins = nc.tensor.matmul(ps_d[0:1, 0:K_], com.ones_col_bf[:],
                                       p_sb[:, 0:K_], start=(st_ == 0),
                                       stop=(st_ == nst - 1), skip_group_check=True)
                ctx.inc(ins, "tensor")
            tpv = ctx.tok("tensor")
            ctx.wait("vector", tpv)
            v1 = ctx.inc(nc.vector.reciprocal(den_r[:], ps_d[0:1, 0:K_]), "vector")
            ctx.wait("vector", v1, tvv)
            v2 = ctx.inc(nc.vector.tensor_tensor(out=den_r[:], in0=den_r[:],
                                                 in1=val_r[:], op=ALU.mult), "vector")
            ctx.wait("tensor", v2)
            bins = ctx.inc(nc.tensor.matmul(ps_d[0:128, 0:K_] if K_ <= 512 else None,
                                            com.ones_row_f32[:], den_r[:],
                                            start=True, stop=True,
                                            skip_group_check=True), "tensor")
            ctx.wait("vector", bins)
            v3 = ctx.inc(nc.vector.tensor_copy(sc_bc[:, 0:K_], ps_d[0:128, 0:K_]), "vector")
            ctx.wait("vector", v3)
            v4 = ctx.inc(nc.vector.tensor_tensor(out=nsc[:, 0:K_], in0=ps_n[0:128, 0:K_],
                                                 in1=sc_bc[:, 0:K_], op=ALU.mult), "vector")
            ctx.wait("tensor", v4, two)
            ev5 = [None, None]
            for mi in range(8):
                pp = ps_a[mi % 2]
                if ev5[mi % 2] is not None:
                    ctx.wait("tensor", ev5[mi % 2])
                msl = slice(mi * 128, (mi + 1) * 128)
                ins = nc.tensor.matmul(pp[:, 0:K_], wo_sb[:, 0, msl], nsc[:, 0:K_],
                                       start=True, stop=True, skip_group_check=True)
                tmm = ctx.inc(ins, "tensor")
                ctx.wait("vector", tmm)
                ev5[mi % 2] = ctx.inc(nc.vector.tensor_copy(ye_sb[:, mi, :], pp[:, 0:K_]),
                                      "vector")
            ctx.wait("tensor", ev5[0], ev5[1])
            prev_yt = None
            for st_ in range(nst):
                for half in range(2):
                    ins = None
                    for mi in range(4):
                        mig = half * 4 + mi
                        ins = nc.tensor.transpose(
                            ps_trf[0:128, mi, :],
                            ye_sb[:, mig, st_ * 128:(st_ + 1) * 128],
                            com.ident_f32[:])
                    tmm = ctx.inc(ins, "tensor")
                    ctx.wait("vector", tmm)
                    if prev_yt is not None:
                        ctx.wait("vector", prev_yt)
                        prev_yt = None
                    vev = ctx.inc(nc.vector.tensor_copy(
                        yt_sb[:, half * 512:(half + 1) * 512],
                        ps_trf.ap().rearrange("p a b -> p (a b)")), "vector")
                    ctx.wait("tensor", vev)
                ctx.wait("gpsimd", vev, ctx.tok("l1f_sc"), ctx.tok("l1f_zot"))
                tsc = ctx.idma("l1f_sc", nc.gpsimd.indirect_dma_start(
                    out=ot_hbm, out_offset=IndirectOffsetOnAxis(
                        ap=idxc[:, st_:st_ + 1], axis=0),
                    in_=yt_sb[:], in_offset=None, compute_op=ALU.add))
                prev_yt = tsc
            ctx.wait("sync", ctx.tok("l1f_sc"))
            ctx.wait("tensor", ctx.tok("l1f_sc"))
            ctx.wait("vector", ctx.tok("l1f_sc"))
            ctx.wait("gpsimd", ctx.tok("l1f_sc"))
    ctx.fence()

    with contextlib.ExitStack() as ls:
        E2 = ls.enter_context
        ot_sb = E2(nc.sbuf_tensor("l1f_ots", [128, D], F32))
        oc_sb = E2(nc.sbuf_tensor("l1f_ocs", [128, 8, TCH], F32))
        ps_trf = E2(nc.psum_tensor("l1f_ptrf2", [128, 4, 128], F32))
        prev_st = None
        for j in range(ntc):
            for tt in range(TCH // 128):
                tg_ = j * (TCH // 128) + tt
                tld = ctx.dma("sync", "l1f_otl", ot_sb[:], otr[:, tg_, :])
                ctx.wait("tensor", tld)
                for half in range(2):
                    ins = None
                    for kb in range(4):
                        kbg = half * 4 + kb
                        ins = nc.tensor.transpose(
                            ps_trf[0:128, kb, :],
                            ot_sb[:, kbg * 128:(kbg + 1) * 128],
                            com.ident_f32[:])
                    tmm = ctx.inc(ins, "tensor")
                    ctx.wait("vector", tmm)
                    if prev_st is not None:
                        ctx.wait("vector", prev_st)
                        prev_st = None
                    vev = ctx.inc(nc.vector.tensor_copy(
                        oc_sb[:, half * 4:(half + 1) * 4, tt * 128:(tt + 1) * 128],
                        ps_trf[:]), "vector")
                    ctx.wait("tensor", vev)
                ctx.wait("sync", vev, tld)
            cj = slice(j * TCH, (j + 1) * TCH)
            tst = ctx.dma("sync", "l1f_ost", outr[:, :, cj], oc_sb[:])
            prev_st = tst
        ctx.wait("sync", ctx.tok("l1f_ost"))
    return


def emit_rms_norm(ctx, com, T, src_hbm, dst_hbm, g_col, eps=1e-6, mean_col=None,
                  tag="rms"):
    nc = ctx.nc
    ntc = T // TCH
    with contextlib.ExitStack() as ls:
        E = ls.enter_context
        x_sb = E(nc.sbuf_tensor(f"{tag}_x", [128, 8, TCH], F32))
        sq_sb = E(nc.sbuf_tensor(f"{tag}_sq", [128, 8, TCH], BF16))
        o_sb = E(nc.sbuf_tensor(f"{tag}_o", [128, 8, TCH], BF16))
        row = E(nc.sbuf_tensor(f"{tag}_row", [1, TCH], F32))
        mrow = E(nc.sbuf_tensor(f"{tag}_mr", [128, 8], F32))
        ps_s = E(nc.psum_tensor(f"{tag}_ps_s", [1, TCH], F32))
        ps_b = E(nc.psum_tensor(f"{tag}_ps_b", [128, TCH], F32))
        xr = src_hbm.rearrange("(kb p) t -> p kb t", p=128)
        dr = dst_hbm.rearrange("(kb p) t -> p kb t", p=128)
        if mean_col is not None:
            gz = ctx.inc(nc.gpsimd.memset(mean_col[:], 0.0), "gpsimd")
            ctx.wait("vector", gz)
        prev = None
        prev_st = None
        for j in range(ntc):
            cj = slice(j * TCH, (j + 1) * TCH)
            if prev is not None:
                ctx.wait("sync", prev)
                ctx.barrier()
            tld = ctx.dma("sync", f"{tag}_ld", x_sb[:], xr[:, :, cj])
            ctx.wait("scalar", tld)
            s1 = act(ctx, com, sq_sb[:], x_sb[:], AF.Square)
            ctx.wait("tensor", s1)
            ins = None
            for kb in range(8):
                ins = nc.tensor.matmul(ps_s[:], com.ones_col_bf[:], sq_sb[:, kb, :],
                                       start=(kb == 0), stop=(kb == 7))
            tmm = ctx.inc(ins, "tensor")
            ctx.wait("scalar", tmm)
            sr = sqrt_via_ln(ctx, com, row[:], ps_s[:], eps=eps, scale_in=1.0 / D)
            ctx.wait("vector", sr)
            v1 = ctx.inc(nc.vector.tensor_scalar(out=row[:], in0=row[:], scalar1=1e-6,
                                                 scalar2=1e6, op0=ALU.max,
                                                 op1=ALU.min), "vector")
            ctx.wait("vector", v1)
            v2 = ctx.inc(nc.vector.reciprocal(row[:], row[:]), "vector")
            ctx.wait("tensor", v2)
            b1 = ctx.inc(nc.tensor.matmul(ps_b[:], com.ones_row_f32[:], row[:],
                                          start=True, stop=True), "tensor")
            ctx.wait("vector", b1)
            if prev_st is not None:
                ctx.wait("vector", prev_st)
                prev_st = None
            vv = []
            for kb in range(8):
                va = ctx.inc(nc.vector.tensor_tensor(out=o_sb[:, kb, :],
                                                     in0=x_sb[:, kb, :], in1=ps_b[:],
                                                     op=ALU.mult), "vector")
                ctx.wait("vector", va)
                vb = ctx.inc(nc.vector.tensor_scalar(out=o_sb[:, kb, :],
                                                     in0=o_sb[:, kb, :],
                                                     scalar1=g_col[:, kb:kb + 1],
                                                     scalar2=None, op0=ALU.mult), "vector")
                vv.append(vb)
            ctx.wait("sync", *vv)
            tst = ctx.dma("sync", f"{tag}_st", dr[:, :, cj], o_sb[:])
            prev_st = tst
            if mean_col is not None:
                for kb in range(8):
                    ctx.wait("vector", vv[kb])
                    vm = ctx.inc(nc.vector.tensor_reduce(
                        row[0:1, 0:1] if False else mrow[:, kb:kb + 1].rearrange("p (g a) -> p g a", a=1),
                        o_sb[:, kb, :].rearrange("p (g f) -> p g f", f=TCH),
                        axis=AX.X, op=ALU.add), "vector")
                    ctx.wait("vector", vm)
                    ctx.inc(nc.vector.tensor_tensor(out=mean_col[:, kb:kb + 1],
                                                    in0=mean_col[:, kb:kb + 1],
                                                    in1=mrow[:, kb:kb + 1],
                                                    op=ALU.add), "vector")
            prev = ctx.tok("vector")
        ctx.wait("sync", ctx.tok(f"{tag}_st"))
    ctx.fence()
    return



def emit_l0_attn(ctx, com, T, xn_hbm_f32, w, out_hbm, n_cores=8,
                 split=None, ratio_hbm=None, x1_ext=None, h2_ext=None):
    nc = ctx.nc
    H = 16
    CH = 256
    F_ = 4608
    ntc = T // TCH
    ntt = T // 128
    nch = T // CH
    h1_hbm = nc.dram_tensor("l0a_h1", [D, T], BF16, kind="Internal").ap()
    qT_hbm = nc.dram_tensor("l0a_qT", [D, T], BF16, kind="Internal").ap()
    kT_hbm = nc.dram_tensor("l0a_kT", [D, T], BF16, kind="Internal").ap()
    kt_hbm = nc.dram_tensor("l0a_kt", [T, D], BF16, kind="Internal").ap()
    vt_hbm = nc.dram_tensor("l0a_vt", [T, D], BF16, kind="Internal").ap()
    ao_hbm = nc.dram_tensor("l0a_ao", [D, T], BF16, kind="Internal").ap()
    x1_hbm = x1_ext if x1_ext is not None else \
        nc.dram_tensor("l0a_x1", [D, T], F32, kind="Internal").ap()
    h2_hbm = h2_ext if h2_ext is not None else \
        nc.dram_tensor("l0a_h2", [D, T], BF16, kind="Internal").ap()
    if split is None:
        cc_in = nc.dram_tensor("l0a_cc_in", [1, 512], F32, kind="Internal").ap()
        cc_out = nc.dram_tensor("l0a_cc_out", [1, 512], F32, kind="Internal").ap()
    else:
        cc_in = cc_out = None
    xnr = xn_hbm_f32.rearrange("(kb p) t -> p kb t", p=128)
    outr = None if out_hbm is None else \
        out_hbm.rearrange("(kb p) t -> p kb t", p=128)
    ctx.fence()

    with contextlib.ExitStack() as ls:
        E = ls.enter_context
        g1c, tg1 = load_bias_col(ctx, ls, "l0a_g1", w["g1"], D)
        ctx.wait("vector", tg1, com.ready)
        ctx.wait("scalar", com.ready)
        ctx.wait("tensor", com.ready)
        emit_rms_norm(ctx, com, T, xn_hbm_f32, h1_hbm, g1c, eps=1e-6, tag="l0a_r1")
    ctx.fence()

    with contextlib.ExitStack() as ls:
        E = ls.enter_context
        wq_sb, twq = load_weight_ch(ctx, ls, "l0a_wqkv", w["w_qkv"], D, 3072)
        h1_sb = E(nc.sbuf_tensor("l0a_h1s", [128, 8, TCH], BF16))
        qkv_sb = E(nc.sbuf_tensor("l0a_qkv", [128, 3072], F32))
        qkb_sb = E(nc.sbuf_tensor("l0a_qkb", [128, 3072], BF16))
        tb_c = E(nc.sbuf_tensor("l0a_tbc", [128, 32], F32))
        tb_s = E(nc.sbuf_tensor("l0a_tbs", [128, 32], F32))
        posr = E(nc.sbuf_tensor("l0a_posr", [1, 128], F32))
        invf = E(nc.sbuf_tensor("l0a_invf", [1, 32], F32))
        tmp_sb = E(nc.sbuf_tensor("l0a_tmp", [128, 1024], F32))
        ps = [E(nc.psum_tensor(f"l0a_ps{i}", [128, TCH], F32)) for i in range(2)]
        ps_tr = E(nc.psum_tensor("l0a_ptr", [128, 8, 128], BF16))
        h1r = h1_hbm.rearrange("(kb p) t -> p kb t", p=128)
        gi = ctx.inc(nc.gpsimd.iota(invf[:], pattern=[[1, 32]], base=0,
                                    channel_multiplier=0,
                                    allow_small_or_imprecise_dtypes=True), "gpsimd")
        ctx.wait("scalar", gi, com.ready)
        si = act(ctx, com, invf[:], invf[:], AF.Exp, scale=-math.log(10000.0) / 32.0)
        ctx.wait("tensor", twq, si, com.ready)
        prev = None
        ev = [None, None]
        prev_bmm = None
        for tt in range(ntt):
            j = tt // 4
            if tt % 4 == 0:
                cj = slice(j * TCH, (j + 1) * TCH)
                if prev is not None:
                    ctx.wait("sync", prev)
                tld = ctx.dma("sync", "l0a_h1l", h1_sb[:], h1r[:, :, cj])
                ctx.wait("tensor", tld)
            lsl = slice((tt % 4) * 128, (tt % 4) * 128 + 128)
            for ni in range(6):
                pp = ps[ni % 2]
                if ev[ni % 2] is not None:
                    ctx.wait("tensor", ev[ni % 2])
                nsl = slice(ni * 512, (ni + 1) * 512)
                ins = mm_acc(nc, pp[:], [h1_sb[:, kc, lsl] for kc in range(8)],
                             [wq_sb[:, kc, nsl] for kc in range(8)])
                tmm = ctx.inc(ins, "tensor")
                ctx.wait("vector", tmm)
                ev[ni % 2] = ctx.inc(nc.vector.tensor_copy(qkv_sb[:, nsl], pp[:]),
                                     "vector")
            if prev_bmm is not None:
                ctx.wait("gpsimd", prev_bmm)
            gp = ctx.inc(nc.gpsimd.iota(posr[:], pattern=[[1, 128]], base=tt * 128,
                                        channel_multiplier=0,
                                        allow_small_or_imprecise_dtypes=True), "gpsimd")
            ctx.wait("tensor", gp, ev[0], ev[1])
            bmm = ctx.inc(nc.tensor.matmul(ps[0][:, 0:32],
                                           posr[:], invf[:], start=True, stop=True,
                                           skip_group_check=True), "tensor")
            prev_bmm = bmm
            ctx.wait("vector", bmm)
            for half, tb in ((0, tb_c), (1, tb_s)):
                sh = math.pi / 2 if half == 0 else 0.0
                v1 = ctx.inc(nc.vector.tensor_scalar(
                    out=tb[:], in0=ps[0][:, 0:32], scalar1=float(sh),
                    scalar2=None, op0=ALU.add), "vector")
                ctx.wait("vector", v1)
                v2 = ctx.inc(nc.vector.tensor_scalar(
                    out=tmp_sb[:, 0:32], in0=tb[:],
                    scalar1=float(1.0 / (2 * math.pi)),
                    scalar2=float(2 ** 23), op0=ALU.mult, op1=ALU.add), "vector")
                ctx.wait("vector", v2)
                v2 = ctx.inc(nc.vector.tensor_scalar(
                    out=tmp_sb[:, 0:32], in0=tmp_sb[:, 0:32],
                    scalar1=float(2 ** 23), scalar2=None,
                    op0=ALU.subtract), "vector")
                ctx.wait("vector", v2)
                v3 = ctx.inc(nc.vector.scalar_tensor_tensor(
                    out=tb[:], in0=tmp_sb[:, 0:32],
                    scalar=float(-2 * math.pi), in1=tb[:],
                    op0=ALU.mult, op1=ALU.add), "vector")
                ctx.wait("scalar", v3)
                s1 = act(ctx, com, tb[:], tb[:], AF.Sin)
                ctx.wait("vector", s1)
            ctx.wait("vector", ev[0], ev[1])
            from concourse.bass import broadcast_tensor_aps as _bta
            for qi in range(2):
                base = qi * 1024
                q4 = qkv_sb[:, base:base + 1024].rearrange("p (h i two) -> p h i two",
                                                           h=16, two=2)
                t4 = tmp_sb[:, 0:1024].rearrange("p (h i two) -> p h i two", h=16, two=2)
                c3 = tb_c.ap().rearrange("p (a i b) -> p a i b", a=1, b=1)
                s3 = tb_s.ap().rearrange("p (a i b) -> p a i b", a=1, b=1)
                qe = q4[:, :, :, 0:1]
                qo = q4[:, :, :, 1:2]
                a1, a2 = _bta(qe, c3)
                v1 = ctx.inc(nc.vector.tensor_tensor(out=t4[:, :, :, 0:1], in0=a1,
                                                     in1=a2, op=ALU.mult), "vector")
                a1, a2 = _bta(qo, s3)
                v2 = ctx.inc(nc.vector.tensor_tensor(out=t4[:, :, :, 1:2], in0=a1,
                                                     in1=a2, op=ALU.mult), "vector")
                ctx.wait("vector", v1, v2)
                v3 = ctx.inc(nc.vector.tensor_tensor(out=t4[:, :, :, 0:1],
                                                     in0=t4[:, :, :, 0:1],
                                                     in1=t4[:, :, :, 1:2],
                                                     op=ALU.subtract), "vector")
                ctx.wait("vector", v3)
                a1, a2 = _bta(qo, c3)
                v4 = ctx.inc(nc.vector.tensor_tensor(out=t4[:, :, :, 1:2], in0=a1,
                                                     in1=a2, op=ALU.mult), "vector")
                a1, a2 = _bta(qe, s3)
                v5 = ctx.inc(nc.vector.tensor_tensor(out=qkv_sb[:, base:base + 1024]
                                                     .rearrange("p (h i two) -> p h i two", h=16, two=2)[:, :, :, 0:1]
                                                     if False else q4[:, :, :, 0:1],
                                                     in0=a1, in1=a2, op=ALU.mult), "vector")
                ctx.wait("vector", v3, v4, v5)
                v6 = ctx.inc(nc.vector.tensor_tensor(out=t4[:, :, :, 1:2],
                                                     in0=t4[:, :, :, 1:2],
                                                     in1=q4[:, :, :, 0:1],
                                                     op=ALU.add), "vector")
                ctx.wait("vector", v6)
                v7 = ctx.inc(nc.vector.tensor_copy(qkv_sb[:, base:base + 1024],
                                                   tmp_sb[:, 0:1024]), "vector")
                ctx.wait("vector", v7)
            v1 = ctx.inc(nc.vector.tensor_scalar(out=tmp_sb[:, 0:1024],
                                                 in0=qkv_sb[:, 0:1024], scalar1=0.0,
                                                 scalar2=None, op0=ALU.min), "vector")
            v1b = ctx.inc(nc.vector.tensor_scalar(out=qkv_sb[:, 0:1024],
                                                  in0=qkv_sb[:, 0:1024], scalar1=0.0,
                                                  scalar2=None, op0=ALU.max), "vector")
            ctx.wait("scalar", v1)
            s1 = act(ctx, com, tmp_sb[:, 0:1024], tmp_sb[:, 0:1024], AF.Exp)
            ctx.wait("vector", s1, v1b)
            v2 = ctx.inc(nc.vector.tensor_tensor(out=qkv_sb[:, 0:1024],
                                                 in0=qkv_sb[:, 0:1024],
                                                 in1=tmp_sb[:, 0:1024], op=ALU.add), "vector")
            ctx.wait("vector", v2)
            v3 = ctx.inc(nc.vector.tensor_scalar(out=tmp_sb[:, 0:1024],
                                                 in0=qkv_sb[:, 1024:2048], scalar1=0.0,
                                                 scalar2=None, op0=ALU.min), "vector")
            v3b = ctx.inc(nc.vector.tensor_scalar(out=qkv_sb[:, 1024:2048],
                                                  in0=qkv_sb[:, 1024:2048], scalar1=0.0,
                                                  scalar2=None, op0=ALU.max), "vector")
            ctx.wait("scalar", v3)
            s2 = act(ctx, com, tmp_sb[:, 0:1024], tmp_sb[:, 0:1024], AF.Exp)
            ctx.wait("vector", s2, v3b)
            v4 = ctx.inc(nc.vector.tensor_tensor(out=qkv_sb[:, 1024:2048],
                                                 in0=qkv_sb[:, 1024:2048],
                                                 in1=tmp_sb[:, 0:1024], op=ALU.add), "vector")
            ctx.wait("vector", v2, v4, ctx.tok("l0a_qkst"),
                     ctx.tok("l0a_kts"), ctx.tok("l0a_vts"))
            vb = ctx.inc(nc.vector.tensor_copy(qkb_sb[:], qkv_sb[:]), "vector")
            ctx.wait("sync", vb)
            ctx.dma("sync", "l0a_kts", kt_hbm[tt * 128:(tt + 1) * 128, :],
                    qkb_sb[:, 1024:2048])
            ctx.dma("sync", "l0a_vts", vt_hbm[tt * 128:(tt + 1) * 128, :],
                    qkb_sb[:, 2048:3072])
            ctx.wait("tensor", vb)
            for qi, dst in ((0, qT_hbm), (1, kT_hbm)):
                tok = emit_transpose_to(ctx, com, qkb_sb[:, qi * 1024:(qi + 1) * 1024],
                                        ps_tr.ap()[:, :, :] if False else
                                        None, ps_tr) if False else None
                ins = None
                for kb in range(8):
                    ins = nc.tensor.transpose(
                        ps_tr[0:128, kb, :],
                        qkb_sb[:, qi * 1024 + kb * 128:qi * 1024 + (kb + 1) * 128],
                        com.ident_bf[:])
                ttr = ctx.inc(ins, "tensor")
                ctx.wait("vector", ttr)
                if qi == 1:
                    ctx.wait("vector", ctx.tok("l0a_kts"), ctx.tok("l0a_vts"))
                vtr = ctx.inc(nc.vector.tensor_copy(
                    qkb_sb[:, qi * 1024:(qi + 1) * 1024],
                    ps_tr.ap().rearrange("p a b -> p (a b)")), "vector")
                ctx.wait("sync", vtr)
                ctx.dma("sync", "l0a_qkst",
                        dst.rearrange("(kb p) t -> p kb t", p=128)[:, :, tt * 128:(tt + 1) * 128],
                        qkb_sb[:, qi * 1024:(qi + 1) * 1024].rearrange("p (kb f) -> p kb f", f=128))
                ctx.wait("tensor", vtr)
            ev = [ctx.tok("vector"), ctx.tok("vector")]
            prev = ctx.tok("vector")
        ctx.wait("sync", ctx.tok("l0a_kts"), ctx.tok("l0a_vts"), ctx.tok("l0a_qkst"))
    ctx.fence()
    hbm = {"xn": xn_hbm_f32, "qT": qT_hbm, "kT": kT_hbm, "kt": kt_hbm,
           "vt": vt_hbm, "ao": ao_hbm, "x1": x1_hbm, "h2": h2_hbm,
           "cc_in": cc_in, "cc_out": cc_out}
    emit_l0_attn_tail(ctx, com, T, w, hbm, out_hbm, n_cores,
                      [list(range(n_cores))], split=split, ratio_hbm=ratio_hbm)
    return


def emit_l0_attn_tail(ctx, com, T, w, hbm, out_hbm, n_cores, replica_groups,
                      split=None, ratio_hbm=None):
    nc = ctx.nc
    H = 16
    CH = 256
    F_ = 4608
    ntc = T // TCH
    nch = T // CH
    xnr = hbm["xn"].rearrange("(kb p) t -> p kb t", p=128)
    outr = None if out_hbm is None else \
        out_hbm.rearrange("(kb p) t -> p kb t", p=128)

    with contextlib.ExitStack() as ls:
        E = ls.enter_context
        qh = E(nc.sbuf_tensor("l0a_qh", [64, T], BF16))
        kh = E(nc.sbuf_tensor("l0a_kh", [64, T], BF16))
        ktok = E(nc.sbuf_tensor("l0a_ktk", [128, T // 128, 64], BF16))
        vaug = E(nc.sbuf_tensor("l0a_vag", [128, T // 128, 65], BF16))
        saug = E(nc.sbuf_tensor("l0a_sag", [64, 65], BF16))
        asb = E(nc.sbuf_tensor("l0a_asb", [128, 512], BF16))
        drow = E(nc.sbuf_tensor("l0a_drow", [1, CH], F32))
        bcs = E(nc.sbuf_tensor("l0a_bcs", [64, CH], F32))
        ao_sb = E(nc.sbuf_tensor("l0a_aos", [64, T], BF16))
        ps_at = E(nc.psum_tensor("l0a_pat", [128, 512], F32))
        ps_n = E(nc.psum_tensor("l0a_pn", [128, 512], F32))
        ps_s = E(nc.psum_tensor("l0a_pss", [128, 512], F32))
        ps_b = E(nc.psum_tensor("l0a_pb", [128, 512], F32))
        cmsk = E(nc.sbuf_tensor("l0a_cmsk", [128, 2, 256], BF16))
        gv = ctx.inc(nc.gpsimd.memset(vaug[:, :, 64:65], 1.0), "gpsimd")
        gm = ctx.inc(nc.gpsimd.memset(cmsk[:], 0.0), "gpsimd")
        ctx.wait("gpsimd", gm)
        for jt in range(2):
            gm = ctx.inc(nc.gpsimd.affine_select(
                out=cmsk[:, jt, :], in_=cmsk[:, jt, :], compare_op=ALU.is_ge,
                fill=1.0, base=jt * 128 - 1, pattern=[[-1, 256]],
                channel_multiplier=1), "gpsimd")
        ctx.wait("sync", gv, gm, com.ready)
        ctx.wait("vector", gm)
        for h in range(H):
            ctx.wait("sync", ctx.tok("tensor"))
            ctx.wait("sync", ctx.tok("vector"))
            tq = ctx.dma("sync", "l0a_qhl", qh[:], hbm["qT"][h * 64:(h + 1) * 64, :])
            tk = ctx.dma("sync", "l0a_khl", kh[:], hbm["kT"][h * 64:(h + 1) * 64, :])
            tkt = ctx.dma("sync", "l0a_ktl", ktok[:],
                          hbm["kt"].rearrange("(a p) d -> p a d", p=128)[:, :, h * 64:(h + 1) * 64])
            tvt = ctx.dma("sync", "l0a_vtl", vaug[:, :, 0:64],
                          hbm["vt"].rearrange("(a p) d -> p a d", p=128)[:, :, h * 64:(h + 1) * 64])
            ctx.wait("vector", tq, tk, tkt, tvt)
            ctx.wait("tensor", tq, tk, tkt, tvt)
            gs = ctx.inc(nc.vector.memset(saug[:], 0.0), "vector")
            ctx.wait("tensor", gs)
            t_supd = None
            t_numev = None
            for c in range(nch):
                csl = slice(c * CH, (c + 1) * CH)
                if c > 0:
                    ctx.wait("vector", t_supd)
                    vs = ctx.inc(nc.vector.tensor_copy(saug[:], ps_s[0:64, 0:65]), "vector")
                    ctx.wait("tensor", vs)
                ins = None
                for jt in range(2):
                    ins = nc.tensor.matmul(ps_s[0:64, 0:65], ktok[:, 2 * c + jt, :],
                                           vaug[:, 2 * c + jt, :],
                                           start=(c == 0 and jt == 0), stop=False,
                                           skip_group_check=True)
                t_supd = ctx.inc(ins, "tensor")
                if t_numev is not None:
                    ctx.wait("tensor", t_numev)
                for jt in range(2):
                    ins = nc.tensor.matmul(ps_at[:, jt * 256:(jt + 1) * 256],
                                           kh[:, c * CH + jt * 128:c * CH + (jt + 1) * 128],
                                           qh[:, csl], start=True, stop=True,
                                           skip_group_check=True)
                tat = ctx.inc(ins, "tensor")
                ctx.wait("vector", tat)
                va = ctx.inc(nc.vector.tensor_tensor(
                    out=asb[:, 0:256], in0=ps_at[:, 0:256], in1=cmsk[:, 0, :],
                    op=ALU.mult), "vector")
                vb = ctx.inc(nc.vector.tensor_tensor(
                    out=asb[:, 256:512], in0=ps_at[:, 256:512], in1=cmsk[:, 1, :],
                    op=ALU.mult), "vector")
                ctx.wait("tensor", va, vb)
                for jt in range(2):
                    nc.tensor.matmul(ps_n[0:65, 0:CH], vaug[:, 2 * c + jt, :],
                                     asb[:, jt * 256:(jt + 1) * 256],
                                     start=(jt == 0), stop=False,
                                     skip_group_check=True)
                ins = nc.tensor.matmul(ps_n[0:65, 0:CH], saug[:], qh[:, csl],
                                       start=False, stop=True, skip_group_check=True)
                tnum = ctx.inc(ins, "tensor")
                ctx.wait("vector", tnum)
                v1 = ctx.inc(nc.vector.tensor_scalar(out=drow[:], in0=ps_n[64:65, 0:CH],
                                                     scalar1=1e-6, scalar2=None,
                                                     op0=ALU.max), "vector")
                ctx.wait("vector", v1)
                v2 = ctx.inc(nc.vector.reciprocal(drow[:], drow[:]), "vector")
                ctx.wait("tensor", v2)
                tb = ctx.inc(nc.tensor.matmul(ps_b[0:64, 0:CH], com.ones_row_f32[0:1, 0:64],
                                              drow[:], start=True, stop=True,
                                              skip_group_check=True), "tensor")
                ctx.wait("vector", tb)
                v3 = ctx.inc(nc.vector.tensor_copy(bcs[:], ps_b[0:64, 0:CH]), "vector")
                ctx.wait("vector", v3, ctx.tok("l0a_aost"))
                v4 = ctx.inc(nc.vector.tensor_tensor(out=ao_sb[:, csl], in0=ps_n[0:64, 0:CH],
                                                     in1=bcs[:], op=ALU.mult), "vector")
                t_numev = v4
            ctx.wait("sync", t_numev)
            ctx.dma("sync", "l0a_aost", hbm["ao"][h * 64:(h + 1) * 64, :], ao_sb[:])
        ctx.wait("sync", ctx.tok("l0a_aost"))
    ctx.fence()

    with contextlib.ExitStack() as ls:
        E = ls.enter_context
        wo_sb, two = load_weight_ch(ctx, ls, "l0a_wo", w["w_out"], D, D)
        ao_sb = E(nc.sbuf_tensor("l0a_aod", [128, 8, TCH], BF16))
        xn_sb = E(nc.sbuf_tensor("l0a_xnd", [128, 8, TCH], F32))
        fo_sb = E(nc.sbuf_tensor("l0a_fod", [128, 8, TCH], F32))
        ps = [E(nc.psum_tensor(f"l0a_psd{i}", [128, TCH], F32)) for i in range(2)]
        aor = hbm["ao"].rearrange("(kb p) t -> p kb t", p=128)
        x1r = hbm["x1"].rearrange("(kb p) t -> p kb t", p=128)
        ctx.wait("tensor", two, com.ready)
        prev = None
        ev = [None, None]
        for j in range(ntc):
            cj = slice(j * TCH, (j + 1) * TCH)
            if prev is not None:
                ctx.wait("sync", prev)
            tao = ctx.dma("sync", "l0a_aodl", ao_sb[:], aor[:, :, cj])
            txn = ctx.dma("sync", "l0a_xndl", xn_sb[:], xnr[:, :, cj])
            ctx.wait("tensor", tao)
            for mi in range(8):
                pp = ps[mi % 2]
                if ev[mi % 2] is not None:
                    ctx.wait("tensor", ev[mi % 2])
                msl = slice(mi * 128, (mi + 1) * 128)
                ins = mm_acc(nc, pp[:], [wo_sb[:, kc, msl] for kc in range(8)],
                             [ao_sb[:, kc, :] for kc in range(8)])
                tmm = ctx.inc(ins, "tensor")
                ctx.wait("vector", tmm, txn)
                ev[mi % 2] = ctx.inc(nc.vector.tensor_tensor(
                    out=fo_sb[:, mi, :], in0=xn_sb[:, mi, :], in1=pp[:], op=ALU.add),
                    "vector")
            ctx.wait("sync", ev[0], ev[1])
            tst = ctx.dma("sync", "l0a_x1st", x1r[:, :, cj], fo_sb[:])
            prev = tst
        ctx.wait("sync", ctx.tok("l0a_x1st"))
    ctx.fence()

    with contextlib.ExitStack() as ls:
        E = ls.enter_context
        g2c, tg2 = load_bias_col(ctx, ls, "l0a_g2", w["g2"], D)
        mcol = E(nc.sbuf_tensor("l0a_mcol", [128, 8], F32))
        ctx.wait("vector", tg2)
        emit_rms_norm(ctx, com, T, hbm["x1"], hbm["h2"], g2c, eps=1e-6,
                      mean_col=mcol, tag="l0a_r2")
        wdp1_sb, t1 = load_weight_ch(ctx, ls, "l0a_wdp1", w["w_dp1"], D, 256)
        wdp2f, t2 = load_bias_col(ctx, ls, "l0a_wdp2f", w["w_dp2"].rearrange("a b -> (a b)"), 256)
        wdp2_sb = ls.enter_context(nc.sbuf_tensor("l0a_wdp2", [128, 2], BF16))
        ctx.wait("vector", t2)
        t2 = ctx.inc(nc.vector.tensor_copy(wdp2_sb[:], wdp2f[:]), "vector")
        mbf = E(nc.sbuf_tensor("l0a_mbf", [128, 8], BF16))
        zrow = E(nc.sbuf_tensor("l0a_zrow", [1, 256], F32))
        zcol = E(nc.sbuf_tensor("l0a_zcol", [128, 2], BF16))
        cc_sb = E(nc.sbuf_tensor("l0a_ccs", [1, 512], F32))
        ps0 = E(nc.psum_tensor("l0a_psf", [128, 512], F32))
        v1 = ctx.inc(nc.vector.tensor_scalar(out=mbf[:], in0=mcol[:],
                                             scalar1=1.0 / T, scalar2=None,
                                             op0=ALU.mult), "vector")
        ctx.wait("tensor", v1, t1, t2)
        ins = None
        for kc in range(8):
            ins = nc.tensor.matmul(ps0[0:1, 0:256], mbf[:, kc:kc + 1],
                                   wdp1_sb[:, kc, :], start=(kc == 0), stop=(kc == 7))
        tmm = ctx.inc(ins, "tensor")
        ctx.wait("scalar", tmm)
        s1 = act(ctx, com, zrow[:], ps0[0:1, 0:256], AF.Silu)
        ctx.wait("tensor", s1)
        ins = None
        for i in range(2):
            ins = nc.tensor.matmul(ps0[0:128, 256 + i:257 + i],
                                   zrow[0:1, i * 128:(i + 1) * 128],
                                   com.ones_row_f32[0:1, 0:1], start=True, stop=True,
                                   skip_group_check=True)
        tmm = ctx.inc(ins, "tensor")
        ctx.wait("vector", tmm)
        v2 = ctx.inc(nc.vector.tensor_copy(zcol[:], ps0[0:128, 256:258]), "vector")
        ctx.wait("tensor", v2)
        ins = None
        for i in range(2):
            ins = nc.tensor.matmul(ps0[0:1, 260:261], zcol[:, i:i + 1],
                                   wdp2_sb[:, i:i + 1], start=(i == 0), stop=(i == 1))
        tmm = ctx.inc(ins, "tensor")
        ctx.wait("scalar", tmm)
        s2 = act(ctx, com, cc_sb[0:1, 0:1], ps0[0:1, 260:261], AF.Sigmoid)
        ctx.wait("vector", s2)
        v3 = ctx.inc(nc.vector.tensor_scalar(out=cc_sb[0:1, 0:1], in0=cc_sb[0:1, 0:1],
                                             scalar1=0.5, scalar2=0.5,
                                             op0=ALU.add, op1=ALU.max), "vector")
        ctx.wait("vector", v3)
        v4 = ctx.inc(nc.vector.tensor_scalar(out=cc_sb[0:1, 0:1], in0=cc_sb[0:1, 0:1],
                                             scalar1=1.5, scalar2=None, op0=ALU.min), "vector")
        ctx.wait("vector", v4)
        if split == "pre":
            ctx.wait("sync", v4)
            ctx.dma("sync", "l0a_ratst", ratio_hbm, cc_sb[0:1, 0:1])
            ls.close()
            ctx.fence()
            return
        v5 = ctx.inc(nc.vector.memset(cc_sb[0:1, 1:512], 0.0), "vector")
        ctx.wait("sync", v4, v5)
        tst = ctx.dma("sync", "l0a_ccst", hbm["cc_in"], cc_sb[:])
        ctx.wait("gpsimd", tst)
        tcc = ctx.idma("l0a_cc", nc.gpsimd.collective_compute(
            "AllReduce", ALU.add, replica_groups,
            [hbm["cc_in"]], [hbm["cc_out"]]))
        ctx.wait("sync", tcc)
        tld = ctx.dma("sync", "l0a_ccld", cc_sb[:], hbm["cc_out"])
        ctx.wait("vector", tld)
        v6 = ctx.inc(nc.vector.tensor_scalar(out=cc_sb[0:1, 0:1], in0=cc_sb[0:1, 0:1],
                                             scalar1=3072.0 / n_cores, scalar2=-1.0,
                                             op0=ALU.mult, op1=ALU.add), "vector")
        ctx.wait("tensor", v6)
        _emit_l0_attn_gh(ctx, com, ls, T, w, hbm["x1"], hbm["h2"], out_hbm,
                         cc_sb, ps0, v6)
    ctx.fence()
    return


def _emit_l0_attn_gh(ctx, com, ls, T, w, x1_hbm, h2_hbm, out_hbm, cc_sb,
                     ps0, vtok):
    nc = ctx.nc
    F_ = 4608
    ntc = T // TCH
    E = ls.enter_context
    outr = out_hbm.rearrange("(kb p) t -> p kb t", p=128)
    if True:
        ctx.wait("tensor", vtok)
        tb = ctx.inc(nc.tensor.matmul(ps0[0:128, 264:265], com.ones_row_f32[:],
                                      cc_sb[0:1, 0:1], start=True, stop=True,
                                      skip_group_check=True), "tensor")
        ctx.wait("vector", tb)
        ym1c = E(nc.sbuf_tensor("l0a_ym1", [128, 1], F32))
        v7 = ctx.inc(nc.vector.tensor_copy(ym1c[:], ps0[0:128, 264:265]), "vector")
        iotc = E(nc.sbuf_tensor("l0a_iotc", [128, 36], F32))
        maskc = E(nc.sbuf_tensor("l0a_maskc", [128, 36], F32))
        maskb = E(nc.sbuf_tensor("l0a_maskb", [128, 36], BF16))
        ghc_raw, tgh = load_bias_col(ctx, ls, "l0a_ghc", w["g_hidden"], F_)
        szr = E(nc.sbuf_tensor("l0a_szr", [1, 36], F32))
        gio = ctx.inc(nc.gpsimd.iota(iotc[:], pattern=[[128, 36]], base=0,
                                     channel_multiplier=1,
                                     allow_small_or_imprecise_dtypes=True), "gpsimd")
        ctx.wait("vector", gio, v7)
        v8 = ctx.inc(nc.vector.tensor_scalar(out=maskc[:], in0=iotc[:], scalar1=ym1c[:],
                                             scalar2=None, op0=ALU.is_le), "vector")
        ctx.wait("vector", v8, tgh)
        v9 = ctx.inc(nc.vector.tensor_tensor(out=ghc_raw[:, 0:36], in0=ghc_raw[:, 0:36],
                                             in1=maskc[:], op=ALU.mult), "vector")
        v8b = ctx.inc(nc.vector.tensor_copy(maskb[:], maskc[:]), "vector")
        ctx.wait("tensor", v8b)
        ins = nc.tensor.matmul(ps0[0:1, 266:302], com.ones_col_bf[:], maskb[:],
                               start=True, stop=True, skip_group_check=True)
        tmm = ctx.inc(ins, "tensor")
        ctx.wait("vector", tmm)
        v10 = ctx.inc(nc.vector.tensor_reduce(
            szr[0:1, 0:1].rearrange("p (g a) -> p g a", a=1),
            ps0[0:1, 266:302].rearrange("p (g f) -> p g f", f=36), axis=AX.X,
            op=ALU.add), "vector")
        ctx.wait("vector", v10)
        v11 = ctx.inc(nc.vector.tensor_scalar(out=szr[0:1, 0:1], in0=szr[0:1, 0:1],
                                              scalar1=1.0, scalar2=None,
                                              op0=ALU.max), "vector")
        ctx.wait("vector", v11)
        v12 = ctx.inc(nc.vector.reciprocal(szr[0:1, 1:2], szr[0:1, 0:1]), "vector")
        h2r = h2_hbm.rearrange("(kb p) t -> p kb t", p=128)
        x1r = x1_hbm.rearrange("(kb p) t -> p kb t", p=128)
        h2_sb = E(nc.sbuf_tensor("l0a_h2s", [128, 8, TCH], BF16))
        x1_sb = E(nc.sbuf_tensor("l0a_x1s", [128, 8, TCH], F32))
        wsl_sb = E(nc.sbuf_tensor("l0a_wsl", [128, 8, 1152], BF16))
        wdm_sb = [E(nc.sbuf_tensor(f"l0a_wdm{i}", [128, 36, 128], BF16)) for i in range(2)]
        hid = E(nc.sbuf_tensor("l0a_hid", [128, 36, TCH], BF16))
        sqv = E(nc.sbuf_tensor("l0a_sqv", [128, TCH], BF16))
        srow = E(nc.sbuf_tensor("l0a_srow", [1, TCH], F32))
        bcr = E(nc.sbuf_tensor("l0a_bcr", [128, TCH], F32))
        sg = E(nc.sbuf_tensor("l0a_sg", [128, TCH], F32))
        fo_sb = E(nc.sbuf_tensor("l0a_fo2", [128, 8, TCH], F32))
        ps = [E(nc.psum_tensor(f"l0a_psh{i}", [128, TCH], F32)) for i in range(2)]
        ps_r = E(nc.psum_tensor("l0a_psr", [1, TCH], F32))
        ctx.wait("vector", v9, v12)
        ev = [None, None]
        prev = None
        prev_st = None
        for j in range(ntc):
            cj = slice(j * TCH, (j + 1) * TCH)
            if prev is not None:
                ctx.wait("sync", prev)
                ctx.barrier()
            th2 = ctx.dma("sync", "l0a_h2l", h2_sb[:], h2r[:, :, cj])
            tx1 = ctx.dma("sync", "l0a_x1l", x1_sb[:], x1r[:, :, cj])
            for phase in range(2):
                wsrc = w["w_gate"] if phase == 0 else w["w_up"]
                for sl in range(4):
                    ctx.wait("gpsimd", ctx.tok("tensor"))
                    csl = slice(sl * 1152, (sl + 1) * 1152)
                    tw = ctx.dma("gpsimd", "l0a_wsl",
                                 wsl_sb[:], wsrc[:, csl].rearrange("(kc p) m -> p kc m", p=128))
                    ctx.wait("tensor", tw, th2)
                    for mi in range(9):
                        gmi = sl * 9 + mi
                        pp = ps[gmi % 2]
                        if ev[gmi % 2] is not None:
                            ctx.wait("tensor", ev[gmi % 2])
                        msl = slice(mi * 128, (mi + 1) * 128)
                        ins = mm_acc(nc, pp[:], [wsl_sb[:, kc, msl] for kc in range(8)],
                                     [h2_sb[:, kc, :] for kc in range(8)])
                        tmm = ctx.inc(ins, "tensor")
                        if phase == 0:
                            ctx.wait("scalar", tmm)
                            if ev[gmi % 2] is not None:
                                ctx.wait("scalar", ev[gmi % 2])
                            ev[gmi % 2] = act(ctx, com, hid[:, gmi, :], pp[:], AF.Silu)
                        else:
                            ctx.wait("vector", tmm)
                            v1 = ctx.inc(nc.vector.tensor_tensor(
                                out=hid[:, gmi, :], in0=hid[:, gmi, :], in1=pp[:],
                                op=ALU.mult), "vector")
                            ctx.wait("vector", v1)
                            ev[gmi % 2] = ctx.inc(nc.vector.tensor_scalar(
                                out=hid[:, gmi, :], in0=hid[:, gmi, :],
                                scalar1=maskc[:, gmi:gmi + 1], scalar2=None,
                                op0=ALU.mult), "vector")
            ctx.wait("scalar", ev[0], ev[1])
            last_sq = None
            for gmi in range(36):
                sq = act(ctx, com, sqv[:], hid[:, gmi, :], AF.Square)
                ctx.wait("tensor", sq)
                ins = nc.tensor.matmul(ps_r[:], com.ones_col_bf[:], sqv[:],
                                       start=(gmi == 0), stop=(gmi == 35),
                                       skip_group_check=True)
                tmm = ctx.inc(ins, "tensor")
                ctx.wait("scalar", tmm)
                last_sq = tmm
            ctx.wait("vector", last_sq)
            v1 = ctx.inc(nc.vector.tensor_scalar(out=srow[:], in0=ps_r[:],
                                                 scalar1=szr[0:1, 1:2], scalar2=None,
                                                 op0=ALU.mult), "vector")
            ctx.wait("scalar", v1)
            s1 = sqrt_via_ln(ctx, com, srow[:], srow[:], eps=1e-6)
            ctx.wait("vector", s1)
            v2 = ctx.inc(nc.vector.tensor_scalar(out=srow[:], in0=srow[:], scalar1=1e-6,
                                                 scalar2=1e6, op0=ALU.max, op1=ALU.min),
                         "vector")
            ctx.wait("vector", v2)
            v3 = ctx.inc(nc.vector.reciprocal(srow[:], srow[:]), "vector")
            ctx.wait("tensor", v3)
            tb = ctx.inc(nc.tensor.matmul(ps[0][:], com.ones_row_f32[:], srow[:],
                                          start=True, stop=True, skip_group_check=True),
                         "tensor")
            ctx.wait("vector", tb)
            v4 = ctx.inc(nc.vector.tensor_copy(bcr[:], ps[0][:]), "vector")
            vlast = v4
            for gmi in range(36):
                ctx.wait("vector", vlast)
                va = ctx.inc(nc.vector.tensor_tensor(out=hid[:, gmi, :],
                                                     in0=hid[:, gmi, :], in1=bcr[:],
                                                     op=ALU.mult), "vector")
                ctx.wait("vector", va)
                vlast = ctx.inc(nc.vector.tensor_scalar(
                    out=hid[:, gmi, :], in0=hid[:, gmi, :],
                    scalar1=ghc_raw[:, gmi:gmi + 1], scalar2=None, op0=ALU.mult), "vector")
            ctx.wait("tensor", vlast)
            tdw = [None, None]
            ev2 = [None, None]
            for mi in range(8):
                if tdw[mi % 2] is not None:
                    ctx.wait("gpsimd", tdw[mi % 2][1])
                tw = ctx.dma("gpsimd", f"l0a_wdl{mi % 2}", wdm_sb[mi % 2][:],
                             w["w_down"][:, mi * 128:(mi + 1) * 128]
                             .rearrange("(kc p) m -> p kc m", p=128))
                ctx.wait("tensor", tw)
                pp = ps[mi % 2]
                if ev2[mi % 2] is not None:
                    ctx.wait("tensor", ev2[mi % 2])
                if mi < 2:
                    ctx.wait("tensor", v4)
                ins = mm_acc(nc, pp[:], [wdm_sb[mi % 2][:, kc, :] for kc in range(36)],
                             [hid[:, kc, :] for kc in range(36)])
                tmm = ctx.inc(ins, "tensor")
                tdw[mi % 2] = (tw, ctx.tok("tensor"))
                ctx.wait("vector", tmm, tx1)
                if prev_st is not None:
                    ctx.wait("vector", prev_st)
                    prev_st = None
                ev2[mi % 2] = ctx.inc(nc.vector.tensor_tensor(
                    out=fo_sb[:, mi, :], in0=x1_sb[:, mi, :], in1=pp[:], op=ALU.add),
                    "vector")
            ctx.wait("sync", ev2[0], ev2[1])
            tst = ctx.dma("sync", "l0a_ost", outr[:, :, cj], fo_sb[:])
            prev_st = tst
            prev = ctx.tok("vector")
            ev = [ctx.tok("vector"), ctx.tok("vector")]
        ctx.wait("sync", ctx.tok("l0a_ost"))
    return


def emit_l0_attn_post(ctx, com, T, w, x1_hbm, h2_hbm, ratios_hbm, out_hbm,
                      n_cores):
    nc = ctx.nc
    with contextlib.ExitStack() as ls:
        E = ls.enter_context
        cc_sb = E(nc.sbuf_tensor("l0a_ccs", [1, 512], F32))
        ps0 = E(nc.psum_tensor("l0a_psf", [128, 512], F32))
        trt = ctx.dma("sync", "l0a_rld", cc_sb[0:1, 0:n_cores],
                      ratios_hbm.rearrange("a b -> b a"))
        ctx.wait("vector", trt)
        vs = ctx.inc(nc.vector.tensor_reduce(
            cc_sb[0:1, 256:257].rearrange("p (g a) -> p g a", a=1),
            cc_sb[0:1, 0:n_cores].rearrange("p (g f) -> p g f", f=n_cores),
            axis=AX.X, op=ALU.add), "vector")
        ctx.wait("vector", vs)
        v6 = ctx.inc(nc.vector.tensor_scalar(out=cc_sb[0:1, 0:1],
                                             in0=cc_sb[0:1, 256:257],
                                             scalar1=3072.0 / n_cores,
                                             scalar2=-1.0,
                                             op0=ALU.mult, op1=ALU.add), "vector")
        ctx.wait("tensor", com.ready)
        ctx.wait("scalar", com.ready)
        ctx.wait("gpsimd", com.ready)
        _emit_l0_attn_gh(ctx, com, ls, T, w, x1_hbm, h2_hbm, out_hbm,
                         cc_sb, ps0, v6)
    ctx.fence()
    return



def emit_l0_ffn(ctx, com, T, xn_hbm_f32, w, out_hbm):
    nc = ctx.nc
    HEADS = 8
    DH = 64
    W_ = 512
    ntc = T // TCH
    ntt = T // 128
    nw = T // W_
    qT_hbm = nc.dram_tensor("l0f_qT", [512, T], BF16, kind="Internal").ap()
    kT_hbm = nc.dram_tensor("l0f_kT", [512, T], BF16, kind="Internal").ap()
    vt_hbm = nc.dram_tensor("l0f_vt", [T, 512], BF16, kind="Internal").ap()
    ao_hbm = nc.dram_tensor("l0f_ao", [512, T], BF16, kind="Internal").ap()
    x1_hbm = nc.dram_tensor("l0f_x1", [D, T], F32, kind="Internal").ap()
    x2_hbm = nc.dram_tensor("l0f_x2", [D, T], F32, kind="Internal").ap()
    ln_hbm = nc.dram_tensor("l0f_ln", [D, T], BF16, kind="Internal").ap()
    outr = out_hbm.rearrange("(kb p) t -> p kb t", p=128)
    ctx.fence()

    x_cur = xn_hbm_f32
    for li in range(2):
        x_next = x1_hbm if li == 0 else x2_hbm
        xr = x_cur.rearrange("(kb p) t -> p kb t", p=128)
        with contextlib.ExitStack() as ls:
            E = ls.enter_context
            wq_sb, twq = load_weight_ch(ctx, ls, f"l0f_wq{li}", w["wqkv"][li], D, 1536)
            qsc = ls.enter_context(nc.sbuf_tensor(f"l0f_qs{li}", [128, 1], F32))
            ksc = ls.enter_context(nc.sbuf_tensor(f"l0f_ks{li}", [128, 1], F32))
            src_q = w["q_scale"][li].rearrange("(p a) -> p a", a=1)
            src_k = w["k_scale"][li].rearrange("(p a) -> p a", a=1)
            ctx.dma("gpsimd", "l0f_qsld", qsc[0:64, 0:1], src_q)
            ctx.dma("gpsimd", "l0f_qsld", qsc[64:128, 0:1], src_q)
            ctx.dma("gpsimd", "l0f_qsld", ksc[0:64, 0:1], src_k)
            tq1 = ctx.dma("gpsimd", "l0f_qsld", ksc[64:128, 0:1], src_k)
            tk1 = tq1
            h_sb = E(nc.sbuf_tensor(f"l0f_h{li}", [128, 8, TCH], F32))
            hb_sb = E(nc.sbuf_tensor(f"l0f_hb{li}", [128, 8, TCH], BF16))
            qkvT = E(nc.sbuf_tensor(f"l0f_qkvT{li}", [128, 12, TCH], F32))
            nrm = E(nc.sbuf_tensor(f"l0f_nrm{li}", [2, TCH], F32))
            sq_sb = E(nc.sbuf_tensor(f"l0f_sq{li}", [128, TCH], BF16))
            hsel = E(nc.sbuf_tensor(f"l0f_hsel{li}", [128, 2], BF16))
            hselT = E(nc.sbuf_tensor(f"l0f_hselT{li}", [2, 128], F32))
            rtm = E(nc.sbuf_tensor(f"l0f_rtm{li}", [128, 128], BF16))
            tbl = E(nc.sbuf_tensor(f"l0f_tbl{li}", [128, 6, TCH], F32))
            iorow = E(nc.sbuf_tensor(f"l0f_ior{li}", [1, 128], F32))
            lnb_row = E(nc.sbuf_tensor(f"l0f_lnb{li}", [1, 128], F32))
            invf_row = E(nc.sbuf_tensor(f"l0f_invf{li}", [1, 128], F32))
            powr = E(nc.sbuf_tensor(f"l0f_powr{li}", [1, TCH], F32))
            qkb = E(nc.sbuf_tensor(f"l0f_qkb{li}", [128, 12, TCH], BF16))
            ps = [E(nc.psum_tensor(f"l0f_ps{i}_{li}", [128, TCH], F32)) for i in range(2)]
            ps_n = E(nc.psum_tensor(f"l0f_psn{li}", [2, TCH], F32))
            ps_b = E(nc.psum_tensor(f"l0f_psb{li}", [128, TCH], F32))
            ps_tr = E(nc.psum_tensor(f"l0f_ptr{li}", [128, 8, 128], BF16))
            g1 = ctx.inc(nc.gpsimd.memset(hsel[:], 0.0), "gpsimd")
            g2 = ctx.inc(nc.gpsimd.memset(hselT[:], 0.0), "gpsimd")
            ctx.wait("gpsimd", g1, g2, com.ready)
            g3 = ctx.inc(nc.gpsimd.memset(hsel[0:64, 0:1], 1.0), "gpsimd")
            g4 = ctx.inc(nc.gpsimd.memset(hsel[64:128, 1:2], 1.0), "gpsimd")
            ctx.wait("tensor", g3, g4, com.ready)
            hself = ls.enter_context(nc.sbuf_tensor(f"l0f_hself_{li}", [128, 2], F32))
            ctx.wait("vector", g4)
            g5v = ctx.inc(nc.vector.tensor_copy(hself[:], hsel[:]), "vector")
            ctx.wait("tensor", g5v)
            g56 = ctx.inc(nc.tensor.matmul(ps_b[0:2, 0:128], hself[:],
                                           com.ident_f32[:], start=True, stop=True,
                                           skip_group_check=True), "tensor")
            ctx.wait("vector", g56)
            g6 = ctx.inc(nc.vector.tensor_copy(hselT[:], ps_b[0:2, 0:128]), "vector")
            g5 = g6
            gr = ctx.inc(nc.gpsimd.memset(rtm[:], 0.0), "gpsimd")
            ctx.wait("gpsimd", gr)
            gr = ctx.inc(nc.gpsimd.affine_select(
                out=rtm[:], in_=rtm[:], compare_op=ALU.not_equal, fill=-1.0,
                base=-32, pattern=[[-1, 128]], channel_multiplier=1), "gpsimd")
            ctx.wait("gpsimd", gr)
            gr = ctx.inc(nc.gpsimd.affine_select(
                out=rtm[:], in_=rtm[:], compare_op=ALU.not_equal, fill=1.0,
                base=32, pattern=[[-1, 128]], channel_multiplier=1), "gpsimd")
            ctx.wait("gpsimd", gr)
            gr1 = ctx.inc(nc.gpsimd.memset(rtm[64:96, 32:64], 0.0), "gpsimd")
            gr2 = ctx.inc(nc.gpsimd.memset(rtm[32:64, 64:96], 0.0), "gpsimd")
            gi = ctx.inc(nc.gpsimd.iota(invf_row[:], pattern=[[0, 4], [1, 32]], base=0,
                                        channel_multiplier=0,
                                        allow_small_or_imprecise_dtypes=True), "gpsimd")
            ctx.wait("scalar", gi)
            si = act(ctx, com, invf_row[:], invf_row[:], AF.Exp,
                     scale=-math.log(10000.0) / 32.0)
            gi2 = ctx.inc(nc.gpsimd.iota(lnb_row[:], pattern=[[0, 4], [1, 32]], base=0,
                                         channel_multiplier=0,
                                         allow_small_or_imprecise_dtypes=True), "gpsimd")
            ctx.wait("vector", gi2)
            v1 = ctx.inc(nc.vector.tensor_scalar(out=lnb_row[:], in0=lnb_row[:],
                                                 scalar1=2.0, scalar2=0.4 * DH,
                                                 op0=ALU.mult, op1=ALU.add), "vector")
            ctx.wait("scalar", v1)
            s1 = act(ctx, com, lnb_row[:], lnb_row[:], AF.Ln, scale=1.0 / (1.4 * DH))
            ctx.wait("tensor", twq, si, s1, g3, g4, g5, g6, gr1, gr2, com.ready)
            ctx.wait("vector", tq1, tk1)
            prev = None
            ev = [None, None]
            last_bv = None
            prev_tf2 = None
            for j in range(ntc):
                cj = slice(j * TCH, (j + 1) * TCH)
                if prev is not None:
                    ctx.wait("sync", prev)
                    ctx.barrier()
                ctx.dma("sync", "l0f_hl", h_sb[:, 0:4, :], xr[:, 0:4, cj])
                if j == 0:
                    t2_ = ctx.dma("sync", "l0f_hl", h_sb[:, 4:8, 1:TCH],
                                  xr[:, 4:8, 0:TCH - 1])
                    gz = ctx.inc(nc.gpsimd.memset(h_sb[:, 4:8, 0:1], 0.0), "gpsimd")
                    ctx.wait("vector", gz)
                else:
                    t2_ = ctx.dma("sync", "l0f_hl", h_sb[:, 4:8, :],
                                  xr[:, 4:8, j * TCH - 1:(j + 1) * TCH - 1])
                ctx.wait("vector", t2_)
                vb = ctx.inc(nc.vector.tensor_copy(hb_sb[:], h_sb[:]), "vector")
                ctx.wait("tensor", vb)
                for mi in range(12):
                    pp = ps[mi % 2]
                    if ev[mi % 2] is not None:
                        ctx.wait("tensor", ev[mi % 2])
                    msl = slice(mi * 128, (mi + 1) * 128)
                    ins = mm_acc(nc, pp[:], [wq_sb[:, kc, msl] for kc in range(8)],
                                 [hb_sb[:, kc, :] for kc in range(8)])
                    tmm = ctx.inc(ins, "tensor")
                    ctx.wait("vector", tmm)
                    ev[mi % 2] = ctx.inc(nc.vector.tensor_copy(qkvT[:, mi, :], pp[:]),
                                         "vector")
                ctx.wait("scalar", ev[0], ev[1])
                for qi in range(2):
                    scol = qsc if qi == 0 else ksc
                    for blk in range(4):
                        gmi = qi * 4 + blk
                        sq = act(ctx, com, sq_sb[:], qkvT[:, gmi, :], AF.Square)
                        ctx.wait("tensor", sq)
                        ins = nc.tensor.matmul(ps_n[0:2, :], hsel[:], sq_sb[:],
                                               start=True, stop=True,
                                               skip_group_check=True)
                        tmm = ctx.inc(ins, "tensor")
                        ctx.wait("scalar", tmm)
                        sn = sqrt_via_ln(ctx, com, nrm[:], ps_n[0:2, :], eps=1e-20,
                                         scale_exp=-0.5)
                        ctx.wait("tensor", sn)
                        if last_bv is not None:
                            ctx.wait("tensor", last_bv)
                        nb = nc.tensor.matmul(ps_b[:], hselT[:], nrm[:],
                                              start=True, stop=True,
                                              skip_group_check=True)
                        tb = ctx.inc(nb, "tensor")
                        ctx.wait("vector", tb)
                        v1 = ctx.inc(nc.vector.tensor_tensor(
                            out=qkvT[:, gmi, :], in0=qkvT[:, gmi, :], in1=ps_b[:],
                            op=ALU.mult), "vector")
                        last_bv = v1
                        ctx.wait("vector", v1)
                        v2 = ctx.inc(nc.vector.tensor_scalar(
                            out=qkvT[:, gmi, :], in0=qkvT[:, gmi, :],
                            scalar1=scol[:, 0:1], scalar2=None, op0=ALU.mult), "vector")
                        ev[gmi % 2] = v2
                ctx.wait("vector", ev[0], ev[1])
                if prev_tf2 is not None:
                    ctx.wait("gpsimd", prev_tf2)
                gi3 = ctx.inc(nc.gpsimd.iota(powr[:], pattern=[[1, TCH]],
                                             base=j * TCH, channel_multiplier=0,
                                             allow_small_or_imprecise_dtypes=True),
                              "gpsimd")
                ctx.wait("vector", gi3)
                vpw = ctx.inc(nc.vector.tensor_scalar(
                    out=powr[:], in0=powr[:], scalar1=float(-(T // 2)),
                    scalar2=1.0 / (W_ // 2), op0=ALU.add, op1=ALU.mult), "vector")
                ctx.wait("tensor", vpw)
                bm1 = nc.tensor.matmul(ps_b[:], invf_row[:], powr[:], start=True,
                                       stop=True, skip_group_check=True)
                t_f = ctx.inc(bm1, "tensor")
                ctx.wait("vector", t_f)
                vfq = ctx.inc(nc.vector.tensor_scalar(
                    out=tbl[:, 4, :], in0=ps_b[:], scalar1=float(W_ // 2),
                    scalar2=None, op0=ALU.mult), "vector")
                ctx.wait("vector", vfq)
                vfq = ctx.inc(nc.vector.tensor_scalar(
                    out=tbl[:, 4, :], in0=tbl[:, 4, :], scalar1=invf2_, scalar2=None,
                    op0=ALU.bypass), "vector") if False else vfq
                ctx.wait("tensor", vfq)
                bm2 = nc.tensor.matmul(ps_b[:], lnb_row[:], powr[:], start=True,
                                       stop=True, skip_group_check=True)
                t_s = ctx.inc(bm2, "tensor")
                ctx.wait("scalar", t_s)
                ssc = act(ctx, com, tbl[:, 2, :], ps_b[:], AF.Exp)
                ctx.wait("scalar", ssc)
                ssci = act(ctx, com, tbl[:, 3, :], ps_b[:], AF.Exp, scale=-1.0)
                ctx.wait("vector", ssci)
                vC = ctx.inc(nc.vector.scalar_tensor_tensor(
                    out=tbl[:, 4, :], in0=tbl[:, 4, :], scalar=1.0, in1=tbl[:, 4, :],
                    op0=ALU.bypass, op1=ALU.bypass), "vector") if False else None
                ctx.wait("gpsimd", t_f, t_s)
                gi4 = ctx.inc(nc.gpsimd.iota(powr[:], pattern=[[1, TCH]],
                                             base=j * TCH, channel_multiplier=0,
                                             allow_small_or_imprecise_dtypes=True),
                              "gpsimd")
                ctx.wait("tensor", gi4, ssc, ssci)
                bm3 = nc.tensor.matmul(ps_b[:], invf_row[:], powr[:], start=True,
                                       stop=True, skip_group_check=True)
                t_f2 = ctx.inc(bm3, "tensor")
                prev_tf2 = t_f2
                for half, ti in ((0, 0), (1, 1)):
                    sh = math.pi / 2 if half == 0 else 0.0
                    ctx.wait("vector", t_f2)
                    v1 = ctx.inc(nc.vector.tensor_scalar(
                        out=tbl[:, ti, :], in0=ps_b[:], scalar1=float(sh),
                        scalar2=None, op0=ALU.add), "vector")
                    ctx.wait("vector", v1)
                    v2 = ctx.inc(nc.vector.tensor_scalar(
                        out=tbl[:, 5, :], in0=tbl[:, ti, :],
                        scalar1=float(1.0 / (2 * math.pi)),
                        scalar2=float(2 ** 23), op0=ALU.mult, op1=ALU.add), "vector")
                    ctx.wait("vector", v2)
                    v2 = ctx.inc(nc.vector.tensor_scalar(
                        out=tbl[:, 5, :], in0=tbl[:, 5, :],
                        scalar1=float(2 ** 23), scalar2=None,
                        op0=ALU.subtract), "vector")
                    ctx.wait("vector", v2)
                    v3 = ctx.inc(nc.vector.scalar_tensor_tensor(
                        out=tbl[:, ti, :], in0=tbl[:, 5, :],
                        scalar=float(-2 * math.pi), in1=tbl[:, ti, :],
                        op0=ALU.mult, op1=ALU.add), "vector")
                    ctx.wait("scalar", v3)
                    s1 = act(ctx, com, tbl[:, ti, :], tbl[:, ti, :], AF.Sin)
                    ctx.wait("vector", s1)
                ctx.wait("vector", ctx.tok("l0f_qst"), ctx.tok("l0f_kst"),
                         ctx.tok("l0f_vst"))
                for gmi in range(8):
                    ti_sc = 2 if gmi < 4 else 3
                    ctx.wait("tensor", ev[gmi % 2] if ev[gmi % 2] else (0, 0))
                    rb = nc.tensor.matmul(ps[gmi % 2][:], rtm[:],
                                          qkb[:, gmi, :] if False else None,
                                          start=True, stop=True) if False else None
                    vc = ctx.inc(nc.vector.tensor_copy(qkb[:, gmi, :], qkvT[:, gmi, :]),
                                 "vector")
                    ctx.wait("tensor", vc)
                    rb = ctx.inc(nc.tensor.matmul(ps[gmi % 2][:], rtm[:], qkb[:, gmi, :],
                                                  start=True, stop=True,
                                                  skip_group_check=True), "tensor")
                    ctx.wait("vector", rb)
                    v1 = ctx.inc(nc.vector.tensor_tensor(
                        out=tbl[:, 5, :], in0=ps[gmi % 2][:], in1=tbl[:, 1, :],
                        op=ALU.mult), "vector")
                    v2 = ctx.inc(nc.vector.tensor_tensor(
                        out=qkvT[:, gmi, :], in0=qkvT[:, gmi, :], in1=tbl[:, 0, :],
                        op=ALU.mult), "vector")
                    ctx.wait("vector", v1, v2)
                    v3 = ctx.inc(nc.vector.tensor_tensor(
                        out=qkvT[:, gmi, :], in0=qkvT[:, gmi, :], in1=tbl[:, 5, :],
                        op=ALU.add), "vector")
                    ctx.wait("vector", v3)
                    v4 = ctx.inc(nc.vector.tensor_tensor(
                        out=qkb[:, gmi, :], in0=qkvT[:, gmi, :], in1=tbl[:, ti_sc, :],
                        op=ALU.mult), "vector")
                    ev[gmi % 2] = v4
                for gmi in range(8, 12):
                    ctx.wait("vector", ev[gmi % 2] if ev[gmi % 2] else (0, 0))
                    ev[gmi % 2] = ctx.inc(nc.vector.tensor_copy(qkb[:, gmi, :],
                                                                qkvT[:, gmi, :]),
                                          "vector")
                ctx.wait("sync", ev[0], ev[1])
                ctx.dma("sync", "l0f_qst",
                        qT_hbm.rearrange("(kb p) t -> p kb t", p=128)[:, :, cj],
                        qkb[:, 0:4, :])
                ctx.dma("sync", "l0f_kst",
                        kT_hbm.rearrange("(kb p) t -> p kb t", p=128)[:, :, cj],
                        qkb[:, 4:8, :])
                tvst = None
                ctx.wait("tensor", ev[0], ev[1])
                for tt4 in range(4):
                    ins = None
                    for kb in range(4):
                        ins = nc.tensor.transpose(
                            ps_tr[0:128, kb, :],
                            qkb[:, 8 + kb, tt4 * 128:(tt4 + 1) * 128],
                            com.ident_bf[:])
                    ttr = ctx.inc(ins, "tensor")
                    ctx.wait("vector", ttr)
                    if tvst is not None:
                        ctx.wait("vector", tvst)
                    vtr = ctx.inc(nc.vector.tensor_copy(
                        sq_sb[:].rearrange("p (a f) -> p a f", f=128) if False else
                        qkvT[:, 0:4, tt4 * 128:(tt4 + 1) * 128].bitcast(BF16)[:, 0:4, 0:128]
                        if False else hb_sb[:, 0:4, 0:128],
                        ps_tr[0:128, 0:4, :]), "vector")
                    ctx.wait("sync", vtr)
                    tvst = ctx.dma("sync", "l0f_vst",
                                   vt_hbm[cj.start + tt4 * 128:cj.start + (tt4 + 1) * 128, :],
                                   hb_sb[:, 0:4, 0:128])
                    ctx.wait("tensor", tvst)
                    ctx.wait("vector", tvst)
                prev = ctx.tok("vector")
                ev = [ctx.tok("vector"), ctx.tok("vector")]
            ctx.wait("sync", ctx.tok("l0f_qst"), ctx.tok("l0f_kst"), ctx.tok("l0f_vst"))
        ctx.fence()

        with contextlib.ExitStack() as ls:
            E = ls.enter_context
            qh = E(nc.sbuf_tensor(f"l0f_qh{li}", [64, T], BF16))
            kh = E(nc.sbuf_tensor(f"l0f_kh{li}", [64, T], BF16))
            vah = E(nc.sbuf_tensor(f"l0f_vah{li}", [128, T // 128, 65], BF16))
            ssb = E(nc.sbuf_tensor(f"l0f_ssb{li}", [128, 1024], F32))
            mrow = E(nc.sbuf_tensor(f"l0f_mrow{li}", [128, 1], F32))
            psb = E(nc.sbuf_tensor(f"l0f_psb2{li}", [128, 1024], BF16))
            drow = E(nc.sbuf_tensor(f"l0f_drow{li}", [1, 128], F32))
            bcs = E(nc.sbuf_tensor(f"l0f_bcs{li}", [64, 128], F32))
            ao_sb = E(nc.sbuf_tensor(f"l0f_aos{li}", [64, T], BF16))
            ps_s = [E(nc.psum_tensor(f"l0f_pss{i}_{li}", [128, TCH], F32)) for i in range(2)]
            ps_pt = E(nc.psum_tensor(f"l0f_ppt{li}", [128, 8, 128], BF16))
            ps_n = E(nc.psum_tensor(f"l0f_pn2{li}", [128, 512], F32))
            ps_b2 = E(nc.psum_tensor(f"l0f_pb2{li}", [128, 512], F32))
            amsk = E(nc.sbuf_tensor(f"l0f_amsk{li}", [128, 4, 1024], F32))
            amsk2 = E(nc.sbuf_tensor(f"l0f_amsk2{li}", [128, 1024], F32))
            gv = ctx.inc(nc.gpsimd.memset(vah[:, :, 64:65], 1.0), "gpsimd")
            ga = ctx.inc(nc.gpsimd.memset(amsk[:], 0.0), "gpsimd")
            ctx.wait("gpsimd", ga)
            for qt in range(4):
                ga = ctx.inc(nc.gpsimd.affine_select(
                    out=amsk[:, qt, :], in_=amsk[:, qt, :], compare_op=ALU.is_ge,
                    fill=-1e30, base=512 + qt * 128, pattern=[[-1, 1024]],
                    channel_multiplier=1), "gpsimd")
                gb = ctx.inc(nc.gpsimd.memset(amsk2[:], -1e30), "gpsimd")
                ctx.wait("gpsimd", ga, gb)
                gb = ctx.inc(nc.gpsimd.affine_select(
                    out=amsk2[:], in_=amsk2[:], compare_op=ALU.is_ge,
                    fill=0.0, base=qt * 128 - 1, pattern=[[-1, 1024]],
                    channel_multiplier=1), "gpsimd")
                ctx.wait("vector", ga, gb)
                ga = ctx.inc(nc.vector.tensor_tensor(
                    out=amsk[:, qt, :], in0=amsk[:, qt, :], in1=amsk2[:],
                    op=ALU.add), "vector")
                ctx.wait("gpsimd", ga)
            ctx.wait("vector", ga)
            ctx.wait("sync", gv, com.ready)
            for h in range(HEADS):
                ctx.wait("sync", ctx.tok("tensor"), ctx.tok("vector"))
                tq = ctx.dma("sync", "l0f_qhl", qh[:], qT_hbm[h * 64:(h + 1) * 64, :])
                tk = ctx.dma("sync", "l0f_khl", kh[:], kT_hbm[h * 64:(h + 1) * 64, :])
                tv = ctx.dma("sync", "l0f_vhl", vah[:, :, 0:64],
                             vt_hbm.rearrange("(a p) d -> p a d", p=128)[:, :, h * 64:(h + 1) * 64])
                ctx.wait("tensor", tq, tk, tv)
                t_ao = None
                for wi in range(nw):
                    k0 = max(0, (wi - 1) * W_)
                    nkv = (wi + 1) * W_ - k0
                    for qt in range(4):
                        q0 = wi * W_ + qt * 128
                        ev_s = None
                        for half in range(nkv // 512):
                            pp = ps_s[half]
                            ins = nc.tensor.matmul(
                                pp[:], qh[:, q0:q0 + 128],
                                kh[:, k0 + half * 512:k0 + (half + 1) * 512],
                                start=True, stop=True, skip_group_check=True)
                            tmm = ctx.inc(ins, "tensor")
                            ctx.wait("vector", tmm)
                            if t_ao is not None:
                                ctx.wait("vector", t_ao)
                                t_ao = None
                            ev_s = ctx.inc(nc.vector.tensor_scalar(
                                out=ssb[:, half * 512:(half + 1) * 512], in0=pp[:],
                                scalar1=8.0, scalar2=None, op0=ALU.mult), "vector")
                            ctx.wait("vector", ev_s)
                        msrc = amsk[:, qt, 1024 - nkv:1024]
                        g2 = ctx.inc(nc.vector.tensor_tensor(
                            out=ssb[:, 0:nkv], in0=ssb[:, 0:nkv], in1=msrc,
                            op=ALU.add), "vector")
                        ctx.wait("vector", g2)
                        vm = ctx.inc(nc.vector.reduce_max(mrow[:], ssb[:, 0:nkv],
                                                          axis=AX.X), "vector")
                        ctx.wait("vector", vm)
                        vneg = ctx.inc(nc.vector.tensor_scalar(
                            out=mrow[:], in0=mrow[:], scalar1=-1.0, scalar2=None,
                            op0=ALU.mult), "vector")
                        ctx.wait("scalar", vneg)
                        sx = act(ctx, com, psb[:, 0:nkv], ssb[:, 0:nkv], AF.Exp,
                                 bias=mrow[:])
                        ctx.wait("tensor", sx)
                        ins = None
                        for kb in range(nkv // 128):
                            ins = nc.tensor.transpose(ps_pt[0:128, kb, :],
                                                      psb[:, kb * 128:(kb + 1) * 128],
                                                      com.ident_bf[:])
                        ttr = ctx.inc(ins, "tensor")
                        ctx.wait("vector", ttr)
                        vtr = ctx.inc(nc.vector.tensor_copy(
                            psb[:, 0:nkv].rearrange("p (a f) -> p a f", f=128),
                            ps_pt[0:128, 0:nkv // 128, :]), "vector")
                        ctx.wait("tensor", vtr)
                        ins = None
                        for kb in range(nkv // 128):
                            ins = nc.tensor.matmul(
                                ps_n[0:65, 0:128], vah[:, k0 // 128 + kb, :],
                                psb[:, kb * 128:(kb + 1) * 128],
                                start=(kb == 0), stop=(kb == nkv // 128 - 1),
                                skip_group_check=True)
                        tnum = ctx.inc(ins, "tensor")
                        ctx.wait("vector", tnum)
                        v1 = ctx.inc(nc.vector.reciprocal(drow[:], ps_n[64:65, 0:128]),
                                     "vector")
                        ctx.wait("tensor", v1)
                        tb = ctx.inc(nc.tensor.matmul(ps_b2[0:64, 0:128],
                                                      com.ones_row_f32[0:1, 0:64],
                                                      drow[:], start=True, stop=True,
                                                      skip_group_check=True), "tensor")
                        ctx.wait("vector", tb)
                        v2 = ctx.inc(nc.vector.tensor_copy(bcs[:], ps_b2[0:64, 0:128]),
                                     "vector")
                        ctx.wait("vector", v2, ctx.tok("l0f_aost"))
                        t_ao = ctx.inc(nc.vector.tensor_tensor(
                            out=ao_sb[:, q0:q0 + 128], in0=ps_n[0:64, 0:128],
                            in1=bcs[:], op=ALU.mult), "vector")
                ctx.wait("sync", t_ao)
                ctx.dma("sync", "l0f_aost", ao_hbm[h * 64:(h + 1) * 64, :], ao_sb[:])
            ctx.wait("sync", ctx.tok("l0f_aost"))
        ctx.fence()

        with contextlib.ExitStack() as ls:
            E = ls.enter_context
            wo_sb, two = load_weight_ch(ctx, ls, f"l0f_wo{li}", w["wo"][li], 512, D)
            ao_sb = E(nc.sbuf_tensor(f"l0f_aoc{li}", [128, 4, TCH], BF16))
            x_sb = E(nc.sbuf_tensor(f"l0f_xc{li}", [128, 8, TCH], F32))
            fo_sb = E(nc.sbuf_tensor(f"l0f_foc{li}", [128, 8, TCH], F32))
            ps = [E(nc.psum_tensor(f"l0f_psc{i}_{li}", [128, TCH], F32)) for i in range(2)]
            aor = ao_hbm.rearrange("(kb p) t -> p kb t", p=128)
            xnr2 = x_cur.rearrange("(kb p) t -> p kb t", p=128)
            x1r = x_next.rearrange("(kb p) t -> p kb t", p=128)
            ctx.wait("tensor", two, com.ready)
            prev = None
            ev = [None, None]
            for j in range(ntc):
                cj = slice(j * TCH, (j + 1) * TCH)
                if prev is not None:
                    ctx.wait("sync", prev)
                tao = ctx.dma("sync", "l0f_aocl", ao_sb[:], aor[:, :, cj])
                txn = ctx.dma("sync", "l0f_xcl", x_sb[:], xnr2[:, :, cj])
                ctx.wait("tensor", tao)
                for mi in range(8):
                    pp = ps[mi % 2]
                    if ev[mi % 2] is not None:
                        ctx.wait("tensor", ev[mi % 2])
                    msl = slice(mi * 128, (mi + 1) * 128)
                    ins = mm_acc(nc, pp[:], [wo_sb[:, kc, msl] for kc in range(4)],
                                 [ao_sb[:, kc, :] for kc in range(4)])
                    tmm = ctx.inc(ins, "tensor")
                    ctx.wait("vector", tmm, txn)
                    ev[mi % 2] = ctx.inc(nc.vector.tensor_tensor(
                        out=fo_sb[:, mi, :], in0=x_sb[:, mi, :], in1=pp[:],
                        op=ALU.add), "vector")
                ctx.wait("sync", ev[0], ev[1])
                tst = ctx.dma("sync", "l0f_x1st", x1r[:, :, cj], fo_sb[:])
                prev = tst
            ctx.wait("sync", ctx.tok("l0f_x1st"))
        ctx.fence()

        x2_dst = out_hbm if li == 1 else x1_hbm
        x_mid = x_next
        x_fin = x2_hbm if li == 0 else out_hbm
        emit_ln(ctx, com, T, x_mid, xn_ch_hbm=ln_hbm, shift_half=True)
        ctx.fence()
        with contextlib.ExitStack() as ls:
            E = ls.enter_context
            lgc, t1_ = load_bias_col(ctx, ls, f"l0f_lg{li}", w["ln_g"][li], D)
            lbc, t2_ = load_bias_col(ctx, ls, f"l0f_lb{li}", w["ln_b"][li], D)
            b1ac, t3_ = load_bias_col(ctx, ls, f"l0f_b1a{li}",
                                      w["b1"][li][0:2730], 2730)
            b1gc, t4_ = load_bias_col(ctx, ls, f"l0f_b1g{li}",
                                      w["b1"][li][2730:5460], 2730)
            b2c, t5_ = load_bias_col(ctx, ls, f"l0f_b2{li}", w["b2"][li], D)
            w2_sb, tw2 = load_weight_ch(ctx, ls, f"l0f_w2{li}", w["w2"][li], 2730, D)
            h_sb = E(nc.sbuf_tensor(f"l0f_hd{li}", [128, 8, TCH], BF16))
            x_sb = E(nc.sbuf_tensor(f"l0f_xd{li}", [128, 8, TCH], F32))
            w1a = E(nc.sbuf_tensor(f"l0f_w1a{li}", [128, 8, 1408], BF16))
            w1g = E(nc.sbuf_tensor(f"l0f_w1g{li}", [128, 8, 1408], BF16))
            hid = E(nc.sbuf_tensor(f"l0f_hid{li}", [128, 22, TCH], BF16))
            sg = E(nc.sbuf_tensor(f"l0f_sgd{li}", [128, TCH], F32))
            fo_sb = E(nc.sbuf_tensor(f"l0f_fod{li}", [128, 8, TCH], F32))
            ps = [E(nc.psum_tensor(f"l0f_psd{i}_{li}", [128, TCH], F32)) for i in range(4)]
            lnr = ln_hbm.rearrange("(kb p) t -> p kb t", p=128)
            xmr = x_mid.rearrange("(kb p) t -> p kb t", p=128)
            xfr = x_fin.rearrange("(kb p) t -> p kb t", p=128)
            ctx.wait("vector", t1_, t2_, t3_, t4_, t5_)
            ctx.wait("tensor", tw2, com.ready)
            ev = [None, None, None, None]
            prev = None
            prev_st = None
            last_v2 = None
            for j in range(ntc):
                cj = slice(j * TCH, (j + 1) * TCH)
                if prev is not None:
                    ctx.wait("sync", prev)
                    ctx.barrier()
                tln = ctx.dma("sync", "l0f_lnl", h_sb[:], lnr[:, :, cj])
                txm = ctx.dma("sync", "l0f_xml", x_sb[:], xmr[:, :, cj])
                ctx.wait("vector", tln)
                va = None
                for kb in range(8):
                    va = ctx.inc(nc.vector.tensor_scalar(
                        out=h_sb[:, kb, :], in0=h_sb[:, kb, :],
                        scalar1=lgc[:, kb:kb + 1], scalar2=lbc[:, kb:kb + 1],
                        op0=ALU.mult, op1=ALU.add), "vector")
                vln = va
                ctx.wait("tensor", vln)
                for sl in range(2):
                    ctx.wait("gpsimd", ctx.tok("tensor"))
                    asl = slice(sl * 1408, min((sl + 1) * 1408, 2730))
                    na = asl.stop - asl.start
                    twa = ctx.dma("gpsimd", "l0f_w1al", w1a[:, :, 0:na],
                                  w["w1"][li][:, asl].rearrange("(kc p) m -> p kc m", p=128))
                    twg = ctx.dma("gpsimd", "l0f_w1gl", w1g[:, :, 0:na],
                                  w["w1"][li][:, 2730 + asl.start:2730 + asl.stop]
                                  .rearrange("(kc p) m -> p kc m", p=128))
                    ctx.wait("tensor", twa, twg)
                    nmi = (na + 127) // 128
                    for mi in range(nmi):
                        gmi = sl * 11 + mi
                        mp = min(128, na - mi * 128)
                        pa = ps[2 * (mi % 2)]
                        pg = ps[2 * (mi % 2) + 1]
                        if ev[2 * (mi % 2)] is not None:
                            ctx.wait("tensor", ev[2 * (mi % 2)])
                        msl = slice(mi * 128, mi * 128 + mp)
                        mm_acc(nc, pa[0:mp, :], [w1a[:, kc, msl] for kc in range(8)],
                               [h_sb[:, kc, :] for kc in range(8)])
                        ins = mm_acc(nc, pg[0:mp, :], [w1g[:, kc, msl] for kc in range(8)],
                                     [h_sb[:, kc, :] for kc in range(8)])
                        tmm = ctx.inc(ins, "tensor")
                        ctx.wait("scalar", tmm)
                        if last_v2 is not None:
                            ctx.wait("scalar", last_v2)
                        sgl = act(ctx, com, sg[0:mp, :], pg[0:mp, :], AF.Gelu,
                                  bias=b1gc[0:mp, gmi:gmi + 1])
                        ctx.wait("vector", sgl)
                        v1 = ctx.inc(nc.vector.tensor_scalar(
                            out=hid[0:mp, gmi, :], in0=pa[0:mp, :],
                            scalar1=b1ac[0:mp, gmi:gmi + 1], scalar2=None,
                            op0=ALU.add), "vector")
                        ctx.wait("vector", v1)
                        v2 = ctx.inc(nc.vector.tensor_tensor(
                            out=hid[0:mp, gmi, :], in0=hid[0:mp, gmi, :],
                            in1=sg[0:mp, :], op=ALU.mult), "vector")
                        last_v2 = v2
                        ev[2 * (mi % 2)] = v2
                        ev[2 * (mi % 2) + 1] = v2
                ctx.wait("tensor", ev[0] or (0, 0), ev[2] or (0, 0))
                for mi in range(8):
                    pp = ps[mi % 2]
                    if ev[mi % 2] is not None:
                        ctx.wait("tensor", ev[mi % 2])
                        ev[mi % 2] = None
                    msl = slice(mi * 128, (mi + 1) * 128)
                    kcs = []
                    for kc in range(22):
                        kp = min(128, 2730 - kc * 128)
                        kcs.append((kc, kp))
                    ins = None
                    for idx2, (kc, kp) in enumerate(kcs):
                        ins = nc.tensor.matmul(pp[:], w2_sb[0:kp, kc, msl],
                                               hid[0:kp, kc, :],
                                               start=(idx2 == 0), stop=(idx2 == 21),
                                               skip_group_check=True)
                    tmm = ctx.inc(ins, "tensor")
                    ctx.wait("vector", tmm, txm)
                    if prev_st is not None:
                        ctx.wait("vector", prev_st)
                        prev_st = None
                    v1 = ctx.inc(nc.vector.tensor_scalar(
                        out=fo_sb[:, mi, :], in0=pp[:], scalar1=b2c[:, mi:mi + 1],
                        scalar2=None, op0=ALU.add), "vector")
                    ctx.wait("vector", v1)
                    ev[mi % 2] = ctx.inc(nc.vector.tensor_tensor(
                        out=fo_sb[:, mi, :], in0=fo_sb[:, mi, :], in1=x_sb[:, mi, :],
                        op=ALU.add), "vector")
                ctx.wait("sync", ev[0], ev[1])
                tst = ctx.dma("sync", "l0f_fst", xfr[:, :, cj], fo_sb[:])
                prev_st = tst
                prev = ctx.tok("vector")
                ev = [ctx.tok("vector")] * 4
            ctx.wait("sync", ctx.tok("l0f_fst"))
        ctx.fence()
        x_cur = x_fin
    return



def emit_l2_attn(ctx, com, T, xn_ch, xn_f32, w, out_hbm):
    nc = ctx.nc
    DS = 256
    ntc = T // TCH
    g_hbm = nc.dram_tensor("l2a_g", [D, T], BF16, kind="Internal").ap()
    xnr = xn_ch.rearrange("(kb p) t -> p kb t", p=128)
    xfr = xn_f32.rearrange("(kb p) t -> p kb t", p=128)
    gr = g_hbm.rearrange("(kb p) t -> p kb t", p=128)
    outr = out_hbm.rearrange("(kb p) t -> p kb t", p=128)
    ctx.fence()
    with contextlib.ExitStack() as ls:
        E = ls.enter_context
        win_sb, tw1 = load_weight_ch(ctx, ls, "l2a_win", w["W_in"], D, DS)
        wg_sb, tw2 = load_weight_ch(ctx, ls, "l2a_wg", w["W_gate"], D, D)
        bm_sb, tw3 = load_weight_ch(ctx, ls, "l2a_bm", w["Bm"], DS, DS)
        a_sb, tw4 = load_weight_ch(ctx, ls, "l2a_a", w["A"], DS, DS)
        binc, tb1 = load_bias_col(ctx, ls, "l2a_bin", w["b_in"], DS)
        bgc, tb2 = load_bias_col(ctx, ls, "l2a_bg", w["b_gate"], D)
        boc, tb3 = load_bias_col(ctx, ls, "l2a_bo", w["b_out"], D)
        xn_sb = E(nc.sbuf_tensor("l2a_xn", [128, 8, TCH], BF16))
        u_sb = E(nc.sbuf_tensor("l2a_u", [128, 2, T], BF16))
        c_sb = E(nc.sbuf_tensor("l2a_c", [128, 2, T], F32))
        hseq = E(nc.sbuf_tensor("l2a_h", [128, 2, T], BF16))
        gt_sb = E(nc.sbuf_tensor("l2a_gt", [128, 8, TCH], BF16))
        ps = [E(nc.psum_tensor(f"l2a_ps{i}", [128, TCH], F32)) for i in range(2)]
        ps_h = E(nc.psum_tensor("l2a_psh", [128, 512], F32))
        ctx.wait("tensor", tw1, tw2, tw3, tw4, com.ready)
        ctx.wait("vector", tb1, tb2, tb3)
        ctx.wait("scalar", com.ready)
        prev = None
        ev = [None, None]
        for j in range(ntc):
            cj = slice(j * TCH, (j + 1) * TCH)
            if prev is not None:
                ctx.wait("sync", prev)
            txn = ctx.dma("sync", "l2a_xnl", xn_sb[:], xnr[:, :, cj])
            ctx.wait("tensor", txn)
            for mi in range(2):
                pp = ps[mi % 2]
                if ev[mi % 2] is not None:
                    ctx.wait("tensor", ev[mi % 2])
                msl = slice(mi * 128, (mi + 1) * 128)
                ins = mm_acc(nc, pp[:], [win_sb[:, kc, msl] for kc in range(8)],
                             [xn_sb[:, kc, :] for kc in range(8)])
                tmm = ctx.inc(ins, "tensor")
                ctx.wait("vector", tmm)
                ev[mi % 2] = ctx.inc(nc.vector.tensor_scalar(
                    out=u_sb[:, mi, cj], in0=pp[:], scalar1=binc[:, mi:mi + 1],
                    scalar2=None, op0=ALU.add), "vector")
            ctx.wait("tensor", ev[0], ev[1])
            for mi in range(2):
                pp = ps[mi % 2]
                msl = slice(mi * 128, (mi + 1) * 128)
                ins = mm_acc(nc, pp[:], [bm_sb[:, kc, msl] for kc in range(2)],
                             [u_sb[:, kc, cj] for kc in range(2)])
                tmm = ctx.inc(ins, "tensor")
                ctx.wait("vector", tmm)
                ev[mi % 2] = ctx.inc(nc.vector.tensor_copy(c_sb[:, mi, cj], pp[:]),
                                     "vector")
            ctx.wait("tensor", ev[0], ev[1])
            ctx.wait("scalar", ctx.tok("l2a_gst"))
            prev_g = None
            for mi in range(8):
                pp = ps[mi % 2]
                if ev[mi % 2] is not None:
                    ctx.wait("tensor", ev[mi % 2])
                msl = slice(mi * 128, (mi + 1) * 128)
                ins = mm_acc(nc, pp[:], [wg_sb[:, kc, msl] for kc in range(8)],
                             [xn_sb[:, kc, :] for kc in range(8)])
                tmm = ctx.inc(ins, "tensor")
                ctx.wait("scalar", tmm)
                if prev_g is not None:
                    ctx.wait("scalar", prev_g)
                ev[mi % 2] = act(ctx, com, gt_sb[:, mi, :], pp[:], AF.Sigmoid,
                                 bias=bgc[:, mi:mi + 1])
                prev_g = ctx.tok("ch_l2a_gst") if False else None
            ctx.wait("sync", ev[0], ev[1], ctx.tok("l2a_gst"))
            tg = ctx.dma("sync", "l2a_gst", gr[:, :, cj], gt_sb[:])
            prev = ctx.tok("vector")
        ctx.wait("sync", ctx.tok("l2a_gst"))
        ctx.barrier()
        ctx.wait("scalar", ctx.tok("vector"), ctx.tok("tensor"))
        s_tok = None
        for t in range(T):
            if t == 0:
                s1 = act(ctx, com, hseq[:, 0, 0:1], c_sb[:, 0, 0:1], AF.Tanh)
                s_tok = act(ctx, com, hseq[:, 1, 0:1], c_sb[:, 1, 0:1], AF.Tanh)
                continue
            ctx.wait("tensor", s_tok)
            ins = None
            for mo in range(2):
                for kc in range(2):
                    ins = nc.tensor.matmul(
                        ps_h[0:128, mo:mo + 1], a_sb[:, kc, mo * 128:(mo + 1) * 128],
                        hseq[:, kc, t - 1:t], start=(kc == 0), stop=(kc == 1),
                        skip_group_check=True)
            tmm = ctx.inc(ins, "tensor")
            ctx.wait("scalar", tmm)
            act(ctx, com, hseq[:, 0, t:t + 1], ps_h[0:128, 0:1], AF.Tanh,
                bias=c_sb[:, 0, t:t + 1])
            s_tok = act(ctx, com, hseq[:, 1, t:t + 1], ps_h[0:128, 1:2], AF.Tanh,
                        bias=c_sb[:, 1, t:t + 1])
        ctx.barrier()
        with contextlib.ExitStack() as l2:
            E2 = l2.enter_context
            tmp_sb = E2(nc.sbuf_tensor("l2a_tmpw", [128, 256], BF16))
            ps_tr = E2(nc.psum_tensor("l2a_ptr", [128, 8, 128], BF16))
            cT_sb, tc1 = load_weight_T(ctx, com, l2, "l2a_cT", w["C"], D, DS, tmp_sb, ps_tr)
            dT_sb, tc2 = load_weight_T(ctx, com, l2, "l2a_dT", w["D"], D, DS, tmp_sb,
                                       ps_tr, prev=tc1)
            wo_sb, tc3 = load_weight_ch(ctx, l2, "l2a_wo", w["W_out"], D, D)
            xf_sb = E2(nc.sbuf_tensor("l2a_xf", [128, 8, TCH], F32))
            yb_sb = E2(nc.sbuf_tensor("l2a_yb", [128, 8, TCH], BF16))
            tmpf = E2(nc.sbuf_tensor("l2a_tmpf", [128, TCH], F32))
            fo_sb = E2(nc.sbuf_tensor("l2a_fo", [128, 8, TCH], F32))
            ctx.wait("tensor", tc1, tc2, tc3)
            prev = None
            ev = [None, None]
            prev_st = None
            last_yb = None
            for j in range(ntc):
                cj = slice(j * TCH, (j + 1) * TCH)
                if prev is not None:
                    ctx.wait("sync", prev)
                    ctx.barrier()
                txf = ctx.dma("sync", "l2a_xfl", xf_sb[:], xfr[:, :, cj])
                tgl = ctx.dma("sync", "l2a_gl", gt_sb[:], gr[:, :, cj])
                txn = ctx.dma("sync", "l2a_xn2", xn_sb[:], xnr[:, :, cj])
                for mi in range(8):
                    pp = ps[mi % 2]
                    if ev[mi % 2] is not None:
                        ctx.wait("tensor", ev[mi % 2])
                    msl = slice(mi * 128, (mi + 1) * 128)
                    for kc in range(2):
                        nc.tensor.matmul(pp[:], cT_sb[:, kc, msl], hseq[:, kc, cj],
                                         start=(kc == 0), stop=False,
                                         skip_group_check=True)
                    ins = None
                    for kc in range(2):
                        ins = nc.tensor.matmul(pp[:], dT_sb[:, kc, msl],
                                               u_sb[:, kc, cj], start=False,
                                               stop=(kc == 1), skip_group_check=True)
                    tmm = ctx.inc(ins, "tensor")
                    ctx.wait("vector", tmm, txn, tgl)
                    if last_yb is not None:
                        ctx.wait("vector", last_yb)
                    v1 = ctx.inc(nc.vector.tensor_tensor(
                        out=tmpf[:], in0=pp[:], in1=xn_sb[:, mi, :],
                        op=ALU.subtract), "vector")
                    ctx.wait("vector", v1)
                    v2 = ctx.inc(nc.vector.tensor_tensor(
                        out=tmpf[:], in0=tmpf[:], in1=gt_sb[:, mi, :],
                        op=ALU.mult), "vector")
                    ctx.wait("vector", v2)
                    ev[mi % 2] = ctx.inc(nc.vector.tensor_tensor(
                        out=yb_sb[:, mi, :], in0=tmpf[:], in1=xn_sb[:, mi, :],
                        op=ALU.add), "vector")
                    last_yb = ev[mi % 2]
                ctx.wait("tensor", ev[0], ev[1])
                for mi in range(8):
                    pp = ps[mi % 2]
                    if ev[mi % 2] is not None:
                        ctx.wait("tensor", ev[mi % 2])
                    msl = slice(mi * 128, (mi + 1) * 128)
                    ins = mm_acc(nc, pp[:], [wo_sb[:, kc, msl] for kc in range(8)],
                                 [yb_sb[:, kc, :] for kc in range(8)])
                    tmm = ctx.inc(ins, "tensor")
                    ctx.wait("vector", tmm, txf)
                    if prev_st is not None:
                        ctx.wait("vector", prev_st)
                        prev_st = None
                    v1 = ctx.inc(nc.vector.tensor_scalar(
                        out=fo_sb[:, mi, :], in0=pp[:], scalar1=boc[:, mi:mi + 1],
                        scalar2=None, op0=ALU.add), "vector")
                    ctx.wait("vector", v1)
                    ev[mi % 2] = ctx.inc(nc.vector.tensor_tensor(
                        out=fo_sb[:, mi, :], in0=fo_sb[:, mi, :],
                        in1=xf_sb[:, mi, :], op=ALU.add), "vector")
                    last_yb = ev[mi % 2]
                ctx.wait("sync", ev[0], ev[1])
                tst = ctx.dma("sync", "l2a_ost", outr[:, :, cj], fo_sb[:])
                prev_st = tst
                prev = ctx.tok("vector")
            ctx.wait("sync", ctx.tok("l2a_ost"))
    ctx.fence()
    return



L0A_W = ["w_qkv", "w_out", "g1", "g2", "w_dp1", "w_dp2", "w_gate", "w_up",
         "w_down", "g_hidden"]
L0F_W = ["wqkv", "q_scale", "k_scale", "wo", "ln_g", "ln_b", "w1", "b1", "w2", "b2"]
L1A_W = ["W1", "b1", "W2", "b2", "Wq", "Wk", "Wv", "Wo"]
L1F_W = ["Wr", "Wqkv", "Wo"]
L2A_W = ["W_in", "b_in", "W_gate", "b_gate", "A", "Bm", "C", "D", "W_out", "b_out"]
L2F_W = ["w_gate", "w_up", "w_down"]
L3A_W = ["Wq", "bq", "Wd", "bd", "Wu", "bu", "Wo", "bo"]
L3F_W = ["router_w", "gate_w", "up_w", "down_w"]

_WSHAPES = {
    "L0_attn": {"w_qkv": [1024, 3072], "w_out": [1024, 1024], "g1": [1024],
                "g2": [1024], "w_dp1": [1024, 256], "w_dp2": [256, 1],
                "w_gate": [1024, 4608], "w_up": [1024, 4608],
                "w_down": [4608, 1024], "g_hidden": [4608]},
    "L0_ffn": {"wqkv": [2, 1024, 1536], "q_scale": [2, 64], "k_scale": [2, 64],
               "wo": [2, 512, 1024], "ln_g": [2, 1024], "ln_b": [2, 1024],
               "w1": [2, 1024, 5460], "b1": [2, 5460], "w2": [2, 2730, 1024],
               "b2": [2, 1024]},
    "L1_attn": {"W1": [2048, 1024], "b1": [2048], "W2": [1024, 2048],
                "b2": [1024], "Wq": [1024, 1024], "Wk": [1024, 1024],
                "Wv": [1024, 1024], "Wo": [1024, 1024]},
    "L1_ffn": {"Wr": [8, 1024], "Wqkv": [8, 1024, 384], "Wo": [8, 128, 1024]},
    "L2_attn": {"W_in": [1024, 256], "b_in": [256], "W_gate": [1024, 1024],
                "b_gate": [1024], "A": [256, 256], "Bm": [256, 256],
                "C": [1024, 256], "D": [1024, 256], "W_out": [1024, 1024],
                "b_out": [1024]},
    "L2_ffn": {"w_gate": [2816, 1024], "w_up": [2816, 1024],
               "w_down": [1024, 2816]},
    "L3_attn": {"Wq": [1024, 1024], "bq": [1024], "Wd": [1024, 256],
                "bd": [256], "Wu": [256, 2048], "bu": [2048],
                "Wo": [1024, 1024], "bo": [1024]},
    "L3_ffn": {"router_w": [8, 1024], "gate_w": [8, 2816, 1024],
               "up_w": [8, 2816, 1024], "down_w": [8, 1024, 2816]},
}


def _declare_w(nc, slot, keys=None):
    return {k: nc.dram_tensor(f"{slot}_{k}", v, F32, kind="ExternalInput").ap()
            for k, v in _WSHAPES[slot].items() if keys is None or k in keys}


L0A_PRE_KEYS = ("w_qkv", "w_out", "g1", "g2", "w_dp1", "w_dp2")
L0A_POST_KEYS = ("w_gate", "w_up", "w_down", "g_hidden")


def build_layer_program(li, T, n_cores):
    nc = bass.Bass("TRN2", target_bir_lowering=False)
    x_in = nc.dram_tensor("x_in", [D, T], F32, kind="ExternalInput").ap()
    x_out = nc.dram_tensor("x_out", [D, T], F32, kind="ExternalOutput").ap()
    a_hbm = nc.dram_tensor("dr_a", [D, T], F32, kind="Internal").ap()
    f_hbm = nc.dram_tensor("dr_f", [D, T], F32, kind="Internal").ap()
    xn_ch = nc.dram_tensor("dr_xn", [D, T], BF16, kind="Internal").ap()
    xn_f32 = nc.dram_tensor("dr_xnf", [D, T], F32, kind="Internal").ap()
    xn_tok = nc.dram_tensor("dr_xnt", [T + 128, D], BF16, kind="Internal").ap()
    xn_tokf = nc.dram_tensor("dr_xntf", [T, D], F32, kind="Internal").ap()
    with contextlib.ExitStack() as st:
        ctx = Ctx(nc, st)
        com = Common(nc, ctx, st)
        ctx.wait("tensor", com.ready)
        ctx.wait("vector", com.ready)
        ctx.wait("scalar", com.ready)
        if li == "0a":
            wA = _declare_w(nc, "L0_attn", L0A_PRE_KEYS)
            wF = _declare_w(nc, "L0_ffn")
            ratio_out = nc.dram_tensor("ratio", [1, 1], F32,
                                       kind="ExternalOutput").ap()
            x1_out = nc.dram_tensor("x1_out", [D, T], F32,
                                    kind="ExternalOutput").ap()
            h2_out = nc.dram_tensor("h2_out", [D, T], BF16,
                                    kind="ExternalOutput").ap()
            f_out = nc.dram_tensor("f_out", [D, T], F32,
                                   kind="ExternalOutput").ap()
            emit_ln(ctx, com, T, x_in, xn_f32_hbm=xn_f32)
            ctx.fence()
            emit_l0_attn(ctx, com, T, xn_f32, wA, None, n_cores=n_cores,
                         split="pre", ratio_hbm=ratio_out, x1_ext=x1_out,
                         h2_ext=h2_out)
            ctx.fence()
            emit_l0_ffn(ctx, com, T, xn_f32, wF, f_out)
            ctx.fence()
            return nc
        elif li == "0b":
            wA = _declare_w(nc, "L0_attn", L0A_POST_KEYS)
            ratios_in = nc.dram_tensor("ratios", [n_cores, 1], F32,
                                       kind="ExternalInput").ap()
            x1_in = nc.dram_tensor("x1_in", [D, T], F32,
                                   kind="ExternalInput").ap()
            h2_in = nc.dram_tensor("h2_in", [D, T], BF16,
                                   kind="ExternalInput").ap()
            f_in = nc.dram_tensor("f_in", [D, T], F32,
                                  kind="ExternalInput").ap()
            emit_l0_attn_post(ctx, com, T, wA, x1_in, h2_in, ratios_in,
                              a_hbm, n_cores)
            ctx.fence()
            emit_ln(ctx, com, T, x_in, add_hbm=(a_hbm, f_in),
                    x_out_hbm=x_out)
            ctx.fence()
            return nc
        elif li == 0:
            wA = _declare_w(nc, "L0_attn")
            wF = _declare_w(nc, "L0_ffn")
            emit_ln(ctx, com, T, x_in, xn_f32_hbm=xn_f32)
            ctx.fence()
            emit_l0_attn(ctx, com, T, xn_f32, wA, a_hbm, n_cores=n_cores)
            ctx.fence()
            emit_l0_ffn(ctx, com, T, xn_f32, wF, f_hbm)
        elif li == 1:
            wA = _declare_w(nc, "L1_attn")
            wF = _declare_w(nc, "L1_ffn")
            emit_ln(ctx, com, T, x_in, xn_ch_hbm=xn_ch, xn_tok_hbm=xn_tok)
            ctx.fence()
            emit_l1_attn(ctx, com, T, xn_ch, wA, a_hbm)
            ctx.fence()
            emit_l1_ffn(ctx, com, T, xn_ch, xn_tok, wF, f_hbm)
        elif li == 2:
            wA = _declare_w(nc, "L2_attn")
            wF = _declare_w(nc, "L2_ffn")
            emit_ln(ctx, com, T, x_in, xn_ch_hbm=xn_ch, xn_f32_hbm=xn_f32,
                    xn_tok_hbm=xn_tokf, tok_dtype=F32)
            ctx.fence()
            emit_l2_attn(ctx, com, T, xn_ch, xn_f32, wA, a_hbm)
            ctx.fence()
            emit_l2_ffn(ctx, com, T, xn_tokf, wF, f_hbm)
        else:
            wA = _declare_w(nc, "L3_attn")
            wF = _declare_w(nc, "L3_ffn")
            emit_ln(ctx, com, T, x_in, xn_ch_hbm=xn_ch, xn_tok_hbm=xn_tok)
            ctx.fence()
            emit_l3_attn(ctx, com, T, xn_ch, wA, a_hbm)
            ctx.fence()
            emit_l3_ffn(ctx, com, T, xn_ch, xn_tok, wF, f_hbm)
        ctx.fence()
        emit_ln(ctx, com, T, x_in, add_hbm=(a_hbm, f_hbm), x_out_hbm=x_out)
        ctx.fence()
    return nc


def build_rest_program(T, n_cores):
    nc = bass.Bass("TRN2", target_bir_lowering=False)
    x_in = nc.dram_tensor("x_in", [D, T], F32, kind="ExternalInput").ap()
    x_out = nc.dram_tensor("x_out", [D, T], F32, kind="ExternalOutput").ap()
    ratios_in = nc.dram_tensor("ratios", [n_cores, 1], F32,
                               kind="ExternalInput").ap()
    x1_in = nc.dram_tensor("x1_in", [D, T], F32, kind="ExternalInput").ap()
    h2_in = nc.dram_tensor("h2_in", [D, T], BF16, kind="ExternalInput").ap()
    f_in = nc.dram_tensor("f_in", [D, T], F32, kind="ExternalInput").ap()
    w0 = _declare_w(nc, "L0_attn", L0A_POST_KEYS)
    ws = {li: (_declare_w(nc, f"L{li}_attn"), _declare_w(nc, f"L{li}_ffn"))
          for li in (1, 2, 3)}
    xbuf = {li: nc.dram_tensor(f"dr_x{li}", [D, T], F32, kind="Internal").ap()
            for li in (0, 1, 2)}
    with contextlib.ExitStack() as st:
        ctx = Ctx(nc, st)
        com = Common(nc, ctx, st)
        ctx.wait("tensor", com.ready)
        ctx.wait("vector", com.ready)
        ctx.wait("scalar", com.ready)
        a0 = nc.dram_tensor("dr_a0", [D, T], F32, kind="Internal").ap()
        emit_l0_attn_post(ctx, com, T, w0, x1_in, h2_in, ratios_in, a0,
                          n_cores)
        ctx.fence()
        emit_ln(ctx, com, T, x_in, add_hbm=(a0, f_in), x_out_hbm=xbuf[0])
        ctx.fence()
        for li in (1, 2, 3):
            wA, wF = ws[li]
            xin_l = xbuf[li - 1]
            xout_l = x_out if li == 3 else xbuf[li]
            a_hbm = nc.dram_tensor(f"dr_a{li}", [D, T], F32, kind="Internal").ap()
            f_hbm = nc.dram_tensor(f"dr_f{li}", [D, T], F32, kind="Internal").ap()
            if li == 1:
                xn_ch = nc.dram_tensor(f"dr_xn{li}", [D, T], BF16, kind="Internal").ap()
                xn_tok = nc.dram_tensor(f"dr_xnt{li}", [T + 128, D], BF16, kind="Internal").ap()
                emit_ln(ctx, com, T, xin_l, xn_ch_hbm=xn_ch, xn_tok_hbm=xn_tok)
                ctx.fence()
                emit_l1_attn(ctx, com, T, xn_ch, wA, a_hbm)
                ctx.fence()
                emit_l1_ffn(ctx, com, T, xn_ch, xn_tok, wF, f_hbm)
            elif li == 2:
                xn_ch = nc.dram_tensor(f"dr_xn{li}", [D, T], BF16, kind="Internal").ap()
                xn_f32 = nc.dram_tensor(f"dr_xnf{li}", [D, T], F32, kind="Internal").ap()
                xn_tokf = nc.dram_tensor(f"dr_xntf{li}", [T, D], F32, kind="Internal").ap()
                emit_ln(ctx, com, T, xin_l, xn_ch_hbm=xn_ch, xn_f32_hbm=xn_f32,
                        xn_tok_hbm=xn_tokf, tok_dtype=F32)
                ctx.fence()
                emit_l2_attn(ctx, com, T, xn_ch, xn_f32, wA, a_hbm)
                ctx.fence()
                emit_l2_ffn(ctx, com, T, xn_tokf, wF, f_hbm)
            else:
                xn_ch = nc.dram_tensor(f"dr_xn{li}", [D, T], BF16, kind="Internal").ap()
                xn_tok = nc.dram_tensor(f"dr_xnt{li}", [T + 128, D], BF16, kind="Internal").ap()
                emit_ln(ctx, com, T, xin_l, xn_ch_hbm=xn_ch, xn_tok_hbm=xn_tok)
                ctx.fence()
                emit_l3_attn(ctx, com, T, xn_ch, wA, a_hbm)
                ctx.fence()
                emit_l3_ffn(ctx, com, T, xn_ch, xn_tok, wF, f_hbm)
            ctx.fence()
            emit_ln(ctx, com, T, xin_l, add_hbm=(a_hbm, f_hbm),
                    x_out_hbm=xout_l)
            ctx.fence()
    return nc



_PROG_CACHE = {}


def _wmap(inputs, slots, keys_of=None):
    m = {}
    for slot in slots:
        for k in _WSHAPES[slot]:
            if keys_of and slot in keys_of and k not in keys_of[slot]:
                continue
            m[f"{slot}_{k}"] = np.ascontiguousarray(inputs[f"{slot}_{k}"])
    return m


def kernel(**inputs):
    T = inputs["L0_attn_x"].shape[1]
    B = inputs["L0_attn_x"].shape[0]
    n_cores = B
    cid = list(range(n_cores))
    xs = [np.ascontiguousarray(inputs["L0_attn_x"][b].T) for b in range(B)]

    def prog(li):
        key = (li, T, n_cores)
        if key not in _PROG_CACHE:
            _PROG_CACHE[key] = build_layer_program(li, T, n_cores)
        return _PROG_CACHE[key]

    wm = _wmap(inputs, ("L0_attn", "L0_ffn"),
               {"L0_attn": L0A_PRE_KEYS})
    res = run_bass_kernel_spmd(prog("0a"), [dict(wm, x_in=xs[b])
                                            for b in range(B)], core_ids=cid)
    ratios = np.ascontiguousarray(
        np.stack([np.asarray(res.results[b]["ratio"], np.float32).reshape(1)
                  for b in range(B)], axis=0))
    wm = _wmap(inputs, ("L0_attn", "L1_attn", "L1_ffn", "L2_attn", "L2_ffn",
                        "L3_attn", "L3_ffn"), {"L0_attn": L0A_POST_KEYS})
    key = ("rest", T, n_cores)
    if key not in _PROG_CACHE:
        _PROG_CACHE[key] = build_rest_program(T, n_cores)
    res = run_bass_kernel_spmd(
        _PROG_CACHE[key],
        [dict(wm, x_in=xs[b], ratios=ratios,
              x1_in=np.asarray(res.results[b]["x1_out"]),
              h2_in=np.asarray(res.results[b]["h2_out"]),
              f_in=np.asarray(res.results[b]["f_out"])) for b in range(B)],
        core_ids=cid)
    out = np.stack([np.asarray(res.results[b]["x_out"], np.float32).T
                    for b in range(B)], axis=0)
    return out
